# version 06
# baseline (speedup 1.0000x reference)
import sys
import numpy as np

sys.path.insert(0, '/opt/trn_rl_repo')
import concourse.bass as bass
import concourse.mybir as mybir
from concourse.bass_utils import run_bass_kernel_spmd

F32 = mybir.dt.float32
BF16 = mybir.dt.bfloat16
AF = mybir.ActivationFunctionType
ALU = mybir.AluOpType
AX = mybir.AxisListType


ENGINES = ("te", "dve", "act", "pool", "sp")
DMA_QUEUES = ("sp", "pool", "act")


class Prog:

    def __init__(self, nc):
        self.nc = nc
        self.eng = {"te": nc.tensor, "dve": nc.vector, "act": nc.scalar,
                    "pool": nc.gpsimd, "sp": nc.sync}
        self.sem = {e: nc.alloc_semaphore(f"S_{e}") for e in ENGINES}
        self.NQS = 4
        self.qsem = {q: [nc.alloc_semaphore(f"Q_{q}{i}") for i in range(self.NQS)]
                     for q in DMA_QUEUES}
        self.cnt = {e: 0 for e in ENGINES}
        self.qcnt = {q: 0 for q in DMA_QUEUES}
        self.known = {e: {} for e in ENGINES}
        self.lastw = {}
        self.readers = {}
        self.queue = {e: [] for e in ENGINES}
        self.nwaits = 0
        self.ninstr = 0

    def _sem_of(self, token):
        if token.startswith("q_"):
            q, i = token[2:].split(".")
            return self.qsem[q][int(i)]
        return self.sem[token]

    def _need_from(self, reads, writes):
        need = {}

        def add(d):
            for name, val in d.items():
                if need.get(name, 0) < val:
                    need[name] = val
        for k in reads:
            add(self.lastw.get(k, {}))
        for k in writes:
            add(self.lastw.get(k, {}))
            add(self.readers.get(k, {}))
        return need

    def _emit_waits(self, e, need):
        for name, val in sorted(need.items()):
            if self.known[e].get(name, 0) >= val:
                continue
            sem = self._sem_of(name)
            self.queue[e].append(lambda eng, s=sem, v=val: eng.wait_ge(s, v))
            self.known[e][name] = val
            self.nwaits += 1

    def _record(self, token, reads, writes):
        name, val = token
        for k in writes:
            self.lastw[k] = {name: val}
            self.readers[k] = {}
        for k in reads:
            d = self.readers.setdefault(k, {})
            if d.get(name, 0) < val:
                d[name] = val

    def run(self, e, fn, reads=(), writes=()):
        need = self._need_from(reads, writes)
        self._emit_waits(e, need)
        self.cnt[e] += 1
        sem, val = self.sem[e], self.cnt[e]
        self.queue[e].append(lambda eng: fn(eng).then_inc(sem, 1))
        self._record((e, val), reads, writes)
        self.ninstr += 1

    def group(self, e, fns, reads=(), writes=()):
        need = self._need_from(reads, writes)
        self._emit_waits(e, need)
        self.cnt[e] += 1
        sem, val = self.sem[e], self.cnt[e]
        for f in fns[:-1]:
            self.queue[e].append(lambda eng, f=f: f(eng))
        self.queue[e].append(lambda eng, f=fns[-1]: f(eng).then_inc(sem, 1))
        self._record((e, val), reads, writes)
        self.ninstr += len(fns)

    def dma(self, q, out, in_, reads=(), writes=(), **kw):
        need = self._need_from(reads, writes)
        idx = self.qcnt[q]
        lane = idx % self.NQS
        name = f"q_{q}.{lane}"
        if idx >= self.NQS:
            prev = 16 * (idx // self.NQS)
            if need.get(name, 0) < prev:
                need[name] = prev
        self._emit_waits(q, need)
        self.qcnt[q] = idx + 1
        sem, val = self.qsem[q][lane], 16 * (idx // self.NQS + 1)
        self.queue[q].append(
            lambda eng, o=out, i=in_: eng.dma_start(o, i, **kw).then_inc(sem, 16))
        self._record((name, val), reads, writes)
        self.ninstr += 1

    def _qfinal(self):
        need = {}
        for q in DMA_QUEUES:
            for lane in range(self.NQS):
                n = self.qcnt[q] // self.NQS + (1 if lane < self.qcnt[q] % self.NQS else 0)
                if n:
                    need[f"q_{q}.{lane}"] = 16 * n
        return need

    def finish(self):
        need = {e: self.cnt[e] for e in ENGINES if self.cnt[e]}
        need.update(self._qfinal())
        self._emit_waits("sp", need)

    def barrier(self):
        need = {e: self.cnt[e] for e in ENGINES if self.cnt[e]}
        need.update(self._qfinal())
        for e in ENGINES:
            my = dict(need)
            my.pop(e, None)
            self._emit_waits(e, my)
        self.lastw = {}
        self.readers = {}
        for e in ENGINES:
            self.known[e] = dict(need)

    def flush(self):
        with self.nc.Block() as blk:
            for e in ENGINES:
                if not self.queue[e]:
                    continue
                deco = {"te": blk.tensor, "dve": blk.vector, "act": blk.scalar,
                        "pool": blk.gpsimd, "sp": blk.sync}[e]

                def body(eng, thunks=list(self.queue[e])):
                    for th in thunks:
                        th(eng)
                deco(body)
                self.queue[e] = []



D = 1024
NC = 8
S = 2048
NSEQ = 2
T = S * NSEQ

ARENA = {}


def _arena_cols():
    spec = [("bpre", 8), ("bproj", 8), ("pi", 1), ("bs1", 32), ("bs11", 32),
            ("bs2", 8), ("ms", 8), ("mls", 8), ("td", 8), ("tf", 8),
            ("b1", 256), ("b11", 256), ("zero", 1)]
    off = 0
    for name, n in spec:
        ARENA[name] = (off, n)
        off += n
    return off


ARENA_COLS = _arena_cols()

CBF_ID = 0
CBF_ONES = 128
CBF_OH32 = 256
CBF_OH16 = CBF_OH32 + 32 * 128
CBF_COLS = CBF_OH16 + 16 * 128
CF_ONES = 0
CF_CONSTS = {1e-5: 128, 1e-6: 129, 1.0: 130, 0.5: 131, 0.0: 132, -1.0: 133,
             1e30: 134, 0.125: 135, 1.0 / 32: 136, 1.0 / 1024: 137}
CF_IDENT = 144
CF_COLS = 272


def host_prepare(inputs):
    import ml_dtypes
    f32 = np.float32
    g = {k: np.asarray(v) for k, v in inputs.items()}

    shared = {}
    C = np.ascontiguousarray
    shared["Wpre"] = C(g["L0_attn_Wpre"].astype(f32))
    shared["Wq"] = C(g["L0_attn_Wq"].astype(f32))
    shared["Wk"] = C(g["L0_attn_Wk"].astype(f32))
    shared["Wv"] = C(g["L0_attn_Wv"].astype(f32))
    shared["Wproj"] = C(g["L0_attn_Wproj"].astype(f32))
    shared["gwT"] = C(g["L0_ffn_gate_w"].T.astype(f32))
    shared["guwT"] = C(g["L0_ffn_gate_up_w"].transpose(0, 2, 1).astype(f32))
    shared["dwT"] = C(g["L0_ffn_down_w"].transpose(0, 2, 1).astype(f32))
    shared["WkT1"] = C(g["L1_attn_Wk"].T.astype(f32))
    shared["WvT1"] = C(g["L1_attn_Wv"].T.astype(f32))
    shared["WrT1"] = C(g["L1_attn_Wr"].T.astype(f32))
    shared["WoT1"] = C(g["L1_attn_Wo"].T.astype(f32))
    shared["WfcT"] = C(g["L1_attn_Wfc"].T.astype(f32))
    shared["WmlpT"] = C(g["L1_attn_Wmlp"].T.astype(f32))
    shared["Wg"] = C(g["L1_ffn_Wg"].astype(f32))
    shared["W1"] = C(g["L1_ffn_W1"].astype(f32))
    shared["W11"] = C(g["L1_ffn_W11"].astype(f32))
    shared["W2"] = C(g["L1_ffn_W2"].astype(f32))
    shared["b2e"] = C(g["L1_ffn_b2"].astype(f32))
    shared["Ws1"] = C(g["L1_ffn_Ws1"].astype(f32))
    shared["Ws11"] = C(g["L1_ffn_Ws11"].astype(f32))
    shared["Ws2"] = C(g["L1_ffn_Ws2"].astype(f32))

    arena = np.zeros((128, ARENA_COLS), f32)

    def put(name, vec):
        off, n = ARENA[name]
        vec = np.asarray(vec, f32).reshape(-1)
        assert vec.size == n * 128, (name, vec.size, n)
        arena[:, off:off + n] = vec.reshape(n, 128).T
    put("bpre", g["L0_attn_bpre"])
    put("bproj", g["L0_attn_bproj"])
    arena[:, ARENA["pi"][0]] = np.repeat(np.asarray(g["L0_attn_pi"], f32), 128)
    put("bs1", g["L1_ffn_bs1"])
    put("bs11", g["L1_ffn_bs11"])
    put("bs2", g["L1_ffn_bs2"])
    put("ms", g["L1_attn_mixer_scale"])
    put("mls", g["L1_attn_mlp_scale"])
    put("td", g["L1_attn_time_decay"])
    put("tf", g["L1_attn_time_first"])
    put("b1", g["L1_ffn_b1"])
    put("b11", g["L1_ffn_b11"])
    shared["arena"] = arena

    bf16 = ml_dtypes.bfloat16
    cbf = np.zeros((128, CBF_COLS), bf16)
    cbf[:, CBF_ID:CBF_ID + 128] = np.eye(128, dtype=bf16)
    cbf[:, CBF_ONES:CBF_ONES + 128] = 1
    for e in range(32):
        cbf[e, CBF_OH32 + e * 128: CBF_OH32 + (e + 1) * 128] = 1
    for e in range(16):
        cbf[e, CBF_OH16 + e * 128: CBF_OH16 + (e + 1) * 128] = 1
    shared["cbf"] = cbf
    cf = np.ones((128, CF_COLS), f32)
    for val, col in CF_CONSTS.items():
        cf[:, col] = val
    cf[:, CF_IDENT:CF_IDENT + 128] = np.eye(128, dtype=f32)
    shared["cf32"] = cf

    xt = np.ascontiguousarray(g["L0_attn_x"].astype(f32).transpose(0, 2, 1))
    in_maps = []
    for c in range(8):
        m = dict(shared)
        m["x0"] = np.ascontiguousarray(
            xt[2 * c:2 * c + 2].reshape(2048, 2048))
        in_maps.append(m)
    return in_maps


def host_gather(results):
    out = np.empty((16, 2048, 1024), np.float32)
    for c, r in enumerate(results):
        o = np.asarray(r["out"], np.float32).reshape(2, 1024, 2048)
        out[2 * c:2 * c + 2] = o.transpose(0, 2, 1)
    return out



WEIGHT_SHAPES = {
    "Wpre": [D, D], "Wq": [D, D], "Wk": [D, D], "Wv": [D, D], "Wproj": [D, D],
    "gwT": [D, 32], "guwT": [32, D, D], "dwT": [32, 512, D],
    "WkT1": [D, D], "WvT1": [D, D], "WrT1": [D, D], "WoT1": [D, D],
    "WfcT": [D, 4 * D], "WmlpT": [4 * D, D],
    "Wg": [D, 16], "W1": [16, D, 2 * D], "W11": [16, D, 2 * D],
    "W2": [16, 2 * D, D], "b2e": [16, D],
    "Ws1": [D, 4 * D], "Ws11": [D, 4 * D], "Ws2": [4 * D, D],
}
SCRATCH = {"xnh": BF16, "xr": F32, "xih": F32, "xr2": F32}


class IO:

    def __init__(self, nc, phase_inputs, phase_outputs):
        self.nc = nc
        self.d = {}
        self.d["x0"] = nc.dram_tensor("x0", [2048, 2048], F32, kind="ExternalInput").ap()
        self.d["arena"] = nc.dram_tensor("arena", [128, ARENA_COLS], F32, kind="ExternalInput").ap()
        self.d["cbf"] = nc.dram_tensor("cbf", [128, CBF_COLS], BF16, kind="ExternalInput").ap()
        self.d["cf32"] = nc.dram_tensor("cf32", [128, CF_COLS], F32, kind="ExternalInput").ap()
        for w, shp in WEIGHT_SHAPES.items():
            self.d[w] = nc.dram_tensor(w, shp, F32, kind="ExternalInput").ap()
        self.d["out"] = nc.dram_tensor("out", [2048, 2048], F32, kind="ExternalOutput").ap()
        for sname, dt in SCRATCH.items():
            if sname in phase_outputs:
                kind = "ExternalOutput"
            elif sname in phase_inputs:
                kind = "ExternalInput"
            else:
                kind = "Internal"
            self.d[sname] = nc.dram_tensor(sname, [2048, 2048], dt, kind=kind).ap()
        kind = ("ExternalOutput" if "stats" in phase_outputs else
                "ExternalInput" if "stats" in phase_inputs else "Internal")
        self.d["stats"] = nc.dram_tensor("stats", [8, 2048], F32, kind=kind).ap()

        self.misc = nc.alloc_sbuf_tensor("misc_sb", [128, 32], F32)
        self.arena = nc.alloc_sbuf_tensor("arena_sb", [128, ARENA_COLS], F32)
        self.cbf = nc.alloc_sbuf_tensor("cbf_sb", [128, CBF_COLS], BF16)
        self.cf32 = nc.alloc_sbuf_tensor("cf32_sb", [128, CF_COLS], F32)
        self.pb = [nc.alloc_psum_tensor(f"pb{i}", [128, 512], F32) for i in range(7)]
        self.pbt = nc.alloc_psum_tensor("pbt", [128, 1024], BF16)
        for val, col in CF_CONSTS.items():
            nc.const_aps.register(F32, val, self.cf32[:, col:col + 1])
        self.ov_base = 33800
        self.ov_top = 229344
        self.ov_cur = self.ov_base
        self.ov_peak = self.ov_base
        assert nc.sbuf_base <= self.ov_base, nc.sbuf_base

    def ov_reset(self):
        self.ov_cur = self.ov_base

    def salloc(self, name, shape, dtype):
        esz = 2 if dtype == BF16 else 4
        nbytes = int(np.prod(shape[1:])) * esz
        addr = (self.ov_cur + 63) & ~63
        self.ov_cur = addr + nbytes
        self.ov_peak = max(self.ov_peak, self.ov_cur)
        assert self.ov_cur <= self.ov_top, (name, self.ov_cur)
        return self.nc.alloc_sbuf_tensor_at(name, shape, dtype, offset=addr)

    def load_consts(self, p):
        p.dma("sp", self.arena[:], self.d["arena"], writes=["arena"])
        p.dma("sp", self.cbf[:], self.d["cbf"], writes=["cbf"])
        p.dma("sp", self.cf32[:], self.d["cf32"], writes=["cf32"])
        p.run("dve", lambda v: v.tensor_scalar(
            self.misc[:, 0:1], self.acol("pi"), 1.0 / 32, None, op0=ALU.mult),
            reads=["arena"], writes=["misc"])

    def acol(self, name, j=0):
        off, n = ARENA[name]
        assert j < n
        return self.arena[:, off + j:off + j + 1]

    def ones_bf(self, p_lo, p_cnt, cols):
        return self.cbf[p_lo:p_lo + p_cnt, CBF_ONES:CBF_ONES + cols]

    def ones_f32(self, p_lo, p_cnt, cols):
        return self.cf32[p_lo:p_lo + p_cnt, CF_ONES:CF_ONES + cols]

    def ident(self):
        return self.cbf[:, CBF_ID:CBF_ID + 128]

    def ident_f32(self):
        return self.cf32[:, CF_IDENT:CF_IDENT + 128]

    def ov_mark(self):
        return self.ov_cur

    def ov_release(self, mark):
        self.ov_cur = mark



def emit_ln(p, io, src_name, layer):
    nc = io.nc
    src = io.d[src_name]
    io.ov_reset()
    xt = [io.salloc(f"ln{layer}_x{i}", [128, 512], F32) for i in range(16)]
    sq = [io.salloc(f"ln{layer}_sq{i}", [128, 512], F32) for i in range(2)]
    rows = io.salloc(f"ln{layer}_rows", [1, 512 * 6], F32)
    bc = [io.salloc(f"ln{layer}_bc{i}", [128, 512], F32) for i in range(4)]
    xnt = [io.salloc(f"ln{layer}_xn{i}", [128, 512], BF16) for i in range(2)]

    for tb in range(8):
        seq, col = tb // 4, (tb % 4) * 512
        row0 = seq * 1024
        buf = tb % 2
        xts = xt[buf * 8:(buf + 1) * 8]
        psA, psB = io.pb[buf * 2], io.pb[buf * 2 + 1]
        kA, kB = f"pb{buf*2}", f"pb{buf*2+1}"
        for c in range(8):
            p.dma("sp", xts[c][:], src[row0 + c * 128: row0 + (c + 1) * 128,
                                       col:col + 512],
                  reads=[src_name], writes=[f"lnx{buf}{c}"])
        mmacc(p, psA[0:1, :], [(io.ones_f32(0, 128, 1), xts[c][:]) for c in range(8)],
              reads=[f"lnx{buf}{c}" for c in range(8)] + ["cf32"], wkey=kA)
        for c in range(8):
            sqb = sq[c % 2]
            p.run("act", lambda s, c=c, xts=xts, sqb=sqb: s.activation(
                sqb[:], xts[c][:], AF.Square),
                reads=[f"lnx{buf}{c}"], writes=[f"lnsq{c%2}"])
            p.run("te", lambda t, c=c, sqb=sqb, psB=psB: t.matmul(
                psB[0:1, :], io.ones_f32(0, 128, 1), sqb[:],
                start=(c == 0), stop=(c == 7)),
                reads=[f"lnsq{c%2}", "cf32"], writes=[kB])
        m_r = rows[:, 0:512]
        msq_r = rows[:, 512:1024]
        var_r = rows[:, 1024:1536]
        rstd_r = rows[:, 1536:2048]
        mr_r = rows[:, 2048:2560]
        p.run("dve", lambda v, psA=psA: v.tensor_scalar(m_r, psA[0:1, :], 1.0 / 1024,
                                                        None, op0=ALU.mult),
              reads=[kA], writes=["lnrows_m"])
        p.run("dve", lambda v, psB=psB: v.tensor_scalar(msq_r, psB[0:1, :], 1.0 / 1024,
                                                        None, op0=ALU.mult),
              reads=[kB], writes=["lnrows_msq"])
        p.run("dve", lambda v: v.scalar_tensor_tensor(
            var_r, m_r, -1.0, m_r, op0=ALU.mult, op1=ALU.mult),
            reads=["lnrows_m"], writes=["lnrows_var"])
        p.run("dve", lambda v: v.tensor_tensor(var_r, msq_r, var_r, op=ALU.add),
              reads=["lnrows_msq", "lnrows_var"], writes=["lnrows_var"])
        p.run("act", lambda s: s.activation(rstd_r, var_r, AF.Sqrt, bias=1e-5),
              reads=["lnrows_var"], writes=["lnrows_rstd"])
        p.run("dve", lambda v: v.reciprocal(rstd_r, rstd_r),
              reads=["lnrows_rstd"], writes=["lnrows_rstd"])
        p.run("dve", lambda v: v.tensor_tensor(mr_r, m_r, rstd_r, op=ALU.mult),
              reads=["lnrows_m", "lnrows_rstd"], writes=["lnrows_mr"])
        p.dma("act", io.d["stats"][layer * 4 + seq * 2: layer * 4 + seq * 2 + 1,
                                   col:col + 512], m_r,
              reads=["lnrows_m"], writes=["stats"])
        p.dma("act", io.d["stats"][layer * 4 + seq * 2 + 1: layer * 4 + seq * 2 + 2,
                                   col:col + 512], rstd_r,
              reads=["lnrows_rstd"], writes=["stats"])
        psC, psD = io.pb[4], io.pb[5]
        p.run("te", lambda t: t.matmul(psC[:, :], io.ones_f32(0, 1, 128), rstd_r,
                                       start=True, stop=True),
              reads=["lnrows_rstd", "cf32"], writes=["pb4"])
        p.run("te", lambda t: t.matmul(psD[:, :], io.ones_f32(0, 1, 128), mr_r,
                                       start=True, stop=True),
              reads=["lnrows_mr", "cf32"], writes=["pb5"])
        p.run("dve", lambda v: v.tensor_copy(bc[0][:], psC[:, :]),
              reads=["pb4"], writes=["lnbc0"])
        p.run("dve", lambda v: v.tensor_copy(bc[1][:], psD[:, :]),
              reads=["pb5"], writes=["lnbc1"])
        for c in range(8):
            xb = xnt[c % 2]
            kx = f"lnxnt{c%2}"
            p.run("dve", lambda v, c=c, xts=xts: v.tensor_tensor(
                bc[2][:], xts[c][:], bc[0][:], op=ALU.mult),
                reads=[f"lnx{buf}{c}", "lnbc0"], writes=["lnbc2"])
            p.run("dve", lambda v, xb=xb: v.tensor_tensor(
                xb[:], bc[2][:], bc[1][:], op=ALU.subtract),
                reads=["lnbc2", "lnbc1"], writes=[kx])
            p.dma("act", io.d["xnh"][row0 + c * 128: row0 + (c + 1) * 128,
                                     col:col + 512], xb[:],
                  reads=[kx], writes=["xnh"])
    p.barrier()



def _wslices(wbuf, k, m_lo, m_cnt):
    return wbuf[:, k * 1024 + m_lo: k * 1024 + m_lo + m_cnt]


def mmacc(p, out_ap, pairs, reads, wkey):
    n = len(pairs)
    fns = [lambda t, lh=lh, rh=rh, i=i: t.matmul(out_ap, lh, rh,
                                                 start=(i == 0), stop=(i == n - 1))
           for i, (lh, rh) in enumerate(pairs)]
    p.group("te", fns, reads=reads, writes=[wkey])


def emit_attn0(p, io, seq):
    nc = io.nc
    sq = f"a{seq}"
    row0 = seq * 1024
    io.ov_reset()
    pre = [io.salloc(f"{sq}_pre{m}", [128, S], BF16) for m in range(8)]
    kk = [io.salloc(f"{sq}_k{m}", [128, S], BF16) for m in range(8)]
    qq = [io.salloc(f"{sq}_q{m}", [128, S], BF16) for m in range(8)]
    vx = [io.salloc(f"{sq}_vx{t}", [128, 1040], BF16) for t in range(16)]
    negB = io.salloc(f"{sq}_negB", [128, 16], F32)

    mark = io.ov_mark()
    wA = io.salloc(f"{sq}_wA", [128, 8192], BF16)
    wB = io.salloc(f"{sq}_wB", [128, 8192], BF16)
    xnsl = [io.salloc(f"{sq}_xnsl{i}", [128, 8 * 512], BF16) for i in range(2)]
    sqt = io.salloc(f"{sq}_sqt", [128, S], BF16)
    maxcol = io.salloc(f"{sq}_maxcol", [128, 1], F32)
    rows = io.salloc(f"{sq}_rows", [1, 512], F32)
    rowbf = io.salloc(f"{sq}_rowbf", [1, 16], BF16)

    def ldw(buf, key, wname):
        src = io.d[wname].rearrange("(c p) m -> p c m", p=128)
        dst = buf[:].rearrange("p (c m) -> p c m", m=1024)
        p.dma("pool", dst, src, reads=[wname], writes=[key])

    def load_xnsl(nq):
        b = nq % 2
        src = io.d["xnh"][row0:row0 + 1024,
                          nq * 512:(nq + 1) * 512].rearrange("(c p) n -> p c n", p=128)
        dst = xnsl[b][:].rearrange("p (c n) -> p c n", n=512)
        p.dma("sp", dst, src, reads=["xnh"], writes=[f"{sq}_xnsl{b}"])
        return xnsl[b], f"{sq}_xnsl{b}"

    ldw(wA, f"{sq}_wA", "Wpre")
    ldw(wB, f"{sq}_wB", "Wk")
    for nq in range(4):
        xs, xk = load_xnsl(nq)
        for m in range(8):
            ps, kps = io.pb[m % 2], f"pb{m%2}"
            mmacc(p, ps[:, 0:512],
                  [(_wslices(wA, k, m * 128, 128), xs[:, k * 512:(k + 1) * 512])
                   for k in range(8)],
                  reads=[f"{sq}_wA", xk], wkey=kps)
            p.run("act", lambda s, ps=ps, m=m, nq=nq: s.activation(
                pre[m][:, nq * 512:(nq + 1) * 512], ps[:, 0:512], AF.Silu,
                bias=io.acol("bpre", m)),
                reads=[kps, "arena"], writes=[f"{sq}_pre{m}"])
        for m in range(8):
            ps, kps = io.pb[2 + m % 2], f"pb{2+m%2}"
            mmacc(p, ps[:, 0:512],
                  [(_wslices(wB, k, m * 128, 128), xs[:, k * 512:(k + 1) * 512])
                   for k in range(8)],
                  reads=[f"{sq}_wB", xk], wkey=kps)
            p.run("dve", lambda v, ps=ps, m=m, nq=nq: v.tensor_copy(
                kk[m][:, nq * 512:(nq + 1) * 512], ps[:, 0:512]),
                reads=[kps], writes=[f"{sq}_k{m}"])

    ldw(wA, f"{sq}_wA", "Wv")
    ldw(wB, f"{sq}_wB", "Wq")
    for t in range(16):
        p.run("dve", lambda v, t=t: v.memset(
            vx[t][:].rearrange("p (h x) -> p h x", x=65)[:, :, 64:65], 1.0),
            writes=[f"{sq}_vx{t}"])
    for nq in range(4):
        xs, xk = load_xnsl(nq)
        for tb in range(4):
            mc = nq * 4 + tb
            for half in range(2):
                ps, kps = io.pb[4 + half], f"pb{4+half}"
                mmacc(p, ps[:, 0:512],
                      [(xs[:, k * 512 + tb * 128: k * 512 + tb * 128 + 128],
                        _wslices(wA, k, half * 512, 512)) for k in range(8)],
                      reads=[f"{sq}_wA", xk], wkey=kps)
                p.run("dve", lambda v, ps=ps, mc=mc, half=half: v.tensor_copy(
                    vx[mc][:, half * 520:half * 520 + 520].rearrange(
                        "p (h x) -> p h x", x=65)[:, :, 0:64],
                    ps[:, 0:512].rearrange("p (h x) -> p h x", x=64)),
                    reads=[kps], writes=[f"{sq}_vx{mc}"])
        for m in range(8):
            ps, kps = io.pb[m % 2], f"pb{m%2}"
            mmacc(p, ps[:, 0:512],
                  [(_wslices(wB, k, m * 128, 128), xs[:, k * 512:(k + 1) * 512])
                   for k in range(8)],
                  reads=[f"{sq}_wB", xk], wkey=kps)
            p.run("dve", lambda v, ps=ps, m=m, nq=nq: v.tensor_scalar(
                qq[m][:, nq * 512:(nq + 1) * 512], ps[:, 0:512], 0.125, None,
                op0=ALU.mult),
                reads=[kps], writes=[f"{sq}_q{m}"])

    p.run("dve", lambda v: v.memset(rows[:, 0:64], -1e30), writes=[f"{sq}_rows"])
    for (src_t, kind) in ((qq, "q"), (kk, "k"), (pre, "p")):
        for c in range(8):
            skey = f"{sq}_pre{c}" if kind == "p" else (
                f"{sq}_q{c}" if kind == "q" else f"{sq}_k{c}")
            p.run("dve", lambda v, c=c, src_t=src_t: v.tensor_tensor(
                sqt[:], src_t[c][:], src_t[c][:], op=ALU.mult),
                reads=[skey], writes=[f"{sq}_sqt"])
            if kind == "p":
                for nq in range(4):
                    p.run("te", lambda t, c=c, nq=nq: t.matmul(
                        io.pb[nq][0:1, 0:512], io.cbf[:, CBF_ONES:CBF_ONES + 1],
                        sqt[:, nq * 512:(nq + 1) * 512],
                        start=(c == 0), stop=(c == 7)),
                        reads=[f"{sq}_sqt", "cbf"], writes=[f"pb{nq}"])
            else:
                o = 0 if kind == "q" else 16
                for hh in range(2):
                    h = 2 * c + hh
                    for nq in range(4):
                        pbk = 4 + (nq % 2)
                        p.run("te", lambda t, hh=hh, nq=nq, pbk=pbk: t.matmul(
                            io.pb[pbk][0:1, 0:512],
                            io.cbf[64 * hh:64 * hh + 64, CBF_ONES:CBF_ONES + 1],
                            sqt[64 * hh:64 * hh + 64, nq * 512:(nq + 1) * 512],
                            start=True, stop=True),
                            reads=[f"{sq}_sqt", "cbf"], writes=[f"pb{pbk}"])
                        p.run("dve", lambda v, pbk=pbk: v.tensor_reduce(
                            rows[:, 63:64], io.pb[pbk][0:1, 0:512], axis=AX.X,
                            op=ALU.max),
                            reads=[f"pb{pbk}"], writes=[f"{sq}_rtmp"])
                        p.run("dve", lambda v, o=o, h=h: v.tensor_tensor(
                            rows[:, o + h:o + h + 1], rows[:, o + h:o + h + 1],
                            rows[:, 63:64], op=ALU.max),
                            reads=[f"{sq}_rtmp", f"{sq}_rows"], writes=[f"{sq}_rows"])
    for nq in range(4):
        p.run("dve", lambda v, nq=nq: v.tensor_reduce(
            rows[:, 48 + nq:49 + nq], io.pb[nq][0:1, 0:512], axis=AX.X, op=ALU.max),
            reads=[f"pb{nq}"], writes=[f"{sq}_rows"])
    p.run("dve", lambda v: v.tensor_reduce(rows[:, 56:57], rows[:, 48:52],
                                           axis=AX.X, op=ALU.max),
          reads=[f"{sq}_rows"], writes=[f"{sq}_rows"])
    p.run("dve", lambda v: v.tensor_tensor(rows[:, 32:48], rows[:, 0:16],
                                           rows[:, 16:32], op=ALU.mult),
          reads=[f"{sq}_rows"], writes=[f"{sq}_rows"])
    p.run("act", lambda s: s.activation(rows[:, 32:48], rows[:, 32:48], AF.Sqrt),
          reads=[f"{sq}_rows"], writes=[f"{sq}_rows"])
    p.run("dve", lambda v: v.tensor_tensor(rows[:, 57:58], rows[:, 56:57],
                                           io.misc[0:1, 0:1], op=ALU.mult),
          reads=[f"{sq}_rows", "misc"], writes=[f"{sq}_rows"])
    p.run("dve", lambda v: v.tensor_scalar(rows[:, 64:80], rows[:, 32:48],
                                           rows[:, 57:58], -1.0, op0=ALU.add,
                                           op1=ALU.mult),
          reads=[f"{sq}_rows"], writes=[f"{sq}_rows"])
    p.run("dve", lambda v: v.tensor_copy(rowbf[:], rows[:, 64:80]),
          reads=[f"{sq}_rows"], writes=[f"{sq}_rowbf"])
    p.run("te", lambda t: t.matmul(io.pb[6][:, 0:16], io.cbf[0:1, CBF_ONES:CBF_ONES + 128],
                                   rowbf[:], start=True, stop=True),
          reads=[f"{sq}_rowbf", "cbf"], writes=["pb6"])
    p.run("dve", lambda v: v.tensor_copy(negB[:], io.pb[6][:, 0:16]),
          reads=["pb6"], writes=[f"{sq}_negB"])

    p.barrier()
    io.ov_release(mark)

    wP = io.salloc(f"{sq}_wP", [128, 8192], BF16)
    Lp = [io.salloc(f"{sq}_L{j}", [128, 512], F32) for j in range(8)]
    ub = [io.salloc(f"{sq}_u{i}", [128, 512], F32) for i in range(3)]
    exb = [io.salloc(f"{sq}_ex{i}", [128, 512], BF16) for i in range(3)]
    acat = [io.salloc(f"{sq}_ac{c}", [128, 256], BF16) for c in range(8)]
    rbt = io.salloc(f"{sq}_rb", [128, 256], F32)
    rrow = io.salloc(f"{sq}_rrow", [1, 256], F32)
    oprb = io.salloc(f"{sq}_oprb", [128, 256], F32)
    mrb = io.salloc(f"{sq}_mrb", [128, 256], F32)
    x0t = [io.salloc(f"{sq}_x0t{i}", [128, 256], F32) for i in range(2)]
    ptt = [io.salloc(f"{sq}_pt{i}", [128, 256], F32) for i in range(2)]
    t1b = [io.salloc(f"{sq}_t1{i}", [128, 256], F32) for i in range(1)]
    onb = io.salloc(f"{sq}_onb", [128, 512], F32)
    srows = io.salloc(f"{sq}_srows", [1, 1024], F32)

    ldw(wP, f"{sq}_wP", "Wproj")
    st_base = 0 * 4 + seq * 2
    for nb in range(8):
        col0 = nb * 256
        p.dma("sp", srows[:, 0:256], io.d["stats"][st_base:st_base + 1, col0:col0 + 256],
              reads=["stats"], writes=[f"{sq}_srA"])
        p.dma("sp", srows[:, 256:512], io.d["stats"][st_base + 1:st_base + 2,
                                                     col0:col0 + 256],
              reads=["stats"], writes=[f"{sq}_srB"])
        p.run("dve", lambda v: v.tensor_scalar(srows[:, 512:768], srows[:, 256:512],
                                               1.0, None, op0=ALU.add),
              reads=[f"{sq}_srB"], writes=[f"{sq}_srC"])
        p.run("dve", lambda v: v.tensor_tensor(srows[:, 768:1024], srows[:, 0:256],
                                               srows[:, 256:512], op=ALU.mult),
              reads=[f"{sq}_srA", f"{sq}_srB"], writes=[f"{sq}_srD"])
        p.run("te", lambda t: t.matmul(io.pb[6][:, 0:256], io.ones_f32(0, 1, 128),
                                       srows[:, 512:768], start=True, stop=True),
              reads=[f"{sq}_srC", "cf32"], writes=["pb6"])
        p.run("dve", lambda v: v.tensor_copy(oprb[:], io.pb[6][:, 0:256]),
              reads=["pb6"], writes=[f"{sq}_oprb"])
        p.run("te", lambda t: t.matmul(io.pb[6][:, 256:512], io.ones_f32(0, 1, 128),
                                       srows[:, 768:1024], start=True, stop=True),
              reads=[f"{sq}_srD", "cf32"], writes=["pb6"])
        p.run("dve", lambda v: v.tensor_copy(mrb[:], io.pb[6][:, 256:512]),
              reads=["pb6"], writes=[f"{sq}_mrb"])
        for mc in range(16):
            ps, kps = io.pb[mc % 2], f"pb{mc%2}"
            mmacc(p, ps[:, 0:256],
                  [(pre[c][:, mc * 128:(mc + 1) * 128], pre[c][:, col0:col0 + 256])
                   for c in range(8)],
                  reads=[f"{sq}_pre{c}" for c in range(8)], wkey=kps)
            p.run("dve", lambda v, ps=ps, mc=mc: v.tensor_scalar(
                Lp[mc // 2][:, (mc % 2) * 256:(mc % 2) * 256 + 256], ps[:, 0:256],
                io.misc[:, 0:1], None, op0=ALU.mult),
                reads=[kps, "misc"], writes=[f"{sq}_Lp{mc//2}"])
        def _finish_head(h):
            c, off = h // 2, 64 * (h % 2)
            psO, kO = io.pb[4 + h % 2], f"pb{4+h%2}"
            p.run("dve", lambda v, psO=psO: v.reciprocal(rrow[:], psO[64:65, 0:256]),
                  reads=[kO], writes=[f"{sq}_rrow"])
            p.run("te", lambda t: t.matmul(io.pb[6][0:64, 0:256],
                                           io.ones_f32(0, 1, 64), rrow[:],
                                           start=True, stop=True),
                  reads=[f"{sq}_rrow", "cf32"], writes=["pb6"])
            p.run("dve", lambda v: v.tensor_copy(rbt[0:64, :], io.pb[6][0:64, 0:256]),
                  reads=["pb6"], writes=[f"{sq}_rbt"])
            p.run("dve", lambda v, psO=psO, c=c, off=off: v.tensor_tensor(
                acat[c][off:off + 64, :], psO[0:64, 0:256], rbt[0:64, :], op=ALU.mult),
                reads=[kO, f"{sq}_rbt"], writes=[f"{sq}_ac{c}"])

        for h in range(16):
            c, off = h // 2, 64 * (h % 2)
            psO, kO = io.pb[4 + h % 2], f"pb{4+h%2}"
            for j in range(8):
                psS, kS = io.pb[j % 4], f"pb{j%4}"
                p.group("te", [
                    lambda t, psS=psS, c=c, off=off, mc=2 * j + half, half=half,
                    col0=col0: t.matmul(
                        psS[:, half * 256:half * 256 + 256],
                        kk[c][off:off + 64, mc * 128:(mc + 1) * 128],
                        qq[c][off:off + 64, col0:col0 + 256], start=True, stop=True)
                    for half in range(2)],
                    reads=[f"{sq}_k{c}", f"{sq}_q{c}"], writes=[kS])
                p.run("dve", lambda v, psS=psS, j=j: v.tensor_tensor(
                    ub[j % 3][:], psS[:, 0:512], Lp[j][:], op=ALU.add),
                    reads=[kS, f"{sq}_Lp{j}"], writes=[f"{sq}_u{j%3}"])
                p.run("act", lambda s, j=j, h=h: s.activation(
                    exb[j % 3][:], ub[j % 3][:], AF.Exp, bias=negB[:, h:h + 1]),
                    reads=[f"{sq}_u{j%3}", f"{sq}_negB"], writes=[f"{sq}_ex{j%3}"])
                p.group("te", [
                    lambda t, psO=psO, mc=2 * j + half, half=half, j=j, h=h: t.matmul(
                        psO[0:65, 0:256], vx[mc][:, 65 * h:65 * h + 65],
                        exb[j % 3][:, half * 256:half * 256 + 256],
                        start=(mc == 0), stop=(mc == 15))
                    for half in range(2)],
                    reads=[f"{sq}_vx{2*j}", f"{sq}_vx{2*j+1}", f"{sq}_ex{j%3}"],
                    writes=[kO])
                if j == 1 and h > 0:
                    _finish_head(h - 1)
        _finish_head(15)
        for m in range(8):
            ps, kps = io.pb[2 + m % 2], f"pb{2+m%2}"
            mmacc(p, ps[:, 0:256],
                  [(_wslices(wP, k, m * 128, 128), acat[k][:]) for k in range(8)],
                  reads=[f"{sq}_wP"] + [f"{sq}_ac{k}" for k in range(8)], wkey=kps)
            p.dma("sp", x0t[m % 2][:], io.d["x0"][row0 + m * 128: row0 + (m + 1) * 128,
                                                  col0:col0 + 256],
                  reads=["x0"], writes=[f"{sq}_x0t{m%2}"])
            p.run("pool", lambda g, m=m: g.tensor_tensor(
                ptt[m % 2][:], x0t[m % 2][:], oprb[:], op=ALU.mult),
                reads=[f"{sq}_x0t{m%2}", f"{sq}_oprb"], writes=[f"{sq}_pt{m%2}"])
            p.run("dve", lambda v, ps=ps, m=m: v.scalar_tensor_tensor(
                t1b[0][:], ps[:, 0:256], io.acol("bproj", m), mrb[:],
                op0=ALU.add, op1=ALU.subtract),
                reads=[kps, "arena", f"{sq}_mrb"], writes=[f"{sq}_t10"])
            p.run("dve", lambda v, m=m: v.tensor_tensor(
                onb[:, (m % 2) * 256:(m % 2) * 256 + 256], t1b[0][:],
                ptt[m % 2][:], op=ALU.add),
                reads=[f"{sq}_t10", f"{sq}_pt{m%2}"], writes=[f"{sq}_onb"])
            if m % 2 == 1:
                g = m // 2
                dst = io.d["xr"][row0 + g * 256:row0 + g * 256 + 256,
                                 col0:col0 + 256].rearrange("(m p) n -> p m n", p=128)
                p.dma("act", dst, onb[:].rearrange("p (m n) -> p m n", n=256),
                      reads=[f"{sq}_onb"], writes=["xr"])
    p.barrier()



def emit_moe0(p, io):
    nc = io.nc
    io.ov_reset()
    xnt = [io.salloc(f"m2_xn{c}", [128, S], BF16) for c in range(8)]
    acc = [io.salloc(f"m2_acc{c}", [128, S], F32) for c in range(8)]
    combT = io.salloc("m2_combT", [32, T], BF16)
    wg = io.salloc("m2_wg", [128, 8 * 32], BF16)
    guw = [io.salloc(f"m2_guw{i}", [128, 8192], BF16) for i in range(2)]
    dw = [io.salloc(f"m2_dw{i}", [128, 4096], BF16) for i in range(2)]
    hgt = [io.salloc(f"m2_hg{i}", [128, 512], BF16) for i in range(8)]
    sgt = [io.salloc(f"m2_sg{i}", [128, 512], BF16) for i in range(2)]
    htt = [io.salloc(f"m2_h{i}", [128, 512], BF16) for i in range(2)]
    gb = [io.salloc(f"m2_gb{i}", [128, 512], BF16) for i in range(2)]
    probs = io.salloc("m2_probs", [128, 32], F32)
    cur = io.salloc("m2_cur", [128, 32], F32)
    selm = io.salloc("m2_selm", [128, 32], F32)
    eqt = io.salloc("m2_eq", [128, 32], F32)
    tmp32 = io.salloc("m2_tmp", [128, 32], F32)
    combf = io.salloc("m2_combf", [128, 32], F32)
    combbf = io.salloc("m2_combbf", [128, 32], BF16)
    cols = io.salloc("m2_cols", [128, 8], F32)

    p.dma("pool", wg[:].rearrange("p (c m) -> p c m", m=32),
          io.d["gwT"].rearrange("(c p) m -> p c m", p=128),
          reads=["gwT"], writes=["m2_wg"])

    for sb in range(2):
        row0 = sb * 1024
        tok0 = sb * S
        for c in range(8):
            p.dma("sp", xnt[c][:], io.d["xnh"][row0 + c * 128: row0 + (c + 1) * 128, :],
                  reads=["xnh"], writes=[f"m2_xn{c}"])
            p.dma("sp", acc[c][:], io.d["xr"][row0 + c * 128: row0 + (c + 1) * 128, :],
                  reads=["xr"], writes=[f"m2_acc{c}"])
        def _gate_tt(tt):
            ps, kps = io.pb[6], "pb6"
            mmacc(p, ps[:, 0:32],
                  [(xnt[c][:, tt * 128:(tt + 1) * 128], wg[:, c * 32:(c + 1) * 32])
                   for c in range(8)],
                  reads=[f"m2_xn{c}" for c in range(8)] + ["m2_wg"], wkey=kps)
            p.run("dve", lambda v, ps=ps: v.reduce_max(cols[:, 0:1], ps[:, 0:32],
                                                       axis=AX.X, negate=True),
                  reads=[kps], writes=["m2_cols"])
            p.run("act", lambda s, ps=ps: s.activation(probs[:], ps[:, 0:32], AF.Exp,
                                                       bias=cols[:, 0:1]),
                  reads=[kps, "m2_cols"], writes=["m2_probs"])
            p.run("dve", lambda v: v.tensor_reduce(cols[:, 2:3], probs[:], axis=AX.X,
                                                   op=ALU.add),
                  reads=["m2_probs"], writes=["m2_cols"])
            p.run("dve", lambda v: v.reciprocal(cols[:, 3:4], cols[:, 2:3]),
                  reads=["m2_cols"], writes=["m2_cols"])
            p.run("dve", lambda v: v.tensor_scalar(probs[:], probs[:], cols[:, 3:4],
                                                   None, op0=ALU.mult),
                  reads=["m2_probs", "m2_cols"], writes=["m2_probs"])
            p.run("dve", lambda v: v.tensor_copy(cur[:], probs[:]),
                  reads=["m2_probs"], writes=["m2_cur"])
            p.run("dve", lambda v: v.memset(selm[:], 0.0), writes=["m2_selm"])
            p.run("dve", lambda v: v.memset(cols[:, 4:5], 0.0), writes=["m2_cols"])
            for i in range(8):
                p.run("dve", lambda v: v.reduce_max(cols[:, 5:6], cur[:], axis=AX.X),
                      reads=["m2_cur"], writes=["m2_cols"])
                p.run("dve", lambda v: v.tensor_tensor(cols[:, 4:5], cols[:, 4:5],
                                                       cols[:, 5:6], op=ALU.add),
                      reads=["m2_cols"], writes=["m2_cols"])
                p.run("dve", lambda v: v.tensor_scalar(eqt[:], cur[:], cols[:, 5:6],
                                                       None, op0=ALU.is_equal),
                      reads=["m2_cur", "m2_cols"], writes=["m2_eq"])
                p.run("dve", lambda v: v.tensor_tensor(selm[:], selm[:], eqt[:],
                                                       op=ALU.add),
                      reads=["m2_selm", "m2_eq"], writes=["m2_selm"])
                p.run("dve", lambda v: v.scalar_tensor_tensor(
                    tmp32[:], eqt[:], cols[:, 5:6], eqt[:], op0=ALU.mult, op1=ALU.add),
                    reads=["m2_eq", "m2_cols"], writes=["m2_tmp"])
                p.run("dve", lambda v: v.tensor_tensor(cur[:], cur[:], tmp32[:],
                                                       op=ALU.subtract),
                      reads=["m2_cur", "m2_tmp"], writes=["m2_cur"])
            p.run("dve", lambda v: v.reciprocal(cols[:, 6:7], cols[:, 4:5]),
                  reads=["m2_cols"], writes=["m2_cols"])
            p.run("dve", lambda v: v.tensor_tensor(combf[:], selm[:], probs[:],
                                                   op=ALU.mult),
                  reads=["m2_selm", "m2_probs"], writes=["m2_combf"])
            p.run("dve", lambda v: v.tensor_scalar(combbf[:], combf[:], cols[:, 6:7],
                                                   None, op0=ALU.mult),
                  reads=["m2_combf", "m2_cols"], writes=["m2_combbf"])
            p.run("te", lambda t: t.transpose(io.pbt[0:32, 0:128], combbf[:],
                                              io.ident()),
                  reads=["m2_combbf", "cbf"], writes=["pbt"])
            p.run("dve", lambda v, tt=tt, tok0=tok0: v.tensor_copy(
                combT[:, tok0 + tt * 128: tok0 + (tt + 1) * 128], io.pbt[0:32, 0:128]),
                reads=["pbt"], writes=[f"m2_combT{tt}"])
        for e in range(32):
            b = e % 2
            p.dma("pool", guw[b][:].rearrange("p (c m) -> p c m", m=1024),
                  io.d["guwT"][e].rearrange("(c p) m -> p c m", p=128),
                  reads=["guwT"], writes=[f"m2_guw{b}"])
            p.dma("pool", dw[b][:].rearrange("p (c m) -> p c m", m=1024),
                  io.d["dwT"][e].rearrange("(c p) m -> p c m", p=128),
                  reads=["dwT"], writes=[f"m2_dw{b}"])
            def _emit_y(nb, b=b, col0_of=lambda nb: nb * 512):
                col0 = col0_of(nb)
                for out in range(8):
                    psY, kY = io.pb[4 + out % 2], f"pb{4+out%2}"
                    hks = [4 * (nb % 2) + k for k in range(4)]
                    mmacc(p, psY[:, 0:512],
                          [(_wslices(dw[b], k, out * 128, 128), hgt[4 * (nb % 2) + k][:])
                           for k in range(4)],
                          reads=[f"m2_dw{b}"] + [f"m2_hg{k}" for k in hks],
                          wkey=kY)
                    p.run("dve", lambda v, psY=psY, out=out, col0=col0: v.tensor_tensor(
                        acc[out][:, col0:col0 + 512], psY[:, 0:512],
                        acc[out][:, col0:col0 + 512], op=ALU.add),
                        reads=[kY, f"m2_acc{out}"], writes=[f"m2_acc{out}"])

            for nb in range(4):
                col0 = nb * 512
                if e == 0:
                    for tt in range(4 * nb, 4 * nb + 4):
                        _gate_tt(tt)
                p.run("te", lambda t, e=e, tok0=tok0, col0=col0: t.matmul(
                    io.pb[6][:, 0:512],
                    io.cbf[0:32, CBF_OH32 + e * 128: CBF_OH32 + (e + 1) * 128],
                    combT[0:32, tok0 + col0: tok0 + col0 + 512],
                    start=True, stop=True),
                    reads=[f"m2_combT{tt}" for tt in range(4 * nb, 4 * nb + 4)] + ["cbf"],
                    writes=["pb6"])
                p.run("dve", lambda v, nb=nb: v.tensor_copy(gb[nb % 2][:],
                                                            io.pb[6][:, 0:512]),
                      reads=["pb6"], writes=[f"m2_gb{nb%2}"])
                xnks = [f"m2_xn{k}" for k in range(8)]
                for mp in range(4):
                    psG, kG = io.pb[mp % 2], f"pb{mp%2}"
                    psU, kU = io.pb[2 + mp % 2], f"pb{2+mp%2}"
                    mmacc(p, psG[:, 0:512],
                          [(_wslices(guw[b], k, mp * 128, 128),
                            xnt[k][:, col0:col0 + 512]) for k in range(8)],
                          reads=[f"m2_guw{b}"] + xnks, wkey=kG)
                    mmacc(p, psU[:, 0:512],
                          [(_wslices(guw[b], k, (mp + 4) * 128, 128),
                            xnt[k][:, col0:col0 + 512]) for k in range(8)],
                          reads=[f"m2_guw{b}"] + xnks, wkey=kU)
                    hj = 4 * (nb % 2) + mp
                    p.run("act", lambda s, psG=psG, mp=mp: s.activation(
                        sgt[mp % 2][:], psG[:, 0:512], AF.Silu),
                        reads=[kG], writes=[f"m2_sg{mp%2}"])
                    p.run("dve", lambda v, psU=psU, mp=mp: v.tensor_tensor(
                        htt[mp % 2][:], psU[:, 0:512], sgt[mp % 2][:], op=ALU.mult),
                        reads=[kU, f"m2_sg{mp%2}"], writes=[f"m2_h{mp%2}"])
                    p.run("pool", lambda g, mp=mp, nb=nb, hj=hj: g.tensor_tensor(
                        hgt[hj][:], htt[mp % 2][:], gb[nb % 2][:], op=ALU.mult),
                        reads=[f"m2_h{mp%2}", f"m2_gb{nb%2}"], writes=[f"m2_hg{hj}"])
                if nb > 0:
                    _emit_y(nb - 1)
            _emit_y(3)
        for c in range(8):
            p.dma("act", io.d["xr"][row0 + c * 128: row0 + (c + 1) * 128, :], acc[c][:],
                  reads=[f"m2_acc{c}"], writes=["xr"])
    p.barrier()



def _ldcol(p, io, buf, key, wname, col_lo, col_cnt, rows=1024, e=None):
    ap = io.d[wname] if e is None else io.d[wname][e]
    src = ap[:, col_lo:col_lo + col_cnt].rearrange("(c p) m -> p c m", p=128)
    dst = buf[:].rearrange("p (c m) -> p c m", m=col_cnt)[:, 0:rows // 128, :]
    p.dma("pool", dst, src, reads=[wname], writes=[key])


def emit_mixer1(p, io, seq):
    nc = io.nc
    sq = f"x{seq}"
    row0 = seq * 1024
    io.ov_reset()
    normed = [io.salloc(f"{sq}_nm{c}", [128, S], BF16) for c in range(8)]
    rwkv = [io.salloc(f"{sq}_rw{c}", [128, S], BF16) for c in range(8)]

    for j in range(8):
        p.run("act", lambda s, j=j: s.activation(io.misc[:, 8 + j:9 + j],
                                                 io.acol("td", j), AF.Exp),
              reads=["arena"], writes=["misc_ew"])
        p.run("act", lambda s, j=j: s.activation(io.misc[:, 8 + j:9 + j],
                                                 io.misc[:, 8 + j:9 + j], AF.Exp,
                                                 scale=-1.0),
              reads=["misc_ew"], writes=["misc_ew"])
        p.run("act", lambda s, j=j: s.activation(io.misc[:, 16 + j:17 + j],
                                                 io.acol("tf", j), AF.Exp),
              reads=["arena"], writes=["misc_eu"])

    mark = io.ov_mark()
    xnt = [io.salloc(f"{sq}_xt{c}", [128, S], BF16) for c in range(8)]
    sqt = io.salloc(f"{sq}_sqt", [128, 512], BF16)
    rrows = io.salloc(f"{sq}_rr", [1, 512], F32)
    rbt = io.salloc(f"{sq}_rbt", [128, 512], F32)
    for c in range(8):
        p.dma("sp", xnt[c][:], io.d["xnh"][row0 + c * 128: row0 + (c + 1) * 128, :],
              reads=["xnh"], writes=[f"{sq}_xt{c}"])
    for nq in range(4):
        ps, kps = io.pb[nq % 2], f"pb{nq%2}"
        for c in range(8):
            p.run("dve", lambda v, c=c, nq=nq: v.tensor_tensor(
                sqt[:], xnt[c][:, nq * 512:(nq + 1) * 512], xnt[c][:, nq * 512:(nq + 1) * 512],
                op=ALU.mult),
                reads=[f"{sq}_xt{c}"], writes=[f"{sq}_sqt"])
            p.run("te", lambda t, ps=ps, c=c: t.matmul(
                ps[0:1, 0:512], io.cbf[:, CBF_ONES:CBF_ONES + 1], sqt[:],
                start=(c == 0), stop=(c == 7)),
                reads=[f"{sq}_sqt", "cbf"], writes=[kps])
        p.run("dve", lambda v, ps=ps: v.tensor_scalar(rrows[:], ps[0:1, 0:512],
                                                      1.0 / 1024, None, op0=ALU.mult),
              reads=[kps], writes=[f"{sq}_rr"])
        p.run("act", lambda s: s.activation(rrows[:], rrows[:], AF.Sqrt, bias=1e-6),
              reads=[f"{sq}_rr"], writes=[f"{sq}_rr"])
        p.run("dve", lambda v: v.reciprocal(rrows[:], rrows[:]),
              reads=[f"{sq}_rr"], writes=[f"{sq}_rr"])
        p.run("te", lambda t: t.matmul(io.pb[6][:, 0:512], io.ones_f32(0, 1, 128),
                                       rrows[:], start=True, stop=True),
              reads=[f"{sq}_rr", "cf32"], writes=["pb6"])
        p.run("dve", lambda v: v.tensor_copy(rbt[:], io.pb[6][:, 0:512]),
              reads=["pb6"], writes=[f"{sq}_rbt"])
        for c in range(8):
            p.run("dve", lambda v, c=c, nq=nq: v.tensor_tensor(
                normed[c][:, nq * 512:(nq + 1) * 512],
                xnt[c][:, nq * 512:(nq + 1) * 512], rbt[:], op=ALU.mult),
                reads=[f"{sq}_xt{c}", f"{sq}_rbt"], writes=[f"{sq}_nm{c}"])
    p.barrier()
    io.ov_release(mark)

    wk = [io.salloc(f"{sq}_wk{i}", [128, 1024], BF16) for i in range(2)]
    wv = [io.salloc(f"{sq}_wv{i}", [128, 1024], BF16) for i in range(2)]
    wr = [io.salloc(f"{sq}_wr{i}", [128, 1024], BF16) for i in range(2)]
    ksb = io.salloc(f"{sq}_ksb", [128, S], F32)
    kvb = io.salloc(f"{sq}_kv", [128, S], F32)
    rsb = io.salloc(f"{sq}_rsb", [128, S], BF16)
    ewt = io.salloc(f"{sq}_ewt", [128, S], F32)
    scan = io.salloc(f"{sq}_scan", [128, S + 1], F32)
    _ldcol(p, io, wk[0], f"{sq}_wk0", "WkT1", 0, 128)
    _ldcol(p, io, wv[0], f"{sq}_wv0", "WvT1", 0, 128)
    _ldcol(p, io, wr[0], f"{sq}_wr0", "WrT1", 0, 128)
    for oc in range(8):
        b = oc % 2
        if oc < 7:
            nb_ = (oc + 1) % 2
            _ldcol(p, io, wk[nb_], f"{sq}_wk{nb_}", "WkT1", (oc + 1) * 128, 128)
            _ldcol(p, io, wv[nb_], f"{sq}_wv{nb_}", "WvT1", (oc + 1) * 128, 128)
            _ldcol(p, io, wr[nb_], f"{sq}_wr{nb_}", "WrT1", (oc + 1) * 128, 128)
        for nq in range(4):
            psK, kK = io.pb[0], "pb0"
            psV, kV = io.pb[1], "pb1"
            psR, kR = io.pb[2], "pb2"
            mmacc(p, psK[:, 0:512],
                  [(wk[b][:, k * 128:(k + 1) * 128],
                    normed[k][:, nq * 512:(nq + 1) * 512]) for k in range(8)],
                  reads=[f"{sq}_wk{b}"] + [f"{sq}_nm{k}" for k in range(8)], wkey=kK)
            p.run("dve", lambda v, psK=psK, nq=nq: v.tensor_copy(
                ksb[:, nq * 512:(nq + 1) * 512], psK[:, 0:512]),
                reads=[kK], writes=[f"{sq}_ksb"])
            mmacc(p, psV[:, 0:512],
                  [(wv[b][:, k * 128:(k + 1) * 128],
                    normed[k][:, nq * 512:(nq + 1) * 512]) for k in range(8)],
                  reads=[f"{sq}_wv{b}"] + [f"{sq}_nm{k}" for k in range(8)], wkey=kV)
            p.run("dve", lambda v, psV=psV, nq=nq: v.tensor_tensor(
                kvb[:, nq * 512:(nq + 1) * 512], psV[:, 0:512],
                ksb[:, nq * 512:(nq + 1) * 512], op=ALU.mult),
                reads=[kV, f"{sq}_ksb"], writes=[f"{sq}_kv"])
            mmacc(p, psR[:, 0:512],
                  [(wr[b][:, k * 128:(k + 1) * 128],
                    normed[k][:, nq * 512:(nq + 1) * 512]) for k in range(8)],
                  reads=[f"{sq}_wr{b}"] + [f"{sq}_nm{k}" for k in range(8)], wkey=kR)
            p.run("act", lambda s, psR=psR, nq=nq: s.activation(
                rsb[:, nq * 512:(nq + 1) * 512], psR[:, 0:512], AF.Sigmoid),
                reads=[kR], writes=[f"{sq}_rsb"])
        p.run("dve", lambda v: v.memset(ewt[:], 1.0), writes=[f"{sq}_ewt"])
        p.run("dve", lambda v, oc=oc: v.tensor_scalar(
            ewt[:], ewt[:], io.misc[:, 8 + oc:9 + oc], None, op0=ALU.mult),
            reads=[f"{sq}_ewt", "misc_ew"], writes=[f"{sq}_ewt"])
        p.run("dve", lambda v: v.memset(scan[:, 0:1], 0.0), writes=[f"{sq}_scan"])
        p.run("dve", lambda v: v.tensor_tensor_scan(
            scan[:, 1:S + 1], ewt[:], kvb[:], 0.0, op0=ALU.mult, op1=ALU.add),
            reads=[f"{sq}_ewt", f"{sq}_kv", f"{sq}_scan"], writes=[f"{sq}_scan"])
        p.run("dve", lambda v, oc=oc: v.scalar_tensor_tensor(
            ewt[:], kvb[:], io.misc[:, 16 + oc:17 + oc], scan[:, 0:S],
            op0=ALU.mult, op1=ALU.add),
            reads=[f"{sq}_kv", "misc_eu", f"{sq}_scan", f"{sq}_ewt"], writes=[f"{sq}_ewt"])
        p.run("dve", lambda v, oc=oc: v.tensor_tensor(
            rwkv[oc][:], ewt[:], rsb[:], op=ALU.mult),
            reads=[f"{sq}_ewt", f"{sq}_rsb"], writes=[f"{sq}_rw{oc}"])
    wo = [io.salloc(f"{sq}_wo{i}", [128, 1024], BF16) for i in range(2)]
    srows = io.salloc(f"{sq}_sr", [1, 1024], F32)
    r1b4 = io.salloc(f"{sq}_r1b", [128, S], F32)
    mrb4 = io.salloc(f"{sq}_mrb", [128, S], F32)
    x1t = [io.salloc(f"{sq}_x1t{i}", [128, 512], F32) for i in range(2)]
    t1t = [io.salloc(f"{sq}_t1t{i}", [128, 512], F32) for i in range(2)]
    t2t = [io.salloc(f"{sq}_t2t{i}", [128, 512], F32) for i in range(2)]
    xit = [io.salloc(f"{sq}_xit{i}", [128, 512], F32) for i in range(2)]
    st_base = 4 + seq * 2
    for nq in range(4):
        col0 = nq * 512
        p.dma("sp", srows[:, 0:512], io.d["stats"][st_base:st_base + 1, col0:col0 + 512],
              reads=["stats"], writes=[f"{sq}_srA"])
        p.dma("sp", srows[:, 512:1024], io.d["stats"][st_base + 1:st_base + 2,
                                                      col0:col0 + 512],
              reads=["stats"], writes=[f"{sq}_srB"])
        p.run("te", lambda t, col0=col0: t.matmul(
            io.pb[6][:, 0:512], io.ones_f32(0, 1, 128),
            srows[:, 512:1024], start=True, stop=True),
            reads=[f"{sq}_srB", "cf32"], writes=["pb6"])
        p.run("dve", lambda v, col0=col0: v.tensor_copy(
            r1b4[:, col0:col0 + 512], io.pb[6][:, 0:512]),
            reads=["pb6"], writes=[f"{sq}_r1b"])
        p.run("dve", lambda v: v.tensor_tensor(srows[:, 0:512], srows[:, 0:512],
                                               srows[:, 512:1024], op=ALU.mult),
              reads=[f"{sq}_srA", f"{sq}_srB"], writes=[f"{sq}_srA"])
        p.run("te", lambda t: t.matmul(io.pb[6][:, 0:512], io.ones_f32(0, 1, 128),
                                       srows[:, 0:512], start=True, stop=True),
              reads=[f"{sq}_srA", "cf32"], writes=["pb6"])
        p.run("dve", lambda v, col0=col0: v.tensor_copy(
            mrb4[:, col0:col0 + 512], io.pb[6][:, 0:512]),
            reads=["pb6"], writes=[f"{sq}_mrb"])
    _ldcol(p, io, wo[0], f"{sq}_wo0", "WoT1", 0, 128)
    for out in range(8):
        b2_ = out % 2
        if out < 7:
            nb_ = (out + 1) % 2
            _ldcol(p, io, wo[nb_], f"{sq}_wo{nb_}", "WoT1", (out + 1) * 128, 128)
        for nq in range(4):
            col0 = nq * 512
            ps, kps = io.pb[3 + nq % 2], f"pb{3+nq%2}"
            mmacc(p, ps[:, 0:512],
                  [(wo[b2_][:, k * 128:(k + 1) * 128], rwkv[k][:, col0:col0 + 512])
                   for k in range(8)],
                  reads=[f"{sq}_wo{b2_}"] + [f"{sq}_rw{k}" for k in range(8)], wkey=kps)
            p.run("dve", lambda v, ps=ps, out=out, col0=col0, nq=nq: v.scalar_tensor_tensor(
                t1t[nq % 2][:], ps[:, 0:512], io.acol("ms", out),
                mrb4[:, col0:col0 + 512], op0=ALU.mult, op1=ALU.subtract),
                reads=[kps, "arena", f"{sq}_mrb"], writes=[f"{sq}_t1t{nq%2}"])
            p.dma("sp", x1t[nq % 2][:], io.d["xr"][row0 + out * 128: row0 + out * 128 + 128,
                                                   col0:col0 + 512],
                  reads=["xr"], writes=[f"{sq}_x1t{nq%2}"])
            p.run("pool", lambda g, nq=nq, col0=col0: g.tensor_tensor(
                t2t[nq % 2][:], x1t[nq % 2][:], r1b4[:, col0:col0 + 512], op=ALU.mult),
                reads=[f"{sq}_x1t{nq%2}", f"{sq}_r1b"], writes=[f"{sq}_t2t{nq%2}"])
            p.run("dve", lambda v, nq=nq: v.tensor_tensor(
                xit[nq % 2][:], t1t[nq % 2][:], t2t[nq % 2][:], op=ALU.add),
                reads=[f"{sq}_t1t{nq%2}", f"{sq}_t2t{nq%2}"], writes=[f"{sq}_xit{nq%2}"])
            p.dma("act", io.d["xih"][row0 + out * 128: row0 + out * 128 + 128,
                                     col0:col0 + 512], xit[nq % 2][:],
                  reads=[f"{sq}_xit{nq%2}"], writes=["xih"])
    p.barrier()



def emit_mlp1(p, io, seq):
    sq = f"y{seq}"
    row0 = seq * 1024
    io.ov_reset()
    xi = [io.salloc(f"{sq}_xi{c}", [128, S], F32) for c in range(8)]
    nm2 = [io.salloc(f"{sq}_n2{c}", [128, S], BF16) for c in range(8)]
    h2 = [io.salloc(f"{sq}_h2{c}", [128, 512], BF16) for c in range(32)]
    sqt = io.salloc(f"{sq}_sqt", [128, 512], F32)
    lrt = [io.salloc(f"{sq}_lr{i}", [128, 512], BF16) for i in range(2)]
    rrows = io.salloc(f"{sq}_rr", [1, 512], F32)
    rbt = io.salloc(f"{sq}_rbt", [128, 512], F32)
    wfc = [io.salloc(f"{sq}_wfc{i}", [128, 4096], BF16) for i in range(2)]
    wml = [io.salloc(f"{sq}_wml{i}", [128, 4096], BF16) for i in range(2)]
    x1t = [io.salloc(f"{sq}_x1t{i}", [128, 512], F32) for i in range(2)]
    xot = [io.salloc(f"{sq}_xot{i}", [128, 512], F32) for i in range(2)]
    for c in range(8):
        p.dma("sp", xi[c][:], io.d["xih"][row0 + c * 128: row0 + (c + 1) * 128, :],
              reads=["xih"], writes=[f"{sq}_xi{c}"])
    for nq in range(4):
        ps, kps = io.pb[nq % 2], f"pb{nq%2}"
        for c in range(8):
            p.run("act", lambda s, c=c, nq=nq: s.activation(
                sqt[:], xi[c][:, nq * 512:(nq + 1) * 512], AF.Square),
                reads=[f"{sq}_xi{c}"], writes=[f"{sq}_sqt"])
            p.run("te", lambda t, ps=ps, c=c: t.matmul(
                ps[0:1, 0:512], io.ones_f32(0, 128, 1), sqt[:],
                start=(c == 0), stop=(c == 7)),
                reads=[f"{sq}_sqt", "cf32"], writes=[kps])
        p.run("dve", lambda v, ps=ps: v.tensor_scalar(rrows[:], ps[0:1, 0:512],
                                                      1.0 / 1024, None, op0=ALU.mult),
              reads=[kps], writes=[f"{sq}_rr"])
        p.run("act", lambda s: s.activation(rrows[:], rrows[:], AF.Sqrt, bias=1e-6),
              reads=[f"{sq}_rr"], writes=[f"{sq}_rr"])
        p.run("dve", lambda v: v.reciprocal(rrows[:], rrows[:]),
              reads=[f"{sq}_rr"], writes=[f"{sq}_rr"])
        p.run("te", lambda t: t.matmul(io.pb[6][:, 0:512], io.ones_f32(0, 1, 128),
                                       rrows[:], start=True, stop=True),
              reads=[f"{sq}_rr", "cf32"], writes=["pb6"])
        p.run("dve", lambda v: v.tensor_copy(rbt[:], io.pb[6][:, 0:512]),
              reads=["pb6"], writes=[f"{sq}_rbt"])
        for c in range(8):
            p.run("dve", lambda v, c=c, nq=nq: v.tensor_tensor(
                nm2[c][:, nq * 512:(nq + 1) * 512], xi[c][:, nq * 512:(nq + 1) * 512],
                rbt[:], op=ALU.mult),
                reads=[f"{sq}_xi{c}", f"{sq}_rbt"], writes=[f"{sq}_n2{c}"])
    _ldcol(p, io, wfc[0], f"{sq}_wfc0", "WfcT", 0, 512)
    for nq in range(4):
        col0 = nq * 512
        for ocg in range(8):
            b = ocg % 2
            nxt = ocg + 1
            if nxt < 8:
                _ldcol(p, io, wfc[nxt % 2], f"{sq}_wfc{nxt%2}", "WfcT", nxt * 512, 512)
            elif nq < 3:
                _ldcol(p, io, wfc[0], f"{sq}_wfc0", "WfcT", 0, 512)
            for oi in range(4):
                oc = ocg * 4 + oi
                ps, kps = io.pb[oc % 2], f"pb{oc%2}"
                mmacc(p, ps[:, 0:512],
                      [(wfc[b][:, k * 512 + oi * 128: k * 512 + oi * 128 + 128],
                        nm2[k][:, col0:col0 + 512]) for k in range(8)],
                      reads=[f"{sq}_wfc{b}"] + [f"{sq}_n2{k}" for k in range(8)],
                      wkey=kps)
                p.run("act", lambda s, ps=ps, oc=oc: s.activation(
                    lrt[oc % 2][:], ps[:, 0:512], AF.Prelu, alpha=0.5),
                    reads=[kps], writes=[f"{sq}_lr{oc%2}"])
                p.run("act", lambda s, oc=oc: s.activation(
                    h2[oc][:], lrt[oc % 2][:], AF.Square),
                    reads=[f"{sq}_lr{oc%2}"], writes=[f"{sq}_h2{oc}"])
        if nq == 0:
            _ldcol(p, io, wml[0], f"{sq}_wml0", "WmlpT", 0, 128, rows=4096)
        for out in range(8):
            b = out % 2
            nxt = out + 1
            if nxt < 8:
                _ldcol(p, io, wml[nxt % 2], f"{sq}_wml{nxt%2}", "WmlpT",
                       nxt * 128, 128, rows=4096)
            elif nq < 3:
                _ldcol(p, io, wml[0], f"{sq}_wml0", "WmlpT", 0, 128, rows=4096)
            ps, kps = io.pb[3 + out % 2], f"pb{3+out%2}"
            mmacc(p, ps[:, 0:512],
                  [(wml[b][:, k * 128:(k + 1) * 128], h2[k][:]) for k in range(32)],
                  reads=[f"{sq}_wml{b}"] + [f"{sq}_h2{k}" for k in range(32)],
                  wkey=kps)
            p.dma("sp", x1t[out % 2][:], io.d["xr"][row0 + out * 128: row0 + out * 128 + 128,
                                                    col0:col0 + 512],
                  reads=["xr"], writes=[f"{sq}_x1t{out%2}"])
            p.run("dve", lambda v, ps=ps, out=out, col0=col0: v.scalar_tensor_tensor(
                xot[out % 2][:], ps[:, 0:512], io.acol("mls", out),
                xi[out][:, col0:col0 + 512], op0=ALU.mult, op1=ALU.add),
                reads=[kps, "arena", f"{sq}_xi{out}"], writes=[f"{sq}_xot{out%2}"])
            p.run("dve", lambda v, out=out: v.tensor_tensor(
                xot[out % 2][:], xot[out % 2][:], x1t[out % 2][:], op=ALU.add),
                reads=[f"{sq}_xot{out%2}", f"{sq}_x1t{out%2}"], writes=[f"{sq}_xot{out%2}"])
            p.dma("act", io.d["xr2"][row0 + out * 128: row0 + out * 128 + 128,
                                     col0:col0 + 512], xot[out % 2][:],
                  reads=[f"{sq}_xot{out%2}"], writes=["xr2"])
    p.barrier()



def emit_moe1(p, io):
    io.ov_reset()
    xnt = [io.salloc(f"m6_xn{c}", [128, 1024], BF16) for c in range(8)]
    acc = [io.salloc(f"m6_acc{c}", [128, 1024], F32) for c in range(8)]
    hgt = [io.salloc(f"m6_hg{j}", [128, 1024], BF16) for j in range(16)]
    gatesT = io.salloc("m6_gT", [16, T], BF16)
    wbank = [io.salloc(f"m6_w{i}", [128, 8192], BF16) for i in range(5)]
    wgl = io.salloc("m6_wgl", [128, 8 * 16], BF16)
    b2sb = io.salloc("m6_b2", [16, 1024], BF16)
    gb = io.salloc("m6_gb", [128, 1024], BF16)
    sgt = [io.salloc(f"m6_sg{i}", [128, 512], BF16) for i in range(2)]
    htmp = [io.salloc(f"m6_ht{i}", [128, 512], BF16) for i in range(2)]
    lsb = io.salloc("m6_lsb", [128, 16], F32)
    l2sb = io.salloc("m6_l2", [128, 16], F32)
    eq1 = io.salloc("m6_eq1", [128, 16], F32)
    eq2 = io.salloc("m6_eq2", [128, 16], F32)
    gts = io.salloc("m6_gts", [128, 16], F32)
    gtb = io.salloc("m6_gtb", [128, 16], BF16)
    cols = io.salloc("m6_cols", [128, 8], F32)

    p.dma("pool", wgl[:].rearrange("p (c m) -> p c m", m=16),
          io.d["Wg"].rearrange("(c p) m -> p c m", p=128),
          reads=["Wg"], writes=["m6_wgl"])
    p.dma("pool", b2sb[:], io.d["b2e"], reads=["b2e"], writes=["m6_b2"])

    def wload(bi, wname, col_lo, col_cnt, rows, e=None):
        _ldcol(p, io, wbank[bi], f"m6_w{bi}", wname, col_lo, col_cnt, rows=rows, e=e)

    for pa in range(4):
        seq, part = pa // 2, pa % 2
        row0 = seq * 1024
        tok0 = pa * 1024
        colp = part * 1024
        for c in range(8):
            p.dma("sp", xnt[c][:], io.d["xnh"][row0 + c * 128: row0 + (c + 1) * 128,
                                               colp:colp + 1024],
                  reads=["xnh"], writes=[f"m6_xn{c}"])
            p.dma("sp", acc[c][:], io.d["xr2"][row0 + c * 128: row0 + (c + 1) * 128,
                                               colp:colp + 1024],
                  reads=["xr2"], writes=[f"m6_acc{c}"])
        def _gate_tt(tt):
            ps, kps = io.pb[6], "pb6"
            mmacc(p, ps[:, 0:16],
                  [(xnt[c][:, tt * 128:(tt + 1) * 128], wgl[:, c * 16:(c + 1) * 16])
                   for c in range(8)],
                  reads=[f"m6_xn{c}" for c in range(8)] + ["m6_wgl"], wkey=kps)
            p.run("dve", lambda v, ps=ps: v.tensor_copy(lsb[:], ps[:, 0:16]),
                  reads=[kps], writes=["m6_lsb"])
            p.run("dve", lambda v: v.reduce_max(cols[:, 0:1], lsb[:], axis=AX.X),
                  reads=["m6_lsb"], writes=["m6_cols"])
            p.run("dve", lambda v: v.tensor_scalar(eq1[:], lsb[:], cols[:, 0:1],
                                                   None, op0=ALU.is_equal),
                  reads=["m6_lsb", "m6_cols"], writes=["m6_eq1"])
            p.run("dve", lambda v: v.scalar_tensor_tensor(
                l2sb[:], eq1[:], -1e30, lsb[:], op0=ALU.mult, op1=ALU.add),
                reads=["m6_eq1", "m6_lsb"], writes=["m6_l2"])
            p.run("dve", lambda v: v.reduce_max(cols[:, 1:2], l2sb[:], axis=AX.X),
                  reads=["m6_l2"], writes=["m6_cols"])
            p.run("dve", lambda v: v.tensor_scalar(eq2[:], l2sb[:], cols[:, 1:2],
                                                   None, op0=ALU.is_equal),
                  reads=["m6_l2", "m6_cols"], writes=["m6_eq2"])
            p.run("dve", lambda v: v.tensor_tensor(cols[:, 2:3], cols[:, 1:2],
                                                   cols[:, 0:1], op=ALU.subtract),
                  reads=["m6_cols"], writes=["m6_cols"])
            p.run("act", lambda s: s.activation(cols[:, 3:4], cols[:, 2:3], AF.Exp),
                  reads=["m6_cols"], writes=["m6_cols"])
            p.run("dve", lambda v: v.tensor_scalar(cols[:, 4:5], cols[:, 3:4], 1.0,
                                                   None, op0=ALU.add),
                  reads=["m6_cols"], writes=["m6_cols"])
            p.run("dve", lambda v: v.reciprocal(cols[:, 5:6], cols[:, 4:5]),
                  reads=["m6_cols"], writes=["m6_cols"])
            p.run("dve", lambda v: v.tensor_tensor(cols[:, 6:7], cols[:, 3:4],
                                                   cols[:, 5:6], op=ALU.mult),
                  reads=["m6_cols"], writes=["m6_cols"])
            p.run("dve", lambda v: v.tensor_scalar(gts[:], eq1[:], cols[:, 5:6],
                                                   None, op0=ALU.mult),
                  reads=["m6_eq1", "m6_cols"], writes=["m6_gts"])
            p.run("dve", lambda v: v.scalar_tensor_tensor(
                gts[:], eq2[:], cols[:, 6:7], gts[:], op0=ALU.mult, op1=ALU.add),
                reads=["m6_eq2", "m6_cols", "m6_gts"], writes=["m6_gts"])
            p.run("dve", lambda v: v.tensor_copy(gtb[:], gts[:]),
                  reads=["m6_gts"], writes=["m6_gtb"])
            p.run("te", lambda t: t.transpose(io.pbt[0:16, 0:128], gtb[:], io.ident()),
                  reads=["m6_gtb", "cbf"], writes=["pbt"])
            p.run("dve", lambda v, tt=tt, tok0=tok0: v.tensor_copy(
                gatesT[:, tok0 + tt * 128: tok0 + (tt + 1) * 128], io.pbt[0:16, 0:128]),
                reads=["pbt"], writes=[f"m6_gT{tt}"])
        for e in range(16):
            if e == 0:
                for tt in range(4):
                    _gate_tt(tt)
            p.run("te", lambda t, e=e, tok0=tok0: t.matmul(
                io.pb[6][:, 0:512],
                io.cbf[0:16, CBF_OH16 + e * 128: CBF_OH16 + (e + 1) * 128],
                gatesT[0:16, tok0: tok0 + 512], start=True, stop=True),
                reads=[f"m6_gT{tt}" for tt in range(4)] + ["cbf"], writes=["pb6"])
            p.run("dve", lambda v: v.tensor_copy(gb[:, 0:512], io.pb[6][:, 0:512]),
                  reads=["pb6"], writes=["m6_gb"])
            if e == 0:
                for tt in range(4, 8):
                    _gate_tt(tt)
            p.run("te", lambda t, e=e, tok0=tok0: t.matmul(
                io.pb[6][:, 0:512],
                io.cbf[0:16, CBF_OH16 + e * 128: CBF_OH16 + (e + 1) * 128],
                gatesT[0:16, tok0 + 512: tok0 + 1024], start=True, stop=True),
                reads=[f"m6_gT{tt}" for tt in range(4, 8)] + ["cbf"], writes=["pb6"])
            p.run("dve", lambda v: v.tensor_copy(gb[:, 512:1024], io.pb[6][:, 0:512]),
                  reads=["pb6"], writes=["m6_gb"])
            for mh in range(2):
                w1b, w11b = (0, 1) if mh == 0 else (2, 3)
                wload(w1b, "W1", mh * 1024, 1024, 1024, e=e)
                wload(w11b, "W11", mh * 1024, 1024, 1024, e=e)
                if mh == 1:
                    wload(0, "W2", 0 * 512, 512, 2048, e=e)
                    wload(1, "W2", 1 * 512, 512, 2048, e=e)
                for m in range(8):
                    j = mh * 8 + m
                    for nb in range(2):
                        psG, kG = io.pb[nb], f"pb{nb}"
                        psU, kU = io.pb[2 + nb], f"pb{2+nb}"
                        xnks = [f"m6_xn{k}" for k in range(8)]
                        mmacc(p, psG[:, 0:512],
                              [(_wslices(wbank[w1b], k, m * 128, 128),
                                xnt[k][:, nb * 512:(nb + 1) * 512]) for k in range(8)],
                              reads=[f"m6_w{w1b}"] + xnks, wkey=kG)
                        mmacc(p, psU[:, 0:512],
                              [(_wslices(wbank[w11b], k, m * 128, 128),
                                xnt[k][:, nb * 512:(nb + 1) * 512]) for k in range(8)],
                              reads=[f"m6_w{w11b}"] + xnks, wkey=kU)
                        p.run("act", lambda s, psG=psG, e=e, j=j, nb=nb: s.activation(
                            sgt[nb][:], psG[:, 0:512], AF.Silu,
                            bias=io.acol("b1", e * 16 + j)),
                            reads=[kG, "arena"], writes=[f"m6_sg{nb}"])
                        p.run("dve", lambda v, psU=psU, e=e, j=j, nb=nb: v.scalar_tensor_tensor(
                            htmp[nb][:], psU[:, 0:512], io.acol("b11", e * 16 + j),
                            sgt[nb][:], op0=ALU.add, op1=ALU.mult),
                            reads=[kU, "arena", f"m6_sg{nb}"], writes=[f"m6_ht{nb}"])
                        p.run("pool", lambda g, j=j, nb=nb: g.tensor_tensor(
                            hgt[j][:, nb * 512:(nb + 1) * 512], htmp[nb][:],
                            gb[:, nb * 512:(nb + 1) * 512], op=ALU.mult),
                            reads=[f"m6_ht{nb}", "m6_gb"], writes=[f"m6_hg{j}"])
            for oh in range(2):
                w2b = oh
                for nb in range(2):
                    for out in range(4):
                        o = oh * 4 + out
                        psY, kY = io.pb[4 + out % 2], f"pb{4+out%2}"
                        mmacc(p, psY[:, 0:512],
                              [(wbank[w2b][:, k * 512 + out * 128: k * 512 + out * 128 + 128],
                                hgt[k][:, nb * 512:(nb + 1) * 512]) for k in range(16)],
                              reads=[f"m6_w{w2b}"] + [f"m6_hg{k}" for k in range(16)],
                              wkey=kY)
                        p.run("dve", lambda v, psY=psY, o=o, nb=nb: v.tensor_tensor(
                            acc[o][:, nb * 512:(nb + 1) * 512], psY[:, 0:512],
                            acc[o][:, nb * 512:(nb + 1) * 512], op=ALU.add),
                            reads=[kY, f"m6_acc{o}"], writes=[f"m6_acc{o}"])
        for out in range(8):
            for nb in range(2):
                psB, kB = io.pb[6], "pb6"
                p.run("te", lambda t, out=out, nb=nb, tok0=tok0, psB=psB: t.matmul(
                    psB[:, 0:512], b2sb[:, out * 128:(out + 1) * 128],
                    gatesT[0:16, tok0 + nb * 512: tok0 + (nb + 1) * 512],
                    start=True, stop=True),
                    reads=["m6_b2", "m6_gT"], writes=[kB])
                p.run("dve", lambda v, psB=psB, out=out, nb=nb: v.tensor_tensor(
                    acc[out][:, nb * 512:(nb + 1) * 512], psB[:, 0:512],
                    acc[out][:, nb * 512:(nb + 1) * 512], op=ALU.add),
                    reads=[kB, f"m6_acc{out}"], writes=[f"m6_acc{out}"])
        for qq in range(4):
            qb1, qb11 = (0, 1) if qq % 2 == 0 else (2, 3)
            wload(qb1, "Ws1", qq * 1024, 1024, 1024)
            wload(qb11, "Ws11", qq * 1024, 1024, 1024)
            for m in range(8):
                for nb in range(2):
                    psG, kG = io.pb[nb], f"pb{nb}"
                    psU, kU = io.pb[2 + nb], f"pb{2+nb}"
                    xnks = [f"m6_xn{k}" for k in range(8)]
                    mmacc(p, psG[:, 0:512],
                          [(_wslices(wbank[qb1], k, m * 128, 128),
                            xnt[k][:, nb * 512:(nb + 1) * 512]) for k in range(8)],
                          reads=[f"m6_w{qb1}"] + xnks, wkey=kG)
                    mmacc(p, psU[:, 0:512],
                          [(_wslices(wbank[qb11], k, m * 128, 128),
                            xnt[k][:, nb * 512:(nb + 1) * 512]) for k in range(8)],
                          reads=[f"m6_w{qb11}"] + xnks, wkey=kU)
                    p.run("act", lambda s, psG=psG, qq=qq, m=m, nb=nb: s.activation(
                        sgt[nb][:], psG[:, 0:512], AF.Silu,
                        bias=io.acol("bs1", qq * 8 + m)),
                        reads=[kG, "arena"], writes=[f"m6_sg{nb}"])
                    p.run("dve", lambda v, psU=psU, qq=qq, m=m, nb=nb: v.scalar_tensor_tensor(
                        hgt[m][:, nb * 512:(nb + 1) * 512], psU[:, 0:512],
                        io.acol("bs11", qq * 8 + m), sgt[nb][:],
                        op0=ALU.add, op1=ALU.mult),
                        reads=[kU, "arena", f"m6_sg{nb}"], writes=[f"m6_hg{m}"])
            srcq = io.d["Ws2"][qq * 1024:(qq + 1) * 1024, :].rearrange(
                "(c p) m -> p c m", p=128)
            p.dma("pool", wbank[4][:].rearrange("p (c m) -> p c m", m=1024), srcq,
                  reads=["Ws2"], writes=["m6_w4"])
            for out in range(8):
                for nb in range(2):
                    psY, kY = io.pb[4 + out % 2], f"pb{4+out%2}"
                    mmacc(p, psY[:, 0:512],
                          [(_wslices(wbank[4], k, out * 128, 128),
                            hgt[k][:, nb * 512:(nb + 1) * 512]) for k in range(8)],
                          reads=["m6_w4"] + [f"m6_hg{k}" for k in range(8)], wkey=kY)
                    if qq == 3:
                        p.run("dve", lambda v, psY=psY, out=out, nb=nb: v.scalar_tensor_tensor(
                            acc[out][:, nb * 512:(nb + 1) * 512], psY[:, 0:512],
                            io.acol("bs2", out), acc[out][:, nb * 512:(nb + 1) * 512],
                            op0=ALU.add, op1=ALU.add),
                            reads=[kY, "arena", f"m6_acc{out}"], writes=[f"m6_acc{out}"])
                    else:
                        p.run("dve", lambda v, psY=psY, out=out, nb=nb: v.tensor_tensor(
                            acc[out][:, nb * 512:(nb + 1) * 512], psY[:, 0:512],
                            acc[out][:, nb * 512:(nb + 1) * 512], op=ALU.add),
                            reads=[kY, f"m6_acc{out}"], writes=[f"m6_acc{out}"])
        for c in range(8):
            p.dma("act", io.d["out"][row0 + c * 128: row0 + (c + 1) * 128,
                                     colp:colp + 1024], acc[c][:],
                  reads=[f"m6_acc{c}"], writes=["out"])
    p.barrier()



def _run_phases(in_maps, build_fns, ext_io, trace=False):
    nc = bass.Bass("TRN2", target_bir_lowering=False)
    io = IO(nc, *ext_io)
    p = Prog(nc)
    io.load_consts(p)
    for fn in build_fns:
        fn(p, io)
    p.finish()
    p.flush()
    res = run_bass_kernel_spmd(nc, in_maps, core_ids=list(range(8)), trace=trace)
    return res


def build_full(nc):
    io = IO(nc, set(), set())
    p = Prog(nc)
    io.load_consts(p)
    emit_ln(p, io, "x0", 0)
    emit_attn0(p, io, 0)
    emit_attn0(p, io, 1)
    emit_moe0(p, io)
    emit_ln(p, io, "xr", 1)
    emit_mixer1(p, io, 0)
    emit_mixer1(p, io, 1)
    emit_mlp1(p, io, 0)
    emit_mlp1(p, io, 1)
    emit_moe1(p, io)
    p.finish()
    p.flush()
    return p


def kernel(**inputs):
    in_maps = host_prepare(inputs)
    nc = bass.Bass("TRN2", target_bir_lowering=False)
    build_full(nc)
    res = run_bass_kernel_spmd(nc, in_maps, core_ids=list(range(8)))
    return host_gather(res.results)


# version 08
# speedup vs baseline: 1.0823x; 1.0823x over previous
import sys
import numpy as np

sys.path.insert(0, '/opt/trn_rl_repo')
import concourse.bass as bass
import concourse.mybir as mybir
from concourse.bass_utils import run_bass_kernel_spmd

F32 = mybir.dt.float32
BF16 = mybir.dt.bfloat16
AF = mybir.ActivationFunctionType
ALU = mybir.AluOpType
AX = mybir.AxisListType


ENGINES = ("te", "dve", "act", "pool", "sp")
DMA_QUEUES = ("sp", "pool", "act")


class Prog:

    def __init__(self, nc):
        self.nc = nc
        self.eng = {"te": nc.tensor, "dve": nc.vector, "act": nc.scalar,
                    "pool": nc.gpsimd, "sp": nc.sync}
        self.sem = {e: nc.alloc_semaphore(f"S_{e}") for e in ENGINES}
        self.NQS = 4
        self.qsem = {q: [nc.alloc_semaphore(f"Q_{q}{i}") for i in range(self.NQS)]
                     for q in DMA_QUEUES}
        self.cnt = {e: 0 for e in ENGINES}
        self.qcnt = {q: 0 for q in DMA_QUEUES}
        self.known = {e: {} for e in ENGINES}
        self.lastw = {}
        self.readers = {}
        self.queue = {e: [] for e in ENGINES}
        self.nwaits = 0
        self.ninstr = 0

    def _sem_of(self, token):
        if token.startswith("q_"):
            q, i = token[2:].split(".")
            return self.qsem[q][int(i)]
        return self.sem[token]

    def _need_from(self, reads, writes):
        need = {}

        def add(d):
            for name, val in d.items():
                if need.get(name, 0) < val:
                    need[name] = val
        for k in reads:
            add(self.lastw.get(k, {}))
        for k in writes:
            add(self.lastw.get(k, {}))
            add(self.readers.get(k, {}))
        return need

    def _emit_waits(self, e, need):
        for name, val in sorted(need.items()):
            if self.known[e].get(name, 0) >= val:
                continue
            sem = self._sem_of(name)
            self.queue[e].append(lambda eng, s=sem, v=val: eng.wait_ge(s, v))
            self.known[e][name] = val
            self.nwaits += 1

    def _record(self, token, reads, writes):
        name, val = token
        for k in writes:
            self.lastw[k] = {name: val}
            self.readers[k] = {}
        for k in reads:
            d = self.readers.setdefault(k, {})
            if d.get(name, 0) < val:
                d[name] = val

    def run(self, e, fn, reads=(), writes=()):
        need = self._need_from(reads, writes)
        self._emit_waits(e, need)
        self.cnt[e] += 1
        sem, val = self.sem[e], self.cnt[e]
        self.queue[e].append(lambda eng: fn(eng).then_inc(sem, 1))
        self._record((e, val), reads, writes)
        self.ninstr += 1

    def group(self, e, fns, reads=(), writes=()):
        need = self._need_from(reads, writes)
        self._emit_waits(e, need)
        self.cnt[e] += 1
        sem, val = self.sem[e], self.cnt[e]
        for f in fns[:-1]:
            self.queue[e].append(lambda eng, f=f: f(eng))
        self.queue[e].append(lambda eng, f=fns[-1]: f(eng).then_inc(sem, 1))
        self._record((e, val), reads, writes)
        self.ninstr += len(fns)

    def dma(self, q, out, in_, reads=(), writes=(), **kw):
        need = self._need_from(reads, writes)
        idx = self.qcnt[q]
        lane = idx % self.NQS
        name = f"q_{q}.{lane}"
        if idx >= self.NQS:
            prev = 16 * (idx // self.NQS)
            if need.get(name, 0) < prev:
                need[name] = prev
        self._emit_waits(q, need)
        self.qcnt[q] = idx + 1
        sem, val = self.qsem[q][lane], 16 * (idx // self.NQS + 1)
        self.queue[q].append(
            lambda eng, o=out, i=in_: eng.dma_start(o, i, **kw).then_inc(sem, 16))
        self._record((name, val), reads, writes)
        self.ninstr += 1

    def _qfinal(self):
        need = {}
        for q in DMA_QUEUES:
            for lane in range(self.NQS):
                n = self.qcnt[q] // self.NQS + (1 if lane < self.qcnt[q] % self.NQS else 0)
                if n:
                    need[f"q_{q}.{lane}"] = 16 * n
        return need

    def finish(self):
        need = {e: self.cnt[e] for e in ENGINES if self.cnt[e]}
        need.update(self._qfinal())
        self._emit_waits("sp", need)

    def barrier(self):
        need = {e: self.cnt[e] for e in ENGINES if self.cnt[e]}
        need.update(self._qfinal())
        for e in ENGINES:
            my = dict(need)
            my.pop(e, None)
            self._emit_waits(e, my)
        self.lastw = {}
        self.readers = {}
        for e in ENGINES:
            self.known[e] = dict(need)

    def flush(self):
        with self.nc.Block() as blk:
            for e in ENGINES:
                if not self.queue[e]:
                    continue
                deco = {"te": blk.tensor, "dve": blk.vector, "act": blk.scalar,
                        "pool": blk.gpsimd, "sp": blk.sync}[e]

                def body(eng, thunks=list(self.queue[e])):
                    for th in thunks:
                        th(eng)
                deco(body)
                self.queue[e] = []



D = 1024
NC = 8
S = 2048
NSEQ = 2
T = S * NSEQ

ARENA = {}


def _arena_cols():
    spec = [("bpre", 8), ("bproj", 8), ("pi", 1), ("bs1", 32), ("bs11", 32),
            ("bs2", 8), ("ms", 8), ("mls", 8), ("td", 8), ("tf", 8),
            ("b1", 256), ("b11", 256), ("zero", 1)]
    off = 0
    for name, n in spec:
        ARENA[name] = (off, n)
        off += n
    return off


ARENA_COLS = _arena_cols()

CBF_ID = 0
CBF_ONES = 128
CBF_OH32 = 256
CBF_OH16 = CBF_OH32 + 32 * 128
CBF_COLS = CBF_OH16 + 16 * 128
CF_ONES = 0
CF_CONSTS = {1e-5: 128, 1e-6: 129, 1.0: 130, 0.5: 131, 0.0: 132, -1.0: 133,
             1e30: 134, 0.125: 135, 1.0 / 32: 136, 1.0 / 1024: 137}
CF_IDENT = 144
CF_COLS = 272


def host_prepare(inputs):
    import ml_dtypes
    f32 = np.float32
    g = {k: np.asarray(v) for k, v in inputs.items()}

    shared = {}
    C = np.ascontiguousarray
    shared["Wpre"] = C(g["L0_attn_Wpre"].astype(f32))
    shared["Wq"] = C(g["L0_attn_Wq"].astype(f32))
    shared["Wk"] = C(g["L0_attn_Wk"].astype(f32))
    shared["Wv"] = C(g["L0_attn_Wv"].astype(f32))
    shared["Wproj"] = C(g["L0_attn_Wproj"].astype(f32))
    shared["gwT"] = C(g["L0_ffn_gate_w"].T.astype(f32))
    shared["guwT"] = C(g["L0_ffn_gate_up_w"].transpose(0, 2, 1).astype(f32))
    shared["dwT"] = C(g["L0_ffn_down_w"].transpose(0, 2, 1).astype(f32))
    shared["WkT1"] = C(g["L1_attn_Wk"].T.astype(f32))
    shared["WvT1"] = C(g["L1_attn_Wv"].T.astype(f32))
    shared["WrT1"] = C(g["L1_attn_Wr"].T.astype(f32))
    shared["WoT1"] = C(g["L1_attn_Wo"].T.astype(f32))
    shared["WfcT"] = C(g["L1_attn_Wfc"].T.astype(f32))
    shared["WmlpT"] = C(g["L1_attn_Wmlp"].T.astype(f32))
    shared["Wg"] = C(g["L1_ffn_Wg"].astype(f32))
    shared["W1"] = C(g["L1_ffn_W1"].astype(f32))
    shared["W11"] = C(g["L1_ffn_W11"].astype(f32))
    shared["W2"] = C(g["L1_ffn_W2"].astype(f32))
    shared["b2e"] = C(g["L1_ffn_b2"].astype(f32))
    shared["Ws1"] = C(g["L1_ffn_Ws1"].astype(f32))
    shared["Ws11"] = C(g["L1_ffn_Ws11"].astype(f32))
    shared["Ws2"] = C(g["L1_ffn_Ws2"].astype(f32))

    arena = np.zeros((128, ARENA_COLS), f32)

    def put(name, vec):
        off, n = ARENA[name]
        vec = np.asarray(vec, f32).reshape(-1)
        assert vec.size == n * 128, (name, vec.size, n)
        arena[:, off:off + n] = vec.reshape(n, 128).T
    put("bpre", g["L0_attn_bpre"])
    put("bproj", g["L0_attn_bproj"])
    arena[:, ARENA["pi"][0]] = np.repeat(np.asarray(g["L0_attn_pi"], f32), 128)
    put("bs1", g["L1_ffn_bs1"])
    put("bs11", g["L1_ffn_bs11"])
    put("bs2", g["L1_ffn_bs2"])
    put("ms", g["L1_attn_mixer_scale"])
    put("mls", g["L1_attn_mlp_scale"])
    put("td", g["L1_attn_time_decay"])
    put("tf", g["L1_attn_time_first"])
    put("b1", g["L1_ffn_b1"])
    put("b11", g["L1_ffn_b11"])
    shared["arena"] = arena

    bf16 = ml_dtypes.bfloat16
    cbf = np.zeros((128, CBF_COLS), bf16)
    cbf[:, CBF_ID:CBF_ID + 128] = np.eye(128, dtype=bf16)
    cbf[:, CBF_ONES:CBF_ONES + 128] = 1
    for e in range(32):
        cbf[e, CBF_OH32 + e * 128: CBF_OH32 + (e + 1) * 128] = 1
    for e in range(16):
        cbf[e, CBF_OH16 + e * 128: CBF_OH16 + (e + 1) * 128] = 1
    shared["cbf"] = cbf
    cf = np.ones((128, CF_COLS), f32)
    for val, col in CF_CONSTS.items():
        cf[:, col] = val
    cf[:, CF_IDENT:CF_IDENT + 128] = np.eye(128, dtype=f32)
    shared["cf32"] = cf

    xt = np.ascontiguousarray(g["L0_attn_x"].astype(f32).transpose(0, 2, 1))
    in_maps = []
    for c in range(8):
        m = dict(shared)
        m["x0"] = np.ascontiguousarray(
            xt[2 * c:2 * c + 2].reshape(2048, 2048))
        in_maps.append(m)
    return in_maps


def host_gather(results):
    out = np.empty((16, 2048, 1024), np.float32)
    for c, r in enumerate(results):
        o = np.asarray(r["out"], np.float32).reshape(2, 1024, 2048)
        out[2 * c:2 * c + 2] = o.transpose(0, 2, 1)
    return out



WEIGHT_SHAPES = {
    "Wpre": [D, D], "Wq": [D, D], "Wk": [D, D], "Wv": [D, D], "Wproj": [D, D],
    "gwT": [D, 32], "guwT": [32, D, D], "dwT": [32, 512, D],
    "WkT1": [D, D], "WvT1": [D, D], "WrT1": [D, D], "WoT1": [D, D],
    "WfcT": [D, 4 * D], "WmlpT": [4 * D, D],
    "Wg": [D, 16], "W1": [16, D, 2 * D], "W11": [16, D, 2 * D],
    "W2": [16, 2 * D, D], "b2e": [16, D],
    "Ws1": [D, 4 * D], "Ws11": [D, 4 * D], "Ws2": [4 * D, D],
}
SCRATCH = {"xnh": BF16, "xr": F32, "xih": F32, "xr2": F32}


class IO:

    def __init__(self, nc, phase_inputs, phase_outputs):
        self.nc = nc
        self.d = {}
        self.d["x0"] = nc.dram_tensor("x0", [2048, 2048], F32, kind="ExternalInput").ap()
        self.d["arena"] = nc.dram_tensor("arena", [128, ARENA_COLS], F32, kind="ExternalInput").ap()
        self.d["cbf"] = nc.dram_tensor("cbf", [128, CBF_COLS], BF16, kind="ExternalInput").ap()
        self.d["cf32"] = nc.dram_tensor("cf32", [128, CF_COLS], F32, kind="ExternalInput").ap()
        for w, shp in WEIGHT_SHAPES.items():
            self.d[w] = nc.dram_tensor(w, shp, F32, kind="ExternalInput").ap()
        self.d["out"] = nc.dram_tensor("out", [2048, 2048], F32, kind="ExternalOutput").ap()
        for sname, dt in SCRATCH.items():
            if sname in phase_outputs:
                kind = "ExternalOutput"
            elif sname in phase_inputs:
                kind = "ExternalInput"
            else:
                kind = "Internal"
            self.d[sname] = nc.dram_tensor(sname, [2048, 2048], dt, kind=kind).ap()
        kind = ("ExternalOutput" if "stats" in phase_outputs else
                "ExternalInput" if "stats" in phase_inputs else "Internal")
        self.d["stats"] = nc.dram_tensor("stats", [8, 2048], F32, kind=kind).ap()

        self.misc = nc.alloc_sbuf_tensor("misc_sb", [128, 32], F32)
        self.arena = nc.alloc_sbuf_tensor("arena_sb", [128, ARENA_COLS], F32)
        self.cbf = nc.alloc_sbuf_tensor("cbf_sb", [128, CBF_COLS], BF16)
        self.cf32 = nc.alloc_sbuf_tensor("cf32_sb", [128, CF_COLS], F32)
        self.pb = [nc.alloc_psum_tensor(f"pb{i}", [128, 512], F32) for i in range(7)]
        self.pbt = nc.alloc_psum_tensor("pbt", [128, 1024], BF16)
        for val, col in CF_CONSTS.items():
            nc.const_aps.register(F32, val, self.cf32[:, col:col + 1])
        self.ov_base = 33800
        self.ov_top = 229344
        self.ov_cur = self.ov_base
        self.ov_peak = self.ov_base
        assert nc.sbuf_base <= self.ov_base, nc.sbuf_base

    def ov_reset(self):
        self.ov_cur = self.ov_base

    def salloc(self, name, shape, dtype):
        esz = 2 if dtype == BF16 else 4
        nbytes = int(np.prod(shape[1:])) * esz
        addr = (self.ov_cur + 63) & ~63
        self.ov_cur = addr + nbytes
        self.ov_peak = max(self.ov_peak, self.ov_cur)
        assert self.ov_cur <= self.ov_top, (name, self.ov_cur)
        return self.nc.alloc_sbuf_tensor_at(name, shape, dtype, offset=addr)

    def load_consts(self, p):
        p.dma("sp", self.arena[:], self.d["arena"], writes=["arena"])
        p.dma("sp", self.cbf[:], self.d["cbf"], writes=["cbf"])
        p.dma("sp", self.cf32[:], self.d["cf32"], writes=["cf32"])
        p.run("dve", lambda v: v.tensor_scalar(
            self.misc[:, 0:1], self.acol("pi"), 1.0 / 32, None, op0=ALU.mult),
            reads=["arena"], writes=["misc"])

    def acol(self, name, j=0):
        off, n = ARENA[name]
        assert j < n
        return self.arena[:, off + j:off + j + 1]

    def ones_bf(self, p_lo, p_cnt, cols):
        return self.cbf[p_lo:p_lo + p_cnt, CBF_ONES:CBF_ONES + cols]

    def ones_f32(self, p_lo, p_cnt, cols):
        return self.cf32[p_lo:p_lo + p_cnt, CF_ONES:CF_ONES + cols]

    def ident(self):
        return self.cbf[:, CBF_ID:CBF_ID + 128]

    def ident_f32(self):
        return self.cf32[:, CF_IDENT:CF_IDENT + 128]

    def ov_mark(self):
        return self.ov_cur

    def ov_release(self, mark):
        self.ov_cur = mark



def emit_ln(p, io, src_name, layer):
    nc = io.nc
    src = io.d[src_name]
    io.ov_reset()
    xt = [io.salloc(f"ln{layer}_x{i}", [128, 512], F32) for i in range(16)]
    sq = [io.salloc(f"ln{layer}_sq{i}", [128, 512], F32) for i in range(8)]
    rows = io.salloc(f"ln{layer}_rows", [1, 512 * 6], F32)
    bc = [io.salloc(f"ln{layer}_bc{i}", [128, 512], F32) for i in range(4)]
    xnt = [io.salloc(f"ln{layer}_xn{i}", [128, 512], BF16) for i in range(2)]

    for tb in range(8):
        seq, col = tb // 4, (tb % 4) * 512
        row0 = seq * 1024
        buf = tb % 2
        xts = xt[buf * 8:(buf + 1) * 8]
        psA, psB = io.pb[buf * 2], io.pb[buf * 2 + 1]
        kA, kB = f"pb{buf*2}", f"pb{buf*2+1}"
        for c in range(8):
            p.dma("sp", xts[c][:], src[row0 + c * 128: row0 + (c + 1) * 128,
                                       col:col + 512],
                  reads=[src_name], writes=[f"lnx{buf}{c}"])
        mmacc(p, psA[0:1, :], [(io.ones_f32(0, 128, 1), xts[c][:]) for c in range(8)],
              reads=[f"lnx{buf}{c}" for c in range(8)] + ["cf32"], wkey=kA)
        for c in range(8):
            p.run("act", lambda s, c=c, xts=xts: s.activation(
                sq[c][:], xts[c][:], AF.Square),
                reads=[f"lnx{buf}{c}"], writes=[f"lnsq{c}"])
        mmacc(p, psB[0:1, :], [(io.ones_f32(0, 128, 1), sq[c][:]) for c in range(8)],
              reads=[f"lnsq{c}" for c in range(8)] + ["cf32"], wkey=kB)
        m_r = rows[:, 0:512]
        msq_r = rows[:, 512:1024]
        var_r = rows[:, 1024:1536]
        rstd_r = rows[:, 1536:2048]
        mr_r = rows[:, 2048:2560]
        p.run("dve", lambda v, psA=psA: v.tensor_scalar(m_r, psA[0:1, :], 1.0 / 1024,
                                                        None, op0=ALU.mult),
              reads=[kA], writes=["lnrows_m"])
        p.run("dve", lambda v, psB=psB: v.tensor_scalar(msq_r, psB[0:1, :], 1.0 / 1024,
                                                        None, op0=ALU.mult),
              reads=[kB], writes=["lnrows_msq"])
        p.run("dve", lambda v: v.scalar_tensor_tensor(
            var_r, m_r, -1.0, m_r, op0=ALU.mult, op1=ALU.mult),
            reads=["lnrows_m"], writes=["lnrows_var"])
        p.run("dve", lambda v: v.tensor_tensor(var_r, msq_r, var_r, op=ALU.add),
              reads=["lnrows_msq", "lnrows_var"], writes=["lnrows_var"])
        p.run("act", lambda s: s.activation(rstd_r, var_r, AF.Sqrt, bias=1e-5),
              reads=["lnrows_var"], writes=["lnrows_rstd"])
        p.run("dve", lambda v: v.reciprocal(rstd_r, rstd_r),
              reads=["lnrows_rstd"], writes=["lnrows_rstd"])
        p.run("dve", lambda v: v.tensor_tensor(mr_r, m_r, rstd_r, op=ALU.mult),
              reads=["lnrows_m", "lnrows_rstd"], writes=["lnrows_mr"])
        p.dma("act", io.d["stats"][layer * 4 + seq * 2: layer * 4 + seq * 2 + 1,
                                   col:col + 512], m_r,
              reads=["lnrows_m"], writes=["stats"])
        p.dma("act", io.d["stats"][layer * 4 + seq * 2 + 1: layer * 4 + seq * 2 + 2,
                                   col:col + 512], rstd_r,
              reads=["lnrows_rstd"], writes=["stats"])
        psC, psD = io.pb[4], io.pb[5]
        p.run("te", lambda t: t.matmul(psC[:, :], io.ones_f32(0, 1, 128), rstd_r,
                                       start=True, stop=True),
              reads=["lnrows_rstd", "cf32"], writes=["pb4"])
        p.run("te", lambda t: t.matmul(psD[:, :], io.ones_f32(0, 1, 128), mr_r,
                                       start=True, stop=True),
              reads=["lnrows_mr", "cf32"], writes=["pb5"])
        p.run("dve", lambda v: v.tensor_copy(bc[0][:], psC[:, :]),
              reads=["pb4"], writes=["lnbc0"])
        p.run("dve", lambda v: v.tensor_copy(bc[1][:], psD[:, :]),
              reads=["pb5"], writes=["lnbc1"])
        for c in range(8):
            xb = xnt[c % 2]
            kx = f"lnxnt{c%2}"
            p.run("dve", lambda v, c=c, xts=xts: v.tensor_tensor(
                bc[2][:], xts[c][:], bc[0][:], op=ALU.mult),
                reads=[f"lnx{buf}{c}", "lnbc0"], writes=["lnbc2"])
            p.run("dve", lambda v, xb=xb: v.tensor_tensor(
                xb[:], bc[2][:], bc[1][:], op=ALU.subtract),
                reads=["lnbc2", "lnbc1"], writes=[kx])
            p.dma("act", io.d["xnh"][row0 + c * 128: row0 + (c + 1) * 128,
                                     col:col + 512], xb[:],
                  reads=[kx], writes=["xnh"])
    p.barrier()



def _wslices(wbuf, k, m_lo, m_cnt):
    return wbuf[:, k * 1024 + m_lo: k * 1024 + m_lo + m_cnt]


def mmacc(p, out_ap, pairs, reads, wkey):
    n = len(pairs)
    fns = [lambda t, lh=lh, rh=rh, i=i: t.matmul(out_ap, lh, rh,
                                                 start=(i == 0), stop=(i == n - 1))
           for i, (lh, rh) in enumerate(pairs)]
    p.group("te", fns, reads=reads, writes=[wkey])


def emit_attn0(p, io, seq):
    nc = io.nc
    sq = f"a{seq}"
    row0 = seq * 1024
    io.ov_reset()
    pre = [io.salloc(f"{sq}_pre{m}", [128, S], BF16) for m in range(8)]
    kk = [io.salloc(f"{sq}_k{m}", [128, S], BF16) for m in range(8)]
    qq = [io.salloc(f"{sq}_q{m}", [128, S], BF16) for m in range(8)]
    vx = [io.salloc(f"{sq}_vx{t}", [128, 1040], BF16) for t in range(16)]
    negB = io.salloc(f"{sq}_negB", [128, 16], F32)

    mark = io.ov_mark()
    wA = io.salloc(f"{sq}_wA", [128, 8192], BF16)
    wB = io.salloc(f"{sq}_wB", [128, 8192], BF16)
    xnsl = [io.salloc(f"{sq}_xnsl{i}", [128, 8 * 512], BF16) for i in range(2)]
    sqt = io.salloc(f"{sq}_sqt", [128, S], BF16)
    maxcol = io.salloc(f"{sq}_maxcol", [128, 1], F32)
    rows = io.salloc(f"{sq}_rows", [1, 512], F32)
    rowbf = io.salloc(f"{sq}_rowbf", [1, 16], BF16)

    def ldw(buf, key, wname):
        src = io.d[wname].rearrange("(c p) m -> p c m", p=128)
        dst = buf[:].rearrange("p (c m) -> p c m", m=1024)
        p.dma("pool", dst, src, reads=[wname], writes=[key])

    def load_xnsl(nq):
        b = nq % 2
        src = io.d["xnh"][row0:row0 + 1024,
                          nq * 512:(nq + 1) * 512].rearrange("(c p) n -> p c n", p=128)
        dst = xnsl[b][:].rearrange("p (c n) -> p c n", n=512)
        p.dma("sp", dst, src, reads=["xnh"], writes=[f"{sq}_xnsl{b}"])
        return xnsl[b], f"{sq}_xnsl{b}"

    ldw(wA, f"{sq}_wA", "Wpre")
    ldw(wB, f"{sq}_wB", "Wk")
    for nq in range(4):
        xs, xk = load_xnsl(nq)
        for m in range(8):
            ps, kps = io.pb[m % 2], f"pb{m%2}"
            mmacc(p, ps[:, 0:512],
                  [(_wslices(wA, k, m * 128, 128), xs[:, k * 512:(k + 1) * 512])
                   for k in range(8)],
                  reads=[f"{sq}_wA", xk], wkey=kps)
            p.run("act", lambda s, ps=ps, m=m, nq=nq: s.activation(
                pre[m][:, nq * 512:(nq + 1) * 512], ps[:, 0:512], AF.Silu,
                bias=io.acol("bpre", m)),
                reads=[kps, "arena"], writes=[f"{sq}_pre{m}"])
        for m in range(8):
            ps, kps = io.pb[2 + m % 2], f"pb{2+m%2}"
            mmacc(p, ps[:, 0:512],
                  [(_wslices(wB, k, m * 128, 128), xs[:, k * 512:(k + 1) * 512])
                   for k in range(8)],
                  reads=[f"{sq}_wB", xk], wkey=kps)
            p.run("dve", lambda v, ps=ps, m=m, nq=nq: v.tensor_copy(
                kk[m][:, nq * 512:(nq + 1) * 512], ps[:, 0:512]),
                reads=[kps], writes=[f"{sq}_k{m}"])

    ldw(wA, f"{sq}_wA", "Wv")
    ldw(wB, f"{sq}_wB", "Wq")
    for t in range(16):
        p.run("dve", lambda v, t=t: v.memset(
            vx[t][:].rearrange("p (h x) -> p h x", x=65)[:, :, 64:65], 1.0),
            writes=[f"{sq}_vx{t}"])
    for nq in range(4):
        xs, xk = load_xnsl(nq)
        for tb in range(4):
            mc = nq * 4 + tb
            for half in range(2):
                ps, kps = io.pb[4 + half], f"pb{4+half}"
                mmacc(p, ps[:, 0:512],
                      [(xs[:, k * 512 + tb * 128: k * 512 + tb * 128 + 128],
                        _wslices(wA, k, half * 512, 512)) for k in range(8)],
                      reads=[f"{sq}_wA", xk], wkey=kps)
                p.run("dve", lambda v, ps=ps, mc=mc, half=half: v.tensor_copy(
                    vx[mc][:, half * 520:half * 520 + 520].rearrange(
                        "p (h x) -> p h x", x=65)[:, :, 0:64],
                    ps[:, 0:512].rearrange("p (h x) -> p h x", x=64)),
                    reads=[kps], writes=[f"{sq}_vx{mc}"])
        for m in range(8):
            ps, kps = io.pb[m % 2], f"pb{m%2}"
            mmacc(p, ps[:, 0:512],
                  [(_wslices(wB, k, m * 128, 128), xs[:, k * 512:(k + 1) * 512])
                   for k in range(8)],
                  reads=[f"{sq}_wB", xk], wkey=kps)
            p.run("dve", lambda v, ps=ps, m=m, nq=nq: v.tensor_scalar(
                qq[m][:, nq * 512:(nq + 1) * 512], ps[:, 0:512], 0.125, None,
                op0=ALU.mult),
                reads=[kps], writes=[f"{sq}_q{m}"])

    p.run("dve", lambda v: v.memset(rows[:, 0:64], -1e30), writes=[f"{sq}_rows"])
    for (src_t, kind) in ((qq, "q"), (kk, "k"), (pre, "p")):
        for c in range(8):
            skey = f"{sq}_pre{c}" if kind == "p" else (
                f"{sq}_q{c}" if kind == "q" else f"{sq}_k{c}")
            p.run("dve", lambda v, c=c, src_t=src_t: v.tensor_tensor(
                sqt[:], src_t[c][:], src_t[c][:], op=ALU.mult),
                reads=[skey], writes=[f"{sq}_sqt"])
            if kind == "p":
                for nq in range(4):
                    p.run("te", lambda t, c=c, nq=nq: t.matmul(
                        io.pb[nq][0:1, 0:512], io.cbf[:, CBF_ONES:CBF_ONES + 1],
                        sqt[:, nq * 512:(nq + 1) * 512],
                        start=(c == 0), stop=(c == 7)),
                        reads=[f"{sq}_sqt", "cbf"], writes=[f"pb{nq}"])
            else:
                o = 0 if kind == "q" else 16
                for hh in range(2):
                    h = 2 * c + hh
                    for nq in range(4):
                        pbk = 4 + (nq % 2)
                        p.run("te", lambda t, hh=hh, nq=nq, pbk=pbk: t.matmul(
                            io.pb[pbk][0:1, 0:512],
                            io.cbf[64 * hh:64 * hh + 64, CBF_ONES:CBF_ONES + 1],
                            sqt[64 * hh:64 * hh + 64, nq * 512:(nq + 1) * 512],
                            start=True, stop=True),
                            reads=[f"{sq}_sqt", "cbf"], writes=[f"pb{pbk}"])
                        p.run("dve", lambda v, pbk=pbk: v.tensor_reduce(
                            rows[:, 63:64], io.pb[pbk][0:1, 0:512], axis=AX.X,
                            op=ALU.max),
                            reads=[f"pb{pbk}"], writes=[f"{sq}_rtmp"])
                        p.run("dve", lambda v, o=o, h=h: v.tensor_tensor(
                            rows[:, o + h:o + h + 1], rows[:, o + h:o + h + 1],
                            rows[:, 63:64], op=ALU.max),
                            reads=[f"{sq}_rtmp", f"{sq}_rows"], writes=[f"{sq}_rows"])
    for nq in range(4):
        p.run("dve", lambda v, nq=nq: v.tensor_reduce(
            rows[:, 48 + nq:49 + nq], io.pb[nq][0:1, 0:512], axis=AX.X, op=ALU.max),
            reads=[f"pb{nq}"], writes=[f"{sq}_rows"])
    p.run("dve", lambda v: v.tensor_reduce(rows[:, 56:57], rows[:, 48:52],
                                           axis=AX.X, op=ALU.max),
          reads=[f"{sq}_rows"], writes=[f"{sq}_rows"])
    p.run("dve", lambda v: v.tensor_tensor(rows[:, 32:48], rows[:, 0:16],
                                           rows[:, 16:32], op=ALU.mult),
          reads=[f"{sq}_rows"], writes=[f"{sq}_rows"])
    p.run("act", lambda s: s.activation(rows[:, 32:48], rows[:, 32:48], AF.Sqrt),
          reads=[f"{sq}_rows"], writes=[f"{sq}_rows"])
    p.run("dve", lambda v: v.tensor_tensor(rows[:, 57:58], rows[:, 56:57],
                                           io.misc[0:1, 0:1], op=ALU.mult),
          reads=[f"{sq}_rows", "misc"], writes=[f"{sq}_rows"])
    p.run("dve", lambda v: v.tensor_scalar(rows[:, 64:80], rows[:, 32:48],
                                           rows[:, 57:58], -1.0, op0=ALU.add,
                                           op1=ALU.mult),
          reads=[f"{sq}_rows"], writes=[f"{sq}_rows"])
    p.run("dve", lambda v: v.tensor_copy(rowbf[:], rows[:, 64:80]),
          reads=[f"{sq}_rows"], writes=[f"{sq}_rowbf"])
    p.run("te", lambda t: t.matmul(io.pb[6][:, 0:16], io.cbf[0:1, CBF_ONES:CBF_ONES + 128],
                                   rowbf[:], start=True, stop=True),
          reads=[f"{sq}_rowbf", "cbf"], writes=["pb6"])
    p.run("dve", lambda v: v.tensor_copy(negB[:], io.pb[6][:, 0:16]),
          reads=["pb6"], writes=[f"{sq}_negB"])

    p.barrier()
    io.ov_release(mark)

    wP = io.salloc(f"{sq}_wP", [128, 8192], BF16)
    Lp = [io.salloc(f"{sq}_L{j}", [128, 512], F32) for j in range(8)]
    ub = [io.salloc(f"{sq}_u{i}", [128, 512], F32) for i in range(3)]
    exb = [io.salloc(f"{sq}_ex{i}", [128, 512], BF16) for i in range(3)]
    acat = [io.salloc(f"{sq}_ac{c}", [128, 256], BF16) for c in range(8)]
    rbt = io.salloc(f"{sq}_rb", [128, 256], F32)
    rrow = io.salloc(f"{sq}_rrow", [1, 256], F32)
    oprb = io.salloc(f"{sq}_oprb", [128, 256], F32)
    mrb = io.salloc(f"{sq}_mrb", [128, 256], F32)
    x0t = [io.salloc(f"{sq}_x0t{i}", [128, 256], F32) for i in range(2)]
    ptt = [io.salloc(f"{sq}_pt{i}", [128, 256], F32) for i in range(2)]
    t1b = [io.salloc(f"{sq}_t1{i}", [128, 256], F32) for i in range(1)]
    onb = io.salloc(f"{sq}_onb", [128, 512], F32)
    srows = io.salloc(f"{sq}_srows", [1, 1024], F32)

    ldw(wP, f"{sq}_wP", "Wproj")
    st_base = 0 * 4 + seq * 2
    for nb in range(8):
        col0 = nb * 256
        p.dma("sp", srows[:, 0:256], io.d["stats"][st_base:st_base + 1, col0:col0 + 256],
              reads=["stats"], writes=[f"{sq}_srA"])
        p.dma("sp", srows[:, 256:512], io.d["stats"][st_base + 1:st_base + 2,
                                                     col0:col0 + 256],
              reads=["stats"], writes=[f"{sq}_srB"])
        p.run("dve", lambda v: v.tensor_scalar(srows[:, 512:768], srows[:, 256:512],
                                               1.0, None, op0=ALU.add),
              reads=[f"{sq}_srB"], writes=[f"{sq}_srC"])
        p.run("dve", lambda v: v.tensor_tensor(srows[:, 768:1024], srows[:, 0:256],
                                               srows[:, 256:512], op=ALU.mult),
              reads=[f"{sq}_srA", f"{sq}_srB"], writes=[f"{sq}_srD"])
        p.run("te", lambda t: t.matmul(io.pb[6][:, 0:256], io.ones_f32(0, 1, 128),
                                       srows[:, 512:768], start=True, stop=True),
              reads=[f"{sq}_srC", "cf32"], writes=["pb6"])
        p.run("dve", lambda v: v.tensor_copy(oprb[:], io.pb[6][:, 0:256]),
              reads=["pb6"], writes=[f"{sq}_oprb"])
        p.run("te", lambda t: t.matmul(io.pb[6][:, 256:512], io.ones_f32(0, 1, 128),
                                       srows[:, 768:1024], start=True, stop=True),
              reads=[f"{sq}_srD", "cf32"], writes=["pb6"])
        p.run("dve", lambda v: v.tensor_copy(mrb[:], io.pb[6][:, 256:512]),
              reads=["pb6"], writes=[f"{sq}_mrb"])
        for mc in range(16):
            ps, kps = io.pb[mc % 2], f"pb{mc%2}"
            mmacc(p, ps[:, 0:256],
                  [(pre[c][:, mc * 128:(mc + 1) * 128], pre[c][:, col0:col0 + 256])
                   for c in range(8)],
                  reads=[f"{sq}_pre{c}" for c in range(8)], wkey=kps)
            p.run("dve", lambda v, ps=ps, mc=mc: v.tensor_scalar(
                Lp[mc // 2][:, (mc % 2) * 256:(mc % 2) * 256 + 256], ps[:, 0:256],
                io.misc[:, 0:1], None, op0=ALU.mult),
                reads=[kps, "misc"], writes=[f"{sq}_Lp{mc//2}"])
        def _finish_head(h):
            c, off = h // 2, 64 * (h % 2)
            psO, kO = io.pb[4 + h % 2], f"pb{4+h%2}"
            p.run("dve", lambda v, psO=psO: v.reciprocal(rrow[:], psO[64:65, 0:256]),
                  reads=[kO], writes=[f"{sq}_rrow"])
            p.run("te", lambda t: t.matmul(io.pb[6][0:64, 0:256],
                                           io.ones_f32(0, 1, 64), rrow[:],
                                           start=True, stop=True),
                  reads=[f"{sq}_rrow", "cf32"], writes=["pb6"])
            p.run("dve", lambda v: v.tensor_copy(rbt[0:64, :], io.pb[6][0:64, 0:256]),
                  reads=["pb6"], writes=[f"{sq}_rbt"])
            p.run("dve", lambda v, psO=psO, c=c, off=off: v.tensor_tensor(
                acat[c][off:off + 64, :], psO[0:64, 0:256], rbt[0:64, :], op=ALU.mult),
                reads=[kO, f"{sq}_rbt"], writes=[f"{sq}_ac{c}"])

        for h in range(16):
            c, off = h // 2, 64 * (h % 2)
            psO, kO = io.pb[4 + h % 2], f"pb{4+h%2}"

            def _av(j, psO=psO, kO=kO, h=h):
                p.group("te", [
                    lambda t, psO=psO, mc=2 * j + half, half=half, j=j, h=h: t.matmul(
                        psO[0:65, 0:256], vx[mc][:, 65 * h:65 * h + 65],
                        exb[j % 3][:, half * 256:half * 256 + 256],
                        start=(mc == 0), stop=(mc == 15))
                    for half in range(2)],
                    reads=[f"{sq}_vx{2*j}", f"{sq}_vx{2*j+1}", f"{sq}_ex{j%3}"],
                    writes=[kO])

            for j in range(8):
                psS, kS = io.pb[j % 4], f"pb{j%4}"
                p.group("te", [
                    lambda t, psS=psS, c=c, off=off, mc=2 * j + half, half=half,
                    col0=col0: t.matmul(
                        psS[:, half * 256:half * 256 + 256],
                        kk[c][off:off + 64, mc * 128:(mc + 1) * 128],
                        qq[c][off:off + 64, col0:col0 + 256], start=True, stop=True)
                    for half in range(2)],
                    reads=[f"{sq}_k{c}", f"{sq}_q{c}"], writes=[kS])
                p.run("dve", lambda v, psS=psS, j=j: v.tensor_tensor(
                    ub[j % 3][:], psS[:, 0:512], Lp[j][:], op=ALU.add),
                    reads=[kS, f"{sq}_Lp{j}"], writes=[f"{sq}_u{j%3}"])
                p.run("act", lambda s, j=j, h=h: s.activation(
                    exb[j % 3][:], ub[j % 3][:], AF.Exp, bias=negB[:, h:h + 1]),
                    reads=[f"{sq}_u{j%3}", f"{sq}_negB"], writes=[f"{sq}_ex{j%3}"])
                if j >= 2:
                    _av(j - 2)
                if j == 3 and h > 0:
                    _finish_head(h - 1)
            _av(6)
            _av(7)
        _finish_head(15)
        for m in range(8):
            ps, kps = io.pb[2 + m % 2], f"pb{2+m%2}"
            mmacc(p, ps[:, 0:256],
                  [(_wslices(wP, k, m * 128, 128), acat[k][:]) for k in range(8)],
                  reads=[f"{sq}_wP"] + [f"{sq}_ac{k}" for k in range(8)], wkey=kps)
            p.dma("sp", x0t[m % 2][:], io.d["x0"][row0 + m * 128: row0 + (m + 1) * 128,
                                                  col0:col0 + 256],
                  reads=["x0"], writes=[f"{sq}_x0t{m%2}"])
            p.run("pool", lambda g, m=m: g.tensor_tensor(
                ptt[m % 2][:], x0t[m % 2][:], oprb[:], op=ALU.mult),
                reads=[f"{sq}_x0t{m%2}", f"{sq}_oprb"], writes=[f"{sq}_pt{m%2}"])
            p.run("dve", lambda v, ps=ps, m=m: v.scalar_tensor_tensor(
                t1b[0][:], ps[:, 0:256], io.acol("bproj", m), mrb[:],
                op0=ALU.add, op1=ALU.subtract),
                reads=[kps, "arena", f"{sq}_mrb"], writes=[f"{sq}_t10"])
            p.run("dve", lambda v, m=m: v.tensor_tensor(
                onb[:, (m % 2) * 256:(m % 2) * 256 + 256], t1b[0][:],
                ptt[m % 2][:], op=ALU.add),
                reads=[f"{sq}_t10", f"{sq}_pt{m%2}"], writes=[f"{sq}_onb"])
            if m % 2 == 1:
                g = m // 2
                dst = io.d["xr"][row0 + g * 256:row0 + g * 256 + 256,
                                 col0:col0 + 256].rearrange("(m p) n -> p m n", p=128)
                p.dma("act", dst, onb[:].rearrange("p (m n) -> p m n", n=256),
                      reads=[f"{sq}_onb"], writes=["xr"])
    p.barrier()



def emit_moe0(p, io):
    nc = io.nc
    io.ov_reset()
    xnt = [io.salloc(f"m2_xn{c}", [128, S], BF16) for c in range(8)]
    acc = [io.salloc(f"m2_acc{c}", [128, S], F32) for c in range(8)]
    combT = io.salloc("m2_combT", [32, T], BF16)
    wg = io.salloc("m2_wg", [128, 8 * 32], BF16)
    guw = [io.salloc(f"m2_guw{i}", [128, 8192], BF16) for i in range(2)]
    dw = [io.salloc(f"m2_dw{i}", [128, 4096], BF16) for i in range(2)]
    hgt = [io.salloc(f"m2_hg{i}", [128, 512], BF16) for i in range(8)]
    sgt = [io.salloc(f"m2_sg{i}", [128, 512], BF16) for i in range(2)]
    htt = [io.salloc(f"m2_h{i}", [128, 512], BF16) for i in range(2)]
    gb = [io.salloc(f"m2_gb{i}", [128, 512], BF16) for i in range(2)]
    probs = io.salloc("m2_probs", [128, 32], F32)
    cur = io.salloc("m2_cur", [128, 32], F32)
    selm = io.salloc("m2_selm", [128, 32], F32)
    eqt = io.salloc("m2_eq", [128, 32], F32)
    tmp32 = io.salloc("m2_tmp", [128, 32], F32)
    combf = io.salloc("m2_combf", [128, 32], F32)
    combbf = io.salloc("m2_combbf", [128, 32], BF16)
    cols = io.salloc("m2_cols", [128, 8], F32)

    p.dma("pool", wg[:].rearrange("p (c m) -> p c m", m=32),
          io.d["gwT"].rearrange("(c p) m -> p c m", p=128),
          reads=["gwT"], writes=["m2_wg"])

    for sb in range(2):
        row0 = sb * 1024
        tok0 = sb * S
        for c in range(8):
            p.dma("sp", xnt[c][:], io.d["xnh"][row0 + c * 128: row0 + (c + 1) * 128, :],
                  reads=["xnh"], writes=[f"m2_xn{c}"])
            p.dma("sp", acc[c][:], io.d["xr"][row0 + c * 128: row0 + (c + 1) * 128, :],
                  reads=["xr"], writes=[f"m2_acc{c}"])
        def _gate_tt(tt):
            ps, kps = io.pb[6], "pb6"
            mmacc(p, ps[:, 0:32],
                  [(xnt[c][:, tt * 128:(tt + 1) * 128], wg[:, c * 32:(c + 1) * 32])
                   for c in range(8)],
                  reads=[f"m2_xn{c}" for c in range(8)] + ["m2_wg"], wkey=kps)
            p.run("dve", lambda v, ps=ps: v.reduce_max(cols[:, 0:1], ps[:, 0:32],
                                                       axis=AX.X, negate=True),
                  reads=[kps], writes=["m2_cols"])
            p.run("act", lambda s, ps=ps: s.activation(probs[:], ps[:, 0:32], AF.Exp,
                                                       bias=cols[:, 0:1]),
                  reads=[kps, "m2_cols"], writes=["m2_probs"])
            p.run("dve", lambda v: v.tensor_reduce(cols[:, 2:3], probs[:], axis=AX.X,
                                                   op=ALU.add),
                  reads=["m2_probs"], writes=["m2_cols"])
            p.run("dve", lambda v: v.reciprocal(cols[:, 3:4], cols[:, 2:3]),
                  reads=["m2_cols"], writes=["m2_cols"])
            p.run("dve", lambda v: v.tensor_scalar(probs[:], probs[:], cols[:, 3:4],
                                                   None, op0=ALU.mult),
                  reads=["m2_probs", "m2_cols"], writes=["m2_probs"])
            p.run("dve", lambda v: v.tensor_copy(cur[:], probs[:]),
                  reads=["m2_probs"], writes=["m2_cur"])
            p.run("dve", lambda v: v.memset(selm[:], 0.0), writes=["m2_selm"])
            p.run("dve", lambda v: v.memset(cols[:, 4:5], 0.0), writes=["m2_cols"])
            for i in range(8):
                p.run("dve", lambda v: v.reduce_max(cols[:, 5:6], cur[:], axis=AX.X),
                      reads=["m2_cur"], writes=["m2_cols"])
                p.run("dve", lambda v: v.tensor_tensor(cols[:, 4:5], cols[:, 4:5],
                                                       cols[:, 5:6], op=ALU.add),
                      reads=["m2_cols"], writes=["m2_cols"])
                p.run("dve", lambda v: v.tensor_scalar(eqt[:], cur[:], cols[:, 5:6],
                                                       None, op0=ALU.is_equal),
                      reads=["m2_cur", "m2_cols"], writes=["m2_eq"])
                p.run("dve", lambda v: v.tensor_tensor(selm[:], selm[:], eqt[:],
                                                       op=ALU.add),
                      reads=["m2_selm", "m2_eq"], writes=["m2_selm"])
                p.run("dve", lambda v: v.scalar_tensor_tensor(
                    tmp32[:], eqt[:], cols[:, 5:6], eqt[:], op0=ALU.mult, op1=ALU.add),
                    reads=["m2_eq", "m2_cols"], writes=["m2_tmp"])
                p.run("dve", lambda v: v.tensor_tensor(cur[:], cur[:], tmp32[:],
                                                       op=ALU.subtract),
                      reads=["m2_cur", "m2_tmp"], writes=["m2_cur"])
            p.run("dve", lambda v: v.reciprocal(cols[:, 6:7], cols[:, 4:5]),
                  reads=["m2_cols"], writes=["m2_cols"])
            p.run("dve", lambda v: v.tensor_tensor(combf[:], selm[:], probs[:],
                                                   op=ALU.mult),
                  reads=["m2_selm", "m2_probs"], writes=["m2_combf"])
            p.run("dve", lambda v: v.tensor_scalar(combbf[:], combf[:], cols[:, 6:7],
                                                   None, op0=ALU.mult),
                  reads=["m2_combf", "m2_cols"], writes=["m2_combbf"])
            p.run("te", lambda t: t.transpose(io.pbt[0:32, 0:128], combbf[:],
                                              io.ident()),
                  reads=["m2_combbf", "cbf"], writes=["pbt"])
            p.run("dve", lambda v, tt=tt, tok0=tok0: v.tensor_copy(
                combT[:, tok0 + tt * 128: tok0 + (tt + 1) * 128], io.pbt[0:32, 0:128]),
                reads=["pbt"], writes=[f"m2_combT{tt}"])
        for e in range(32):
            b = e % 2
            p.dma("pool", guw[b][:].rearrange("p (c m) -> p c m", m=1024),
                  io.d["guwT"][e].rearrange("(c p) m -> p c m", p=128),
                  reads=["guwT"], writes=[f"m2_guw{b}"])
            p.dma("pool", dw[b][:].rearrange("p (c m) -> p c m", m=1024),
                  io.d["dwT"][e].rearrange("(c p) m -> p c m", p=128),
                  reads=["dwT"], writes=[f"m2_dw{b}"])
            def _emit_y(nb, b=b, col0_of=lambda nb: nb * 512):
                col0 = col0_of(nb)
                for out in range(8):
                    psY, kY = io.pb[4 + out % 2], f"pb{4+out%2}"
                    hks = [4 * (nb % 2) + k for k in range(4)]
                    mmacc(p, psY[:, 0:512],
                          [(_wslices(dw[b], k, out * 128, 128), hgt[4 * (nb % 2) + k][:])
                           for k in range(4)],
                          reads=[f"m2_dw{b}"] + [f"m2_hg{k}" for k in hks],
                          wkey=kY)
                    p.run("dve", lambda v, psY=psY, out=out, col0=col0: v.tensor_tensor(
                        acc[out][:, col0:col0 + 512], psY[:, 0:512],
                        acc[out][:, col0:col0 + 512], op=ALU.add),
                        reads=[kY, f"m2_acc{out}"], writes=[f"m2_acc{out}"])

            for nb in range(4):
                col0 = nb * 512
                if e == 0:
                    for tt in range(4 * nb, 4 * nb + 4):
                        _gate_tt(tt)
                p.run("te", lambda t, e=e, tok0=tok0, col0=col0: t.matmul(
                    io.pb[6][:, 0:512],
                    io.cbf[0:32, CBF_OH32 + e * 128: CBF_OH32 + (e + 1) * 128],
                    combT[0:32, tok0 + col0: tok0 + col0 + 512],
                    start=True, stop=True),
                    reads=[f"m2_combT{tt}" for tt in range(4 * nb, 4 * nb + 4)] + ["cbf"],
                    writes=["pb6"])
                p.run("dve", lambda v, nb=nb: v.tensor_copy(gb[nb % 2][:],
                                                            io.pb[6][:, 0:512]),
                      reads=["pb6"], writes=[f"m2_gb{nb%2}"])
                xnks = [f"m2_xn{k}" for k in range(8)]
                for mp in range(4):
                    psG, kG = io.pb[mp % 2], f"pb{mp%2}"
                    psU, kU = io.pb[2 + mp % 2], f"pb{2+mp%2}"
                    mmacc(p, psG[:, 0:512],
                          [(_wslices(guw[b], k, mp * 128, 128),
                            xnt[k][:, col0:col0 + 512]) for k in range(8)],
                          reads=[f"m2_guw{b}"] + xnks, wkey=kG)
                    mmacc(p, psU[:, 0:512],
                          [(_wslices(guw[b], k, (mp + 4) * 128, 128),
                            xnt[k][:, col0:col0 + 512]) for k in range(8)],
                          reads=[f"m2_guw{b}"] + xnks, wkey=kU)
                    hj = 4 * (nb % 2) + mp
                    p.run("act", lambda s, psG=psG, mp=mp: s.activation(
                        sgt[mp % 2][:], psG[:, 0:512], AF.Silu),
                        reads=[kG], writes=[f"m2_sg{mp%2}"])
                    p.run("dve", lambda v, psU=psU, mp=mp: v.tensor_tensor(
                        htt[mp % 2][:], psU[:, 0:512], sgt[mp % 2][:], op=ALU.mult),
                        reads=[kU, f"m2_sg{mp%2}"], writes=[f"m2_h{mp%2}"])
                    p.run("pool", lambda g, mp=mp, nb=nb, hj=hj: g.tensor_tensor(
                        hgt[hj][:], htt[mp % 2][:], gb[nb % 2][:], op=ALU.mult),
                        reads=[f"m2_h{mp%2}", f"m2_gb{nb%2}"], writes=[f"m2_hg{hj}"])
                if nb > 0:
                    _emit_y(nb - 1)
            _emit_y(3)
        for c in range(8):
            p.dma("act", io.d["xr"][row0 + c * 128: row0 + (c + 1) * 128, :], acc[c][:],
                  reads=[f"m2_acc{c}"], writes=["xr"])
    p.barrier()



def _ldcol(p, io, buf, key, wname, col_lo, col_cnt, rows=1024, e=None):
    ap = io.d[wname] if e is None else io.d[wname][e]
    src = ap[:, col_lo:col_lo + col_cnt].rearrange("(c p) m -> p c m", p=128)
    dst = buf[:].rearrange("p (c m) -> p c m", m=col_cnt)[:, 0:rows // 128, :]
    p.dma("pool", dst, src, reads=[wname], writes=[key])


def emit_mixer1(p, io, seq):
    nc = io.nc
    sq = f"x{seq}"
    row0 = seq * 1024
    io.ov_reset()
    normed = [io.salloc(f"{sq}_nm{c}", [128, S], BF16) for c in range(8)]
    rwkv = [io.salloc(f"{sq}_rw{c}", [128, S], BF16) for c in range(8)]

    for j in range(8):
        p.run("act", lambda s, j=j: s.activation(io.misc[:, 8 + j:9 + j],
                                                 io.acol("td", j), AF.Exp),
              reads=["arena"], writes=["misc_ew"])
        p.run("act", lambda s, j=j: s.activation(io.misc[:, 8 + j:9 + j],
                                                 io.misc[:, 8 + j:9 + j], AF.Exp,
                                                 scale=-1.0),
              reads=["misc_ew"], writes=["misc_ew"])
        p.run("act", lambda s, j=j: s.activation(io.misc[:, 16 + j:17 + j],
                                                 io.acol("tf", j), AF.Exp),
              reads=["arena"], writes=["misc_eu"])

    mark = io.ov_mark()
    xnt = [io.salloc(f"{sq}_xt{c}", [128, S], BF16) for c in range(8)]
    sqt = io.salloc(f"{sq}_sqt", [128, 512], BF16)
    rrows = io.salloc(f"{sq}_rr", [1, 512], F32)
    rbt = io.salloc(f"{sq}_rbt", [128, 512], F32)
    for c in range(8):
        p.dma("sp", xnt[c][:], io.d["xnh"][row0 + c * 128: row0 + (c + 1) * 128, :],
              reads=["xnh"], writes=[f"{sq}_xt{c}"])
    for nq in range(4):
        ps, kps = io.pb[nq % 2], f"pb{nq%2}"
        for c in range(8):
            p.run("dve", lambda v, c=c, nq=nq: v.tensor_tensor(
                sqt[:], xnt[c][:, nq * 512:(nq + 1) * 512], xnt[c][:, nq * 512:(nq + 1) * 512],
                op=ALU.mult),
                reads=[f"{sq}_xt{c}"], writes=[f"{sq}_sqt"])
            p.run("te", lambda t, ps=ps, c=c: t.matmul(
                ps[0:1, 0:512], io.cbf[:, CBF_ONES:CBF_ONES + 1], sqt[:],
                start=(c == 0), stop=(c == 7)),
                reads=[f"{sq}_sqt", "cbf"], writes=[kps])
        p.run("dve", lambda v, ps=ps: v.tensor_scalar(rrows[:], ps[0:1, 0:512],
                                                      1.0 / 1024, None, op0=ALU.mult),
              reads=[kps], writes=[f"{sq}_rr"])
        p.run("act", lambda s: s.activation(rrows[:], rrows[:], AF.Sqrt, bias=1e-6),
              reads=[f"{sq}_rr"], writes=[f"{sq}_rr"])
        p.run("dve", lambda v: v.reciprocal(rrows[:], rrows[:]),
              reads=[f"{sq}_rr"], writes=[f"{sq}_rr"])
        p.run("te", lambda t: t.matmul(io.pb[6][:, 0:512], io.ones_f32(0, 1, 128),
                                       rrows[:], start=True, stop=True),
              reads=[f"{sq}_rr", "cf32"], writes=["pb6"])
        p.run("dve", lambda v: v.tensor_copy(rbt[:], io.pb[6][:, 0:512]),
              reads=["pb6"], writes=[f"{sq}_rbt"])
        for c in range(8):
            p.run("dve", lambda v, c=c, nq=nq: v.tensor_tensor(
                normed[c][:, nq * 512:(nq + 1) * 512],
                xnt[c][:, nq * 512:(nq + 1) * 512], rbt[:], op=ALU.mult),
                reads=[f"{sq}_xt{c}", f"{sq}_rbt"], writes=[f"{sq}_nm{c}"])
    p.barrier()
    io.ov_release(mark)

    wk = [io.salloc(f"{sq}_wk{i}", [128, 1024], BF16) for i in range(2)]
    wv = [io.salloc(f"{sq}_wv{i}", [128, 1024], BF16) for i in range(2)]
    wr = [io.salloc(f"{sq}_wr{i}", [128, 1024], BF16) for i in range(2)]
    ksb = io.salloc(f"{sq}_ksb", [128, S], F32)
    kvb = io.salloc(f"{sq}_kv", [128, S], F32)
    rsb = io.salloc(f"{sq}_rsb", [128, S], BF16)
    ewt = io.salloc(f"{sq}_ewt", [128, S], F32)
    scan = io.salloc(f"{sq}_scan", [128, S + 1], F32)
    _ldcol(p, io, wk[0], f"{sq}_wk0", "WkT1", 0, 128)
    _ldcol(p, io, wv[0], f"{sq}_wv0", "WvT1", 0, 128)
    _ldcol(p, io, wr[0], f"{sq}_wr0", "WrT1", 0, 128)
    for oc in range(8):
        b = oc % 2
        if oc < 7:
            nb_ = (oc + 1) % 2
            _ldcol(p, io, wk[nb_], f"{sq}_wk{nb_}", "WkT1", (oc + 1) * 128, 128)
            _ldcol(p, io, wv[nb_], f"{sq}_wv{nb_}", "WvT1", (oc + 1) * 128, 128)
            _ldcol(p, io, wr[nb_], f"{sq}_wr{nb_}", "WrT1", (oc + 1) * 128, 128)
        for nq in range(4):
            psK, kK = io.pb[0], "pb0"
            psV, kV = io.pb[1], "pb1"
            psR, kR = io.pb[2], "pb2"
            mmacc(p, psK[:, 0:512],
                  [(wk[b][:, k * 128:(k + 1) * 128],
                    normed[k][:, nq * 512:(nq + 1) * 512]) for k in range(8)],
                  reads=[f"{sq}_wk{b}"] + [f"{sq}_nm{k}" for k in range(8)], wkey=kK)
            p.run("act", lambda s, psK=psK, nq=nq: s.activation(
                ksb[:, nq * 512:(nq + 1) * 512], psK[:, 0:512], AF.Copy),
                reads=[kK], writes=[f"{sq}_ksb"])
            mmacc(p, psV[:, 0:512],
                  [(wv[b][:, k * 128:(k + 1) * 128],
                    normed[k][:, nq * 512:(nq + 1) * 512]) for k in range(8)],
                  reads=[f"{sq}_wv{b}"] + [f"{sq}_nm{k}" for k in range(8)], wkey=kV)
            p.run("dve", lambda v, psV=psV, nq=nq: v.tensor_tensor(
                kvb[:, nq * 512:(nq + 1) * 512], psV[:, 0:512],
                ksb[:, nq * 512:(nq + 1) * 512], op=ALU.mult),
                reads=[kV, f"{sq}_ksb"], writes=[f"{sq}_kv"])
            mmacc(p, psR[:, 0:512],
                  [(wr[b][:, k * 128:(k + 1) * 128],
                    normed[k][:, nq * 512:(nq + 1) * 512]) for k in range(8)],
                  reads=[f"{sq}_wr{b}"] + [f"{sq}_nm{k}" for k in range(8)], wkey=kR)
            p.run("act", lambda s, psR=psR, nq=nq: s.activation(
                rsb[:, nq * 512:(nq + 1) * 512], psR[:, 0:512], AF.Sigmoid),
                reads=[kR], writes=[f"{sq}_rsb"])
        p.run("pool", lambda g: g.memset(ewt[:], 1.0), writes=[f"{sq}_ewt"])
        p.run("dve", lambda v, oc=oc: v.tensor_scalar(
            ewt[:], ewt[:], io.misc[:, 8 + oc:9 + oc], None, op0=ALU.mult),
            reads=[f"{sq}_ewt", "misc_ew"], writes=[f"{sq}_ewt"])
        p.run("dve", lambda v: v.memset(scan[:, 0:1], 0.0), writes=[f"{sq}_scan"])
        p.run("dve", lambda v: v.tensor_tensor_scan(
            scan[:, 1:S + 1], ewt[:], kvb[:], 0.0, op0=ALU.mult, op1=ALU.add),
            reads=[f"{sq}_ewt", f"{sq}_kv", f"{sq}_scan"], writes=[f"{sq}_scan"])
        p.run("dve", lambda v, oc=oc: v.scalar_tensor_tensor(
            ewt[:], kvb[:], io.misc[:, 16 + oc:17 + oc], scan[:, 0:S],
            op0=ALU.mult, op1=ALU.add),
            reads=[f"{sq}_kv", "misc_eu", f"{sq}_scan", f"{sq}_ewt"], writes=[f"{sq}_ewt"])
        p.run("pool", lambda g, oc=oc: g.tensor_tensor(
            rwkv[oc][:], ewt[:], rsb[:], op=ALU.mult),
            reads=[f"{sq}_ewt", f"{sq}_rsb"], writes=[f"{sq}_rw{oc}"])
    wo = [io.salloc(f"{sq}_wo{i}", [128, 1024], BF16) for i in range(2)]
    srows = io.salloc(f"{sq}_sr", [1, 1024], F32)
    r1b4 = io.salloc(f"{sq}_r1b", [128, S], F32)
    mrb4 = io.salloc(f"{sq}_mrb", [128, S], F32)
    x1t = [io.salloc(f"{sq}_x1t{i}", [128, 512], F32) for i in range(2)]
    t1t = [io.salloc(f"{sq}_t1t{i}", [128, 512], F32) for i in range(2)]
    t2t = [io.salloc(f"{sq}_t2t{i}", [128, 512], F32) for i in range(2)]
    xit = [io.salloc(f"{sq}_xit{i}", [128, 512], F32) for i in range(2)]
    st_base = 4 + seq * 2
    for nq in range(4):
        col0 = nq * 512
        p.dma("sp", srows[:, 0:512], io.d["stats"][st_base:st_base + 1, col0:col0 + 512],
              reads=["stats"], writes=[f"{sq}_srA"])
        p.dma("sp", srows[:, 512:1024], io.d["stats"][st_base + 1:st_base + 2,
                                                      col0:col0 + 512],
              reads=["stats"], writes=[f"{sq}_srB"])
        p.run("te", lambda t, col0=col0: t.matmul(
            io.pb[6][:, 0:512], io.ones_f32(0, 1, 128),
            srows[:, 512:1024], start=True, stop=True),
            reads=[f"{sq}_srB", "cf32"], writes=["pb6"])
        p.run("dve", lambda v, col0=col0: v.tensor_copy(
            r1b4[:, col0:col0 + 512], io.pb[6][:, 0:512]),
            reads=["pb6"], writes=[f"{sq}_r1b"])
        p.run("dve", lambda v: v.tensor_tensor(srows[:, 0:512], srows[:, 0:512],
                                               srows[:, 512:1024], op=ALU.mult),
              reads=[f"{sq}_srA", f"{sq}_srB"], writes=[f"{sq}_srA"])
        p.run("te", lambda t: t.matmul(io.pb[6][:, 0:512], io.ones_f32(0, 1, 128),
                                       srows[:, 0:512], start=True, stop=True),
              reads=[f"{sq}_srA", "cf32"], writes=["pb6"])
        p.run("dve", lambda v, col0=col0: v.tensor_copy(
            mrb4[:, col0:col0 + 512], io.pb[6][:, 0:512]),
            reads=["pb6"], writes=[f"{sq}_mrb"])
    _ldcol(p, io, wo[0], f"{sq}_wo0", "WoT1", 0, 128)
    for out in range(8):
        b2_ = out % 2
        if out < 7:
            nb_ = (out + 1) % 2
            _ldcol(p, io, wo[nb_], f"{sq}_wo{nb_}", "WoT1", (out + 1) * 128, 128)
        for nq in range(4):
            col0 = nq * 512
            ps, kps = io.pb[3 + nq % 2], f"pb{3+nq%2}"
            mmacc(p, ps[:, 0:512],
                  [(wo[b2_][:, k * 128:(k + 1) * 128], rwkv[k][:, col0:col0 + 512])
                   for k in range(8)],
                  reads=[f"{sq}_wo{b2_}"] + [f"{sq}_rw{k}" for k in range(8)], wkey=kps)
            p.run("dve", lambda v, ps=ps, out=out, col0=col0, nq=nq: v.scalar_tensor_tensor(
                t1t[nq % 2][:], ps[:, 0:512], io.acol("ms", out),
                mrb4[:, col0:col0 + 512], op0=ALU.mult, op1=ALU.subtract),
                reads=[kps, "arena", f"{sq}_mrb"], writes=[f"{sq}_t1t{nq%2}"])
            p.dma("sp", x1t[nq % 2][:], io.d["xr"][row0 + out * 128: row0 + out * 128 + 128,
                                                   col0:col0 + 512],
                  reads=["xr"], writes=[f"{sq}_x1t{nq%2}"])
            p.run("pool", lambda g, nq=nq, col0=col0: g.tensor_tensor(
                t2t[nq % 2][:], x1t[nq % 2][:], r1b4[:, col0:col0 + 512], op=ALU.mult),
                reads=[f"{sq}_x1t{nq%2}", f"{sq}_r1b"], writes=[f"{sq}_t2t{nq%2}"])
            p.run("dve", lambda v, nq=nq: v.tensor_tensor(
                xit[nq % 2][:], t1t[nq % 2][:], t2t[nq % 2][:], op=ALU.add),
                reads=[f"{sq}_t1t{nq%2}", f"{sq}_t2t{nq%2}"], writes=[f"{sq}_xit{nq%2}"])
            p.dma("act", io.d["xih"][row0 + out * 128: row0 + out * 128 + 128,
                                     col0:col0 + 512], xit[nq % 2][:],
                  reads=[f"{sq}_xit{nq%2}"], writes=["xih"])
    p.barrier()



def emit_mlp1(p, io, seq):
    sq = f"y{seq}"
    row0 = seq * 1024
    io.ov_reset()
    xi = [io.salloc(f"{sq}_xi{c}", [128, S], F32) for c in range(8)]
    nm2 = [io.salloc(f"{sq}_n2{c}", [128, S], BF16) for c in range(8)]
    h2 = [io.salloc(f"{sq}_h2{c}", [128, 512], BF16) for c in range(32)]
    sqt = io.salloc(f"{sq}_sqt", [128, 512], F32)
    lrt = [io.salloc(f"{sq}_lr{i}", [128, 512], BF16) for i in range(2)]
    rrows = io.salloc(f"{sq}_rr", [1, 512], F32)
    rbt = io.salloc(f"{sq}_rbt", [128, 512], F32)
    wfc = [io.salloc(f"{sq}_wfc{i}", [128, 4096], BF16) for i in range(2)]
    wml = [io.salloc(f"{sq}_wml{i}", [128, 4096], BF16) for i in range(2)]
    x1t = [io.salloc(f"{sq}_x1t{i}", [128, 512], F32) for i in range(2)]
    xot = [io.salloc(f"{sq}_xot{i}", [128, 512], F32) for i in range(2)]
    for c in range(8):
        p.dma("sp", xi[c][:], io.d["xih"][row0 + c * 128: row0 + (c + 1) * 128, :],
              reads=["xih"], writes=[f"{sq}_xi{c}"])
    for nq in range(4):
        ps, kps = io.pb[nq % 2], f"pb{nq%2}"
        for c in range(8):
            p.run("act", lambda s, c=c, nq=nq: s.activation(
                sqt[:], xi[c][:, nq * 512:(nq + 1) * 512], AF.Square),
                reads=[f"{sq}_xi{c}"], writes=[f"{sq}_sqt"])
            p.run("te", lambda t, ps=ps, c=c: t.matmul(
                ps[0:1, 0:512], io.ones_f32(0, 128, 1), sqt[:],
                start=(c == 0), stop=(c == 7)),
                reads=[f"{sq}_sqt", "cf32"], writes=[kps])
        p.run("dve", lambda v, ps=ps: v.tensor_scalar(rrows[:], ps[0:1, 0:512],
                                                      1.0 / 1024, None, op0=ALU.mult),
              reads=[kps], writes=[f"{sq}_rr"])
        p.run("act", lambda s: s.activation(rrows[:], rrows[:], AF.Sqrt, bias=1e-6),
              reads=[f"{sq}_rr"], writes=[f"{sq}_rr"])
        p.run("dve", lambda v: v.reciprocal(rrows[:], rrows[:]),
              reads=[f"{sq}_rr"], writes=[f"{sq}_rr"])
        p.run("te", lambda t: t.matmul(io.pb[6][:, 0:512], io.ones_f32(0, 1, 128),
                                       rrows[:], start=True, stop=True),
              reads=[f"{sq}_rr", "cf32"], writes=["pb6"])
        p.run("dve", lambda v: v.tensor_copy(rbt[:], io.pb[6][:, 0:512]),
              reads=["pb6"], writes=[f"{sq}_rbt"])
        for c in range(8):
            p.run("dve", lambda v, c=c, nq=nq: v.tensor_tensor(
                nm2[c][:, nq * 512:(nq + 1) * 512], xi[c][:, nq * 512:(nq + 1) * 512],
                rbt[:], op=ALU.mult),
                reads=[f"{sq}_xi{c}", f"{sq}_rbt"], writes=[f"{sq}_n2{c}"])
    _ldcol(p, io, wfc[0], f"{sq}_wfc0", "WfcT", 0, 512)
    for nq in range(4):
        col0 = nq * 512
        for ocg in range(8):
            b = ocg % 2
            nxt = ocg + 1
            if nxt < 8:
                _ldcol(p, io, wfc[nxt % 2], f"{sq}_wfc{nxt%2}", "WfcT", nxt * 512, 512)
            elif nq < 3:
                _ldcol(p, io, wfc[0], f"{sq}_wfc0", "WfcT", 0, 512)
            for oi in range(4):
                oc = ocg * 4 + oi
                ps, kps = io.pb[oc % 2], f"pb{oc%2}"
                mmacc(p, ps[:, 0:512],
                      [(wfc[b][:, k * 512 + oi * 128: k * 512 + oi * 128 + 128],
                        nm2[k][:, col0:col0 + 512]) for k in range(8)],
                      reads=[f"{sq}_wfc{b}"] + [f"{sq}_n2{k}" for k in range(8)],
                      wkey=kps)
                p.run("act", lambda s, ps=ps, oc=oc: s.activation(
                    lrt[oc % 2][:], ps[:, 0:512], AF.Prelu, alpha=0.5),
                    reads=[kps], writes=[f"{sq}_lr{oc%2}"])
                p.run("act", lambda s, oc=oc: s.activation(
                    h2[oc][:], lrt[oc % 2][:], AF.Square),
                    reads=[f"{sq}_lr{oc%2}"], writes=[f"{sq}_h2{oc}"])
        if nq == 0:
            _ldcol(p, io, wml[0], f"{sq}_wml0", "WmlpT", 0, 128, rows=4096)
        for out in range(8):
            b = out % 2
            nxt = out + 1
            if nxt < 8:
                _ldcol(p, io, wml[nxt % 2], f"{sq}_wml{nxt%2}", "WmlpT",
                       nxt * 128, 128, rows=4096)
            elif nq < 3:
                _ldcol(p, io, wml[0], f"{sq}_wml0", "WmlpT", 0, 128, rows=4096)
            ps, kps = io.pb[3 + out % 2], f"pb{3+out%2}"
            mmacc(p, ps[:, 0:512],
                  [(wml[b][:, k * 128:(k + 1) * 128], h2[k][:]) for k in range(32)],
                  reads=[f"{sq}_wml{b}"] + [f"{sq}_h2{k}" for k in range(32)],
                  wkey=kps)
            p.dma("sp", x1t[out % 2][:], io.d["xr"][row0 + out * 128: row0 + out * 128 + 128,
                                                    col0:col0 + 512],
                  reads=["xr"], writes=[f"{sq}_x1t{out%2}"])
            p.run("dve", lambda v, ps=ps, out=out, col0=col0: v.scalar_tensor_tensor(
                xot[out % 2][:], ps[:, 0:512], io.acol("mls", out),
                xi[out][:, col0:col0 + 512], op0=ALU.mult, op1=ALU.add),
                reads=[kps, "arena", f"{sq}_xi{out}"], writes=[f"{sq}_xot{out%2}"])
            p.run("dve", lambda v, out=out: v.tensor_tensor(
                xot[out % 2][:], xot[out % 2][:], x1t[out % 2][:], op=ALU.add),
                reads=[f"{sq}_xot{out%2}", f"{sq}_x1t{out%2}"], writes=[f"{sq}_xot{out%2}"])
            p.dma("act", io.d["xr2"][row0 + out * 128: row0 + out * 128 + 128,
                                     col0:col0 + 512], xot[out % 2][:],
                  reads=[f"{sq}_xot{out%2}"], writes=["xr2"])
    p.barrier()



def emit_moe1(p, io):
    io.ov_reset()
    xnt = [io.salloc(f"m6_xn{c}", [128, 1024], BF16) for c in range(8)]
    acc = [io.salloc(f"m6_acc{c}", [128, 1024], F32) for c in range(8)]
    hgt = [io.salloc(f"m6_hg{j}", [128, 1024], BF16) for j in range(16)]
    gatesT = io.salloc("m6_gT", [16, T], BF16)
    wbank = [io.salloc(f"m6_w{i}", [128, 8192], BF16) for i in range(5)]
    wgl = io.salloc("m6_wgl", [128, 8 * 16], BF16)
    b2sb = io.salloc("m6_b2", [16, 1024], BF16)
    gb = io.salloc("m6_gb", [128, 1024], BF16)
    sgt = [io.salloc(f"m6_sg{i}", [128, 512], BF16) for i in range(2)]
    htmp = [io.salloc(f"m6_ht{i}", [128, 512], BF16) for i in range(2)]
    lsb = io.salloc("m6_lsb", [128, 16], F32)
    l2sb = io.salloc("m6_l2", [128, 16], F32)
    eq1 = io.salloc("m6_eq1", [128, 16], F32)
    eq2 = io.salloc("m6_eq2", [128, 16], F32)
    gts = io.salloc("m6_gts", [128, 16], F32)
    gtb = io.salloc("m6_gtb", [128, 16], BF16)
    cols = io.salloc("m6_cols", [128, 8], F32)

    p.dma("pool", wgl[:].rearrange("p (c m) -> p c m", m=16),
          io.d["Wg"].rearrange("(c p) m -> p c m", p=128),
          reads=["Wg"], writes=["m6_wgl"])
    p.dma("pool", b2sb[:], io.d["b2e"], reads=["b2e"], writes=["m6_b2"])

    def wload(bi, wname, col_lo, col_cnt, rows, e=None):
        _ldcol(p, io, wbank[bi], f"m6_w{bi}", wname, col_lo, col_cnt, rows=rows, e=e)

    for pa in range(4):
        seq, part = pa // 2, pa % 2
        row0 = seq * 1024
        tok0 = pa * 1024
        colp = part * 1024
        for c in range(8):
            p.dma("sp", xnt[c][:], io.d["xnh"][row0 + c * 128: row0 + (c + 1) * 128,
                                               colp:colp + 1024],
                  reads=["xnh"], writes=[f"m6_xn{c}"])
            p.dma("sp", acc[c][:], io.d["xr2"][row0 + c * 128: row0 + (c + 1) * 128,
                                               colp:colp + 1024],
                  reads=["xr2"], writes=[f"m6_acc{c}"])
        def _gate_tt(tt):
            ps, kps = io.pb[6], "pb6"
            mmacc(p, ps[:, 0:16],
                  [(xnt[c][:, tt * 128:(tt + 1) * 128], wgl[:, c * 16:(c + 1) * 16])
                   for c in range(8)],
                  reads=[f"m6_xn{c}" for c in range(8)] + ["m6_wgl"], wkey=kps)
            p.run("dve", lambda v, ps=ps: v.tensor_copy(lsb[:], ps[:, 0:16]),
                  reads=[kps], writes=["m6_lsb"])
            p.run("dve", lambda v: v.reduce_max(cols[:, 0:1], lsb[:], axis=AX.X),
                  reads=["m6_lsb"], writes=["m6_cols"])
            p.run("dve", lambda v: v.tensor_scalar(eq1[:], lsb[:], cols[:, 0:1],
                                                   None, op0=ALU.is_equal),
                  reads=["m6_lsb", "m6_cols"], writes=["m6_eq1"])
            p.run("dve", lambda v: v.scalar_tensor_tensor(
                l2sb[:], eq1[:], -1e30, lsb[:], op0=ALU.mult, op1=ALU.add),
                reads=["m6_eq1", "m6_lsb"], writes=["m6_l2"])
            p.run("dve", lambda v: v.reduce_max(cols[:, 1:2], l2sb[:], axis=AX.X),
                  reads=["m6_l2"], writes=["m6_cols"])
            p.run("dve", lambda v: v.tensor_scalar(eq2[:], l2sb[:], cols[:, 1:2],
                                                   None, op0=ALU.is_equal),
                  reads=["m6_l2", "m6_cols"], writes=["m6_eq2"])
            p.run("dve", lambda v: v.tensor_tensor(cols[:, 2:3], cols[:, 1:2],
                                                   cols[:, 0:1], op=ALU.subtract),
                  reads=["m6_cols"], writes=["m6_cols"])
            p.run("act", lambda s: s.activation(cols[:, 3:4], cols[:, 2:3], AF.Exp),
                  reads=["m6_cols"], writes=["m6_cols"])
            p.run("dve", lambda v: v.tensor_scalar(cols[:, 4:5], cols[:, 3:4], 1.0,
                                                   None, op0=ALU.add),
                  reads=["m6_cols"], writes=["m6_cols"])
            p.run("dve", lambda v: v.reciprocal(cols[:, 5:6], cols[:, 4:5]),
                  reads=["m6_cols"], writes=["m6_cols"])
            p.run("dve", lambda v: v.tensor_tensor(cols[:, 6:7], cols[:, 3:4],
                                                   cols[:, 5:6], op=ALU.mult),
                  reads=["m6_cols"], writes=["m6_cols"])
            p.run("dve", lambda v: v.tensor_scalar(gts[:], eq1[:], cols[:, 5:6],
                                                   None, op0=ALU.mult),
                  reads=["m6_eq1", "m6_cols"], writes=["m6_gts"])
            p.run("dve", lambda v: v.scalar_tensor_tensor(
                gts[:], eq2[:], cols[:, 6:7], gts[:], op0=ALU.mult, op1=ALU.add),
                reads=["m6_eq2", "m6_cols", "m6_gts"], writes=["m6_gts"])
            p.run("dve", lambda v: v.tensor_copy(gtb[:], gts[:]),
                  reads=["m6_gts"], writes=["m6_gtb"])
            p.run("te", lambda t: t.transpose(io.pbt[0:16, 0:128], gtb[:], io.ident()),
                  reads=["m6_gtb", "cbf"], writes=["pbt"])
            p.run("dve", lambda v, tt=tt, tok0=tok0: v.tensor_copy(
                gatesT[:, tok0 + tt * 128: tok0 + (tt + 1) * 128], io.pbt[0:16, 0:128]),
                reads=["pbt"], writes=[f"m6_gT{tt}"])
        for e in range(16):
            if e == 0:
                for tt in range(4):
                    _gate_tt(tt)
            p.run("te", lambda t, e=e, tok0=tok0: t.matmul(
                io.pb[6][:, 0:512],
                io.cbf[0:16, CBF_OH16 + e * 128: CBF_OH16 + (e + 1) * 128],
                gatesT[0:16, tok0: tok0 + 512], start=True, stop=True),
                reads=[f"m6_gT{tt}" for tt in range(4)] + ["cbf"], writes=["pb6"])
            p.run("dve", lambda v: v.tensor_copy(gb[:, 0:512], io.pb[6][:, 0:512]),
                  reads=["pb6"], writes=["m6_gb"])
            if e == 0:
                for tt in range(4, 8):
                    _gate_tt(tt)
            p.run("te", lambda t, e=e, tok0=tok0: t.matmul(
                io.pb[6][:, 0:512],
                io.cbf[0:16, CBF_OH16 + e * 128: CBF_OH16 + (e + 1) * 128],
                gatesT[0:16, tok0 + 512: tok0 + 1024], start=True, stop=True),
                reads=[f"m6_gT{tt}" for tt in range(4, 8)] + ["cbf"], writes=["pb6"])
            p.run("dve", lambda v: v.tensor_copy(gb[:, 512:1024], io.pb[6][:, 0:512]),
                  reads=["pb6"], writes=["m6_gb"])
            for mh in range(2):
                w1b, w11b = (0, 1) if mh == 0 else (2, 3)
                wload(w1b, "W1", mh * 1024, 1024, 1024, e=e)
                wload(w11b, "W11", mh * 1024, 1024, 1024, e=e)
                if mh == 1:
                    wload(0, "W2", 0 * 512, 512, 2048, e=e)
                    wload(1, "W2", 1 * 512, 512, 2048, e=e)
                for m in range(8):
                    j = mh * 8 + m
                    for nb in range(2):
                        psG, kG = io.pb[nb], f"pb{nb}"
                        psU, kU = io.pb[2 + nb], f"pb{2+nb}"
                        xnks = [f"m6_xn{k}" for k in range(8)]
                        mmacc(p, psG[:, 0:512],
                              [(_wslices(wbank[w1b], k, m * 128, 128),
                                xnt[k][:, nb * 512:(nb + 1) * 512]) for k in range(8)],
                              reads=[f"m6_w{w1b}"] + xnks, wkey=kG)
                        mmacc(p, psU[:, 0:512],
                              [(_wslices(wbank[w11b], k, m * 128, 128),
                                xnt[k][:, nb * 512:(nb + 1) * 512]) for k in range(8)],
                              reads=[f"m6_w{w11b}"] + xnks, wkey=kU)
                        p.run("act", lambda s, psG=psG, e=e, j=j, nb=nb: s.activation(
                            sgt[nb][:], psG[:, 0:512], AF.Silu,
                            bias=io.acol("b1", e * 16 + j)),
                            reads=[kG, "arena"], writes=[f"m6_sg{nb}"])
                        p.run("dve", lambda v, psU=psU, e=e, j=j, nb=nb: v.scalar_tensor_tensor(
                            htmp[nb][:], psU[:, 0:512], io.acol("b11", e * 16 + j),
                            sgt[nb][:], op0=ALU.add, op1=ALU.mult),
                            reads=[kU, "arena", f"m6_sg{nb}"], writes=[f"m6_ht{nb}"])
                        p.run("pool", lambda g, j=j, nb=nb: g.tensor_tensor(
                            hgt[j][:, nb * 512:(nb + 1) * 512], htmp[nb][:],
                            gb[:, nb * 512:(nb + 1) * 512], op=ALU.mult),
                            reads=[f"m6_ht{nb}", "m6_gb"], writes=[f"m6_hg{j}"])
            for oh in range(2):
                w2b = oh
                for nb in range(2):
                    for out in range(4):
                        o = oh * 4 + out
                        psY, kY = io.pb[4 + out % 2], f"pb{4+out%2}"
                        mmacc(p, psY[:, 0:512],
                              [(wbank[w2b][:, k * 512 + out * 128: k * 512 + out * 128 + 128],
                                hgt[k][:, nb * 512:(nb + 1) * 512]) for k in range(16)],
                              reads=[f"m6_w{w2b}"] + [f"m6_hg{k}" for k in range(16)],
                              wkey=kY)
                        p.run("dve", lambda v, psY=psY, o=o, nb=nb: v.tensor_tensor(
                            acc[o][:, nb * 512:(nb + 1) * 512], psY[:, 0:512],
                            acc[o][:, nb * 512:(nb + 1) * 512], op=ALU.add),
                            reads=[kY, f"m6_acc{o}"], writes=[f"m6_acc{o}"])
        for out in range(8):
            for nb in range(2):
                psB, kB = io.pb[6], "pb6"
                p.run("te", lambda t, out=out, nb=nb, tok0=tok0, psB=psB: t.matmul(
                    psB[:, 0:512], b2sb[:, out * 128:(out + 1) * 128],
                    gatesT[0:16, tok0 + nb * 512: tok0 + (nb + 1) * 512],
                    start=True, stop=True),
                    reads=["m6_b2", "m6_gT"], writes=[kB])
                p.run("dve", lambda v, psB=psB, out=out, nb=nb: v.tensor_tensor(
                    acc[out][:, nb * 512:(nb + 1) * 512], psB[:, 0:512],
                    acc[out][:, nb * 512:(nb + 1) * 512], op=ALU.add),
                    reads=[kB, f"m6_acc{out}"], writes=[f"m6_acc{out}"])
        for qq in range(4):
            qb1, qb11 = (0, 1) if qq % 2 == 0 else (2, 3)
            wload(qb1, "Ws1", qq * 1024, 1024, 1024)
            wload(qb11, "Ws11", qq * 1024, 1024, 1024)
            for m in range(8):
                for nb in range(2):
                    psG, kG = io.pb[nb], f"pb{nb}"
                    psU, kU = io.pb[2 + nb], f"pb{2+nb}"
                    xnks = [f"m6_xn{k}" for k in range(8)]
                    mmacc(p, psG[:, 0:512],
                          [(_wslices(wbank[qb1], k, m * 128, 128),
                            xnt[k][:, nb * 512:(nb + 1) * 512]) for k in range(8)],
                          reads=[f"m6_w{qb1}"] + xnks, wkey=kG)
                    mmacc(p, psU[:, 0:512],
                          [(_wslices(wbank[qb11], k, m * 128, 128),
                            xnt[k][:, nb * 512:(nb + 1) * 512]) for k in range(8)],
                          reads=[f"m6_w{qb11}"] + xnks, wkey=kU)
                    p.run("act", lambda s, psG=psG, qq=qq, m=m, nb=nb: s.activation(
                        sgt[nb][:], psG[:, 0:512], AF.Silu,
                        bias=io.acol("bs1", qq * 8 + m)),
                        reads=[kG, "arena"], writes=[f"m6_sg{nb}"])
                    p.run("dve", lambda v, psU=psU, qq=qq, m=m, nb=nb: v.scalar_tensor_tensor(
                        hgt[m][:, nb * 512:(nb + 1) * 512], psU[:, 0:512],
                        io.acol("bs11", qq * 8 + m), sgt[nb][:],
                        op0=ALU.add, op1=ALU.mult),
                        reads=[kU, "arena", f"m6_sg{nb}"], writes=[f"m6_hg{m}"])
            srcq = io.d["Ws2"][qq * 1024:(qq + 1) * 1024, :].rearrange(
                "(c p) m -> p c m", p=128)
            p.dma("pool", wbank[4][:].rearrange("p (c m) -> p c m", m=1024), srcq,
                  reads=["Ws2"], writes=["m6_w4"])
            for out in range(8):
                for nb in range(2):
                    psY, kY = io.pb[4 + out % 2], f"pb{4+out%2}"
                    mmacc(p, psY[:, 0:512],
                          [(_wslices(wbank[4], k, out * 128, 128),
                            hgt[k][:, nb * 512:(nb + 1) * 512]) for k in range(8)],
                          reads=["m6_w4"] + [f"m6_hg{k}" for k in range(8)], wkey=kY)
                    if qq == 3:
                        p.run("dve", lambda v, psY=psY, out=out, nb=nb: v.scalar_tensor_tensor(
                            acc[out][:, nb * 512:(nb + 1) * 512], psY[:, 0:512],
                            io.acol("bs2", out), acc[out][:, nb * 512:(nb + 1) * 512],
                            op0=ALU.add, op1=ALU.add),
                            reads=[kY, "arena", f"m6_acc{out}"], writes=[f"m6_acc{out}"])
                    else:
                        p.run("dve", lambda v, psY=psY, out=out, nb=nb: v.tensor_tensor(
                            acc[out][:, nb * 512:(nb + 1) * 512], psY[:, 0:512],
                            acc[out][:, nb * 512:(nb + 1) * 512], op=ALU.add),
                            reads=[kY, f"m6_acc{out}"], writes=[f"m6_acc{out}"])
        for c in range(8):
            p.dma("act", io.d["out"][row0 + c * 128: row0 + (c + 1) * 128,
                                     colp:colp + 1024], acc[c][:],
                  reads=[f"m6_acc{c}"], writes=["out"])
    p.barrier()



def _run_phases(in_maps, build_fns, ext_io, trace=False):
    nc = bass.Bass("TRN2", target_bir_lowering=False)
    io = IO(nc, *ext_io)
    p = Prog(nc)
    io.load_consts(p)
    for fn in build_fns:
        fn(p, io)
    p.finish()
    p.flush()
    res = run_bass_kernel_spmd(nc, in_maps, core_ids=list(range(8)), trace=trace)
    return res


def build_full(nc):
    io = IO(nc, set(), set())
    p = Prog(nc)
    io.load_consts(p)
    emit_ln(p, io, "x0", 0)
    emit_attn0(p, io, 0)
    emit_attn0(p, io, 1)
    emit_moe0(p, io)
    emit_ln(p, io, "xr", 1)
    emit_mixer1(p, io, 0)
    emit_mixer1(p, io, 1)
    emit_mlp1(p, io, 0)
    emit_mlp1(p, io, 1)
    emit_moe1(p, io)
    p.finish()
    p.flush()
    return p


def kernel(**inputs):
    in_maps = host_prepare(inputs)
    nc = bass.Bass("TRN2", target_bir_lowering=False)
    build_full(nc)
    res = run_bass_kernel_spmd(nc, in_maps, core_ids=list(range(8)))
    return host_gather(res.results)


# version 10
# speedup vs baseline: 1.1743x; 1.0851x over previous
import sys
import numpy as np

sys.path.insert(0, '/opt/trn_rl_repo')
import concourse.bass as bass
import concourse.mybir as mybir
from concourse.bass_utils import run_bass_kernel_spmd

F32 = mybir.dt.float32
BF16 = mybir.dt.bfloat16
AF = mybir.ActivationFunctionType
ALU = mybir.AluOpType
AX = mybir.AxisListType


ENGINES = ("te", "dve", "act", "pool", "sp")
DMA_QUEUES = ("sp", "pool", "act")


class Prog:

    def __init__(self, nc):
        self.nc = nc
        self.eng = {"te": nc.tensor, "dve": nc.vector, "act": nc.scalar,
                    "pool": nc.gpsimd, "sp": nc.sync}
        self.sem = {e: nc.alloc_semaphore(f"S_{e}") for e in ENGINES}
        self.NQS = 4
        self.qsem = {q: [nc.alloc_semaphore(f"Q_{q}{i}") for i in range(self.NQS)]
                     for q in DMA_QUEUES}
        self.cnt = {e: 0 for e in ENGINES}
        self.qcnt = {q: 0 for q in DMA_QUEUES}
        self.known = {e: {} for e in ENGINES}
        self.lastw = {}
        self.readers = {}
        self.queue = {e: [] for e in ENGINES}
        self.nwaits = 0
        self.ninstr = 0

    def _sem_of(self, token):
        if token.startswith("q_"):
            q, i = token[2:].split(".")
            return self.qsem[q][int(i)]
        return self.sem[token]

    def _need_from(self, reads, writes):
        need = {}

        def add(d):
            for name, val in d.items():
                if need.get(name, 0) < val:
                    need[name] = val
        for k in reads:
            add(self.lastw.get(k, {}))
        for k in writes:
            add(self.lastw.get(k, {}))
            add(self.readers.get(k, {}))
        return need

    def _emit_waits(self, e, need):
        for name, val in sorted(need.items()):
            if self.known[e].get(name, 0) >= val:
                continue
            sem = self._sem_of(name)
            self.queue[e].append(lambda eng, s=sem, v=val: eng.wait_ge(s, v))
            self.known[e][name] = val
            self.nwaits += 1

    def _record(self, token, reads, writes):
        name, val = token
        for k in writes:
            self.lastw[k] = {name: val}
            self.readers[k] = {}
        for k in reads:
            d = self.readers.setdefault(k, {})
            if d.get(name, 0) < val:
                d[name] = val

    def run(self, e, fn, reads=(), writes=()):
        need = self._need_from(reads, writes)
        self._emit_waits(e, need)
        self.cnt[e] += 1
        sem, val = self.sem[e], self.cnt[e]
        self.queue[e].append(lambda eng: fn(eng).then_inc(sem, 1))
        self._record((e, val), reads, writes)
        self.ninstr += 1

    def group(self, e, fns, reads=(), writes=()):
        need = self._need_from(reads, writes)
        self._emit_waits(e, need)
        self.cnt[e] += 1
        sem, val = self.sem[e], self.cnt[e]
        for f in fns[:-1]:
            self.queue[e].append(lambda eng, f=f: f(eng))
        self.queue[e].append(lambda eng, f=fns[-1]: f(eng).then_inc(sem, 1))
        self._record((e, val), reads, writes)
        self.ninstr += len(fns)

    def dma(self, q, out, in_, reads=(), writes=(), **kw):
        need = self._need_from(reads, writes)
        idx = self.qcnt[q]
        lane = idx % self.NQS
        name = f"q_{q}.{lane}"
        if idx >= self.NQS:
            prev = 16 * (idx // self.NQS)
            if need.get(name, 0) < prev:
                need[name] = prev
        self._emit_waits(q, need)
        self.qcnt[q] = idx + 1
        sem, val = self.qsem[q][lane], 16 * (idx // self.NQS + 1)
        self.queue[q].append(
            lambda eng, o=out, i=in_: eng.dma_start(o, i, **kw).then_inc(sem, 16))
        self._record((name, val), reads, writes)
        self.ninstr += 1

    def _qfinal(self):
        need = {}
        for q in DMA_QUEUES:
            for lane in range(self.NQS):
                n = self.qcnt[q] // self.NQS + (1 if lane < self.qcnt[q] % self.NQS else 0)
                if n:
                    need[f"q_{q}.{lane}"] = 16 * n
        return need

    def finish(self):
        need = {e: self.cnt[e] for e in ENGINES if self.cnt[e]}
        need.update(self._qfinal())
        self._emit_waits("sp", need)

    def barrier(self):
        need = {e: self.cnt[e] for e in ENGINES if self.cnt[e]}
        need.update(self._qfinal())
        for e in ENGINES:
            my = dict(need)
            my.pop(e, None)
            self._emit_waits(e, my)
        self.lastw = {}
        self.readers = {}
        for e in ENGINES:
            self.known[e] = dict(need)

    def flush(self):
        with self.nc.Block() as blk:
            for e in ENGINES:
                if not self.queue[e]:
                    continue
                deco = {"te": blk.tensor, "dve": blk.vector, "act": blk.scalar,
                        "pool": blk.gpsimd, "sp": blk.sync}[e]

                def body(eng, thunks=list(self.queue[e])):
                    for th in thunks:
                        th(eng)
                deco(body)
                self.queue[e] = []



D = 1024
NC = 8
S = 2048
NSEQ = 2
T = S * NSEQ

ARENA = {}


def _arena_cols():
    spec = [("bpre", 8), ("bproj", 8), ("pi", 1), ("bs1", 32), ("bs11", 32),
            ("bs2", 8), ("ms", 8), ("mls", 8), ("td", 8), ("tf", 8),
            ("b1", 256), ("b11", 256), ("zero", 1)]
    off = 0
    for name, n in spec:
        ARENA[name] = (off, n)
        off += n
    return off


ARENA_COLS = _arena_cols()

CBF_ID = 0
CBF_ONES = 128
CBF_OH32 = 256
CBF_OH16 = CBF_OH32 + 32 * 128
CBF_COLS = CBF_OH16 + 16 * 128
CF_ONES = 0
CF_CONSTS = {1e-5: 128, 1e-6: 129, 1.0: 130, 0.5: 131, 0.0: 132, -1.0: 133,
             1e30: 134, 0.125: 135, 1.0 / 32: 136, 1.0 / 1024: 137}
CF_IDENT = 144
CF_COLS = 272


def host_prepare(inputs):
    import ml_dtypes
    f32 = np.float32
    g = {k: np.asarray(v) for k, v in inputs.items()}

    shared = {}
    C = np.ascontiguousarray
    shared["Wpre"] = C(g["L0_attn_Wpre"].astype(f32))
    shared["Wq"] = C(g["L0_attn_Wq"].astype(f32))
    shared["Wk"] = C(g["L0_attn_Wk"].astype(f32))
    shared["Wv"] = C(g["L0_attn_Wv"].astype(f32))
    shared["Wproj"] = C(g["L0_attn_Wproj"].astype(f32))
    shared["gwT"] = C(g["L0_ffn_gate_w"].T.astype(f32))
    shared["guwT"] = C(g["L0_ffn_gate_up_w"].transpose(0, 2, 1).astype(f32))
    shared["dwT"] = C(g["L0_ffn_down_w"].transpose(0, 2, 1).astype(f32))
    shared["WkT1"] = C(g["L1_attn_Wk"].T.astype(f32))
    shared["WvT1"] = C(g["L1_attn_Wv"].T.astype(f32))
    shared["WrT1"] = C(g["L1_attn_Wr"].T.astype(f32))
    shared["WoT1"] = C(g["L1_attn_Wo"].T.astype(f32))
    shared["WfcT"] = C(g["L1_attn_Wfc"].T.astype(f32))
    shared["WmlpT"] = C(g["L1_attn_Wmlp"].T.astype(f32))
    shared["Wg"] = C(g["L1_ffn_Wg"].astype(f32))
    shared["W1"] = C(g["L1_ffn_W1"].astype(f32))
    shared["W11"] = C(g["L1_ffn_W11"].astype(f32))
    shared["W2"] = C(g["L1_ffn_W2"].astype(f32))
    shared["b2e"] = C(g["L1_ffn_b2"].astype(f32))
    shared["Ws1"] = C(g["L1_ffn_Ws1"].astype(f32))
    shared["Ws11"] = C(g["L1_ffn_Ws11"].astype(f32))
    shared["Ws2"] = C(g["L1_ffn_Ws2"].astype(f32))

    arena = np.zeros((128, ARENA_COLS), f32)

    def put(name, vec):
        off, n = ARENA[name]
        vec = np.asarray(vec, f32).reshape(-1)
        assert vec.size == n * 128, (name, vec.size, n)
        arena[:, off:off + n] = vec.reshape(n, 128).T
    put("bpre", g["L0_attn_bpre"])
    put("bproj", g["L0_attn_bproj"])
    arena[:, ARENA["pi"][0]] = np.repeat(np.asarray(g["L0_attn_pi"], f32), 128)
    put("bs1", g["L1_ffn_bs1"])
    put("bs11", g["L1_ffn_bs11"])
    put("bs2", g["L1_ffn_bs2"])
    put("ms", g["L1_attn_mixer_scale"])
    put("mls", g["L1_attn_mlp_scale"])
    put("td", g["L1_attn_time_decay"])
    put("tf", g["L1_attn_time_first"])
    put("b1", g["L1_ffn_b1"])
    put("b11", g["L1_ffn_b11"])
    shared["arena"] = arena

    bf16 = ml_dtypes.bfloat16
    cbf = np.zeros((128, CBF_COLS), bf16)
    cbf[:, CBF_ID:CBF_ID + 128] = np.eye(128, dtype=bf16)
    cbf[:, CBF_ONES:CBF_ONES + 128] = 1
    for e in range(32):
        cbf[e, CBF_OH32 + e * 128: CBF_OH32 + (e + 1) * 128] = 1
    for e in range(16):
        cbf[e, CBF_OH16 + e * 128: CBF_OH16 + (e + 1) * 128] = 1
    shared["cbf"] = cbf
    cf = np.ones((128, CF_COLS), f32)
    for val, col in CF_CONSTS.items():
        cf[:, col] = val
    cf[:, CF_IDENT:CF_IDENT + 128] = np.eye(128, dtype=f32)
    shared["cf32"] = cf

    xt = np.ascontiguousarray(g["L0_attn_x"].astype(f32).transpose(0, 2, 1))
    in_maps = []
    for c in range(8):
        m = dict(shared)
        m["x0"] = np.ascontiguousarray(
            xt[2 * c:2 * c + 2].reshape(2048, 2048))
        in_maps.append(m)
    return in_maps


def host_gather(results):
    out = np.empty((16, 2048, 1024), np.float32)
    for c, r in enumerate(results):
        o = np.asarray(r["out"], np.float32).reshape(2, 1024, 2048)
        out[2 * c:2 * c + 2] = o.transpose(0, 2, 1)
    return out



WEIGHT_SHAPES = {
    "Wpre": [D, D], "Wq": [D, D], "Wk": [D, D], "Wv": [D, D], "Wproj": [D, D],
    "gwT": [D, 32], "guwT": [32, D, D], "dwT": [32, 512, D],
    "WkT1": [D, D], "WvT1": [D, D], "WrT1": [D, D], "WoT1": [D, D],
    "WfcT": [D, 4 * D], "WmlpT": [4 * D, D],
    "Wg": [D, 16], "W1": [16, D, 2 * D], "W11": [16, D, 2 * D],
    "W2": [16, 2 * D, D], "b2e": [16, D],
    "Ws1": [D, 4 * D], "Ws11": [D, 4 * D], "Ws2": [4 * D, D],
}
SCRATCH = {"xnh": BF16, "xr": F32, "xih": F32, "xr2": F32}


class IO:

    def __init__(self, nc, phase_inputs, phase_outputs):
        self.nc = nc
        self.d = {}
        self.d["x0"] = nc.dram_tensor("x0", [2048, 2048], F32, kind="ExternalInput").ap()
        self.d["arena"] = nc.dram_tensor("arena", [128, ARENA_COLS], F32, kind="ExternalInput").ap()
        self.d["cbf"] = nc.dram_tensor("cbf", [128, CBF_COLS], BF16, kind="ExternalInput").ap()
        self.d["cf32"] = nc.dram_tensor("cf32", [128, CF_COLS], F32, kind="ExternalInput").ap()
        for w, shp in WEIGHT_SHAPES.items():
            self.d[w] = nc.dram_tensor(w, shp, F32, kind="ExternalInput").ap()
        self.d["out"] = nc.dram_tensor("out", [2048, 2048], F32, kind="ExternalOutput").ap()
        for sname, dt in SCRATCH.items():
            if sname in phase_outputs:
                kind = "ExternalOutput"
            elif sname in phase_inputs:
                kind = "ExternalInput"
            else:
                kind = "Internal"
            self.d[sname] = nc.dram_tensor(sname, [2048, 2048], dt, kind=kind).ap()
        kind = ("ExternalOutput" if "stats" in phase_outputs else
                "ExternalInput" if "stats" in phase_inputs else "Internal")
        self.d["stats"] = nc.dram_tensor("stats", [8, 2048], F32, kind=kind).ap()

        self.misc = nc.alloc_sbuf_tensor("misc_sb", [128, 32], F32)
        self.arena = nc.alloc_sbuf_tensor("arena_sb", [128, ARENA_COLS], F32)
        self.cbf = nc.alloc_sbuf_tensor("cbf_sb", [128, CBF_COLS], BF16)
        self.cf32 = nc.alloc_sbuf_tensor("cf32_sb", [128, CF_COLS], F32)
        self.pb = [nc.alloc_psum_tensor(f"pb{i}", [128, 512], F32) for i in range(7)]
        self.pbt = nc.alloc_psum_tensor("pbt", [128, 1024], BF16)
        for val, col in CF_CONSTS.items():
            nc.const_aps.register(F32, val, self.cf32[:, col:col + 1])
        self.ov_base = 33800
        self.ov_top = 229344
        self.ov_cur = self.ov_base
        self.ov_peak = self.ov_base
        assert nc.sbuf_base <= self.ov_base, nc.sbuf_base

    def ov_reset(self):
        self.ov_cur = self.ov_base

    def salloc(self, name, shape, dtype):
        esz = 2 if dtype == BF16 else 4
        nbytes = int(np.prod(shape[1:])) * esz
        addr = (self.ov_cur + 63) & ~63
        self.ov_cur = addr + nbytes
        self.ov_peak = max(self.ov_peak, self.ov_cur)
        assert self.ov_cur <= self.ov_top, (name, self.ov_cur)
        return self.nc.alloc_sbuf_tensor_at(name, shape, dtype, offset=addr)

    def load_consts(self, p):
        p.dma("sp", self.arena[:], self.d["arena"], writes=["arena"])
        p.dma("sp", self.cbf[:], self.d["cbf"], writes=["cbf"])
        p.dma("sp", self.cf32[:], self.d["cf32"], writes=["cf32"])
        p.run("dve", lambda v: v.tensor_scalar(
            self.misc[:, 0:1], self.acol("pi"), 1.0 / 32, None, op0=ALU.mult),
            reads=["arena"], writes=["misc"])

    def acol(self, name, j=0):
        off, n = ARENA[name]
        assert j < n
        return self.arena[:, off + j:off + j + 1]

    def ones_bf(self, p_lo, p_cnt, cols):
        return self.cbf[p_lo:p_lo + p_cnt, CBF_ONES:CBF_ONES + cols]

    def ones_f32(self, p_lo, p_cnt, cols):
        return self.cf32[p_lo:p_lo + p_cnt, CF_ONES:CF_ONES + cols]

    def ident(self):
        return self.cbf[:, CBF_ID:CBF_ID + 128]

    def ident_f32(self):
        return self.cf32[:, CF_IDENT:CF_IDENT + 128]

    def ov_mark(self):
        return self.ov_cur

    def ov_release(self, mark):
        self.ov_cur = mark



def emit_ln(p, io, src_name, layer):
    nc = io.nc
    src = io.d[src_name]
    io.ov_reset()
    xt = [io.salloc(f"ln{layer}_x{i}", [128, 512], F32) for i in range(16)]
    xbf = [io.salloc(f"ln{layer}_xb{i}", [128, 512], BF16) for i in range(8)]
    sq = [io.salloc(f"ln{layer}_sq{i}", [128, 512], BF16) for i in range(8)]
    rows = io.salloc(f"ln{layer}_rows", [1, 512 * 6], F32)
    bc = [io.salloc(f"ln{layer}_bc{i}", [128, 512], F32) for i in range(4)]
    xnt = [io.salloc(f"ln{layer}_xn{i}", [128, 512], BF16) for i in range(2)]

    for tb in range(8):
        seq, col = tb // 4, (tb % 4) * 512
        row0 = seq * 1024
        buf = tb % 2
        xts = xt[buf * 8:(buf + 1) * 8]
        psA, psB = io.pb[buf * 2], io.pb[buf * 2 + 1]
        kA, kB = f"pb{buf*2}", f"pb{buf*2+1}"
        for c in range(8):
            p.dma("sp", xts[c][:], src[row0 + c * 128: row0 + (c + 1) * 128,
                                       col:col + 512],
                  reads=[src_name], writes=[f"lnx{buf}{c}"])
        for c in range(8):
            p.run("pool", lambda g, c=c, xts=xts: g.tensor_copy(xbf[c][:], xts[c][:]),
                  reads=[f"lnx{buf}{c}"], writes=[f"lnxb{c}"])
            p.run("act", lambda s, c=c, xts=xts: s.activation(
                sq[c][:], xts[c][:], AF.Square),
                reads=[f"lnx{buf}{c}"], writes=[f"lnsq{c}"])
        mmacc(p, psA[0:1, :], [(io.ones_bf(0, 128, 1), xbf[c][:]) for c in range(8)],
              reads=[f"lnxb{c}" for c in range(8)] + ["cbf"], wkey=kA)
        mmacc(p, psB[0:1, :], [(io.ones_bf(0, 128, 1), sq[c][:]) for c in range(8)],
              reads=[f"lnsq{c}" for c in range(8)] + ["cbf"], wkey=kB)
        m_r = rows[:, 0:512]
        msq_r = rows[:, 512:1024]
        var_r = rows[:, 1024:1536]
        rstd_r = rows[:, 1536:2048]
        mr_r = rows[:, 2048:2560]
        p.run("dve", lambda v, psA=psA: v.tensor_scalar(m_r, psA[0:1, :], 1.0 / 1024,
                                                        None, op0=ALU.mult),
              reads=[kA], writes=["lnrows_m"])
        p.run("dve", lambda v, psB=psB: v.tensor_scalar(msq_r, psB[0:1, :], 1.0 / 1024,
                                                        None, op0=ALU.mult),
              reads=[kB], writes=["lnrows_msq"])
        p.run("dve", lambda v: v.scalar_tensor_tensor(
            var_r, m_r, -1.0, m_r, op0=ALU.mult, op1=ALU.mult),
            reads=["lnrows_m"], writes=["lnrows_var"])
        p.run("dve", lambda v: v.tensor_tensor(var_r, msq_r, var_r, op=ALU.add),
              reads=["lnrows_msq", "lnrows_var"], writes=["lnrows_var"])
        p.run("act", lambda s: s.activation(rstd_r, var_r, AF.Sqrt, bias=1e-5),
              reads=["lnrows_var"], writes=["lnrows_rstd"])
        p.run("dve", lambda v: v.reciprocal(rstd_r, rstd_r),
              reads=["lnrows_rstd"], writes=["lnrows_rstd"])
        p.run("dve", lambda v: v.tensor_tensor(mr_r, m_r, rstd_r, op=ALU.mult),
              reads=["lnrows_m", "lnrows_rstd"], writes=["lnrows_mr"])
        p.dma("act", io.d["stats"][layer * 4 + seq * 2: layer * 4 + seq * 2 + 1,
                                   col:col + 512], m_r,
              reads=["lnrows_m"], writes=["stats"])
        p.dma("act", io.d["stats"][layer * 4 + seq * 2 + 1: layer * 4 + seq * 2 + 2,
                                   col:col + 512], rstd_r,
              reads=["lnrows_rstd"], writes=["stats"])
        psC, psD = io.pb[4], io.pb[5]
        p.run("te", lambda t: t.matmul(psC[:, :], io.ones_f32(0, 1, 128), rstd_r,
                                       start=True, stop=True),
              reads=["lnrows_rstd", "cf32"], writes=["pb4"])
        p.run("te", lambda t: t.matmul(psD[:, :], io.ones_f32(0, 1, 128), mr_r,
                                       start=True, stop=True),
              reads=["lnrows_mr", "cf32"], writes=["pb5"])
        p.run("dve", lambda v: v.tensor_copy(bc[0][:], psC[:, :]),
              reads=["pb4"], writes=["lnbc0"])
        p.run("dve", lambda v: v.tensor_copy(bc[1][:], psD[:, :]),
              reads=["pb5"], writes=["lnbc1"])
        for c in range(8):
            xb = xnt[c % 2]
            kx = f"lnxnt{c%2}"
            p.run("dve", lambda v, c=c, xts=xts: v.tensor_tensor(
                bc[2][:], xts[c][:], bc[0][:], op=ALU.mult),
                reads=[f"lnx{buf}{c}", "lnbc0"], writes=["lnbc2"])
            p.run("dve", lambda v, xb=xb: v.tensor_tensor(
                xb[:], bc[2][:], bc[1][:], op=ALU.subtract),
                reads=["lnbc2", "lnbc1"], writes=[kx])
            p.dma("act", io.d["xnh"][row0 + c * 128: row0 + (c + 1) * 128,
                                     col:col + 512], xb[:],
                  reads=[kx], writes=["xnh"])
    p.barrier()



def _wslices(wbuf, k, m_lo, m_cnt):
    return wbuf[:, k * 1024 + m_lo: k * 1024 + m_lo + m_cnt]


def mmacc(p, out_ap, pairs, reads, wkey):
    n = len(pairs)
    fns = [lambda t, lh=lh, rh=rh, i=i: t.matmul(out_ap, lh, rh,
                                                 start=(i == 0), stop=(i == n - 1))
           for i, (lh, rh) in enumerate(pairs)]
    p.group("te", fns, reads=reads, writes=[wkey])


def emit_attn0(p, io, seq):
    nc = io.nc
    sq = f"a{seq}"
    row0 = seq * 1024
    io.ov_reset()
    pre = [io.salloc(f"{sq}_pre{m}", [128, S], BF16) for m in range(8)]
    kk = [io.salloc(f"{sq}_k{m}", [128, S], BF16) for m in range(8)]
    qq = [io.salloc(f"{sq}_q{m}", [128, S], BF16) for m in range(8)]
    vx = [io.salloc(f"{sq}_vx{t}", [128, 1040], BF16) for t in range(16)]
    negB = io.salloc(f"{sq}_negB", [128, 16], F32)

    mark = io.ov_mark()
    wA = io.salloc(f"{sq}_wA", [128, 8192], BF16)
    wB = io.salloc(f"{sq}_wB", [128, 8192], BF16)
    xnsl = [io.salloc(f"{sq}_xnsl{i}", [128, 8 * 512], BF16) for i in range(2)]
    sqt = io.salloc(f"{sq}_sqt", [128, S], BF16)
    maxcol = io.salloc(f"{sq}_maxcol", [128, 1], F32)
    rows = io.salloc(f"{sq}_rows", [1, 512], F32)
    rowbf = io.salloc(f"{sq}_rowbf", [1, 16], BF16)

    def ldw(buf, key, wname):
        src = io.d[wname].rearrange("(c p) m -> p c m", p=128)
        dst = buf[:].rearrange("p (c m) -> p c m", m=1024)
        p.dma("pool", dst, src, reads=[wname], writes=[key])

    def load_xnsl(nq):
        b = nq % 2
        src = io.d["xnh"][row0:row0 + 1024,
                          nq * 512:(nq + 1) * 512].rearrange("(c p) n -> p c n", p=128)
        dst = xnsl[b][:].rearrange("p (c n) -> p c n", n=512)
        p.dma("sp", dst, src, reads=["xnh"], writes=[f"{sq}_xnsl{b}"])
        return xnsl[b], f"{sq}_xnsl{b}"

    ldw(wA, f"{sq}_wA", "Wpre")
    ldw(wB, f"{sq}_wB", "Wk")
    for nq in range(4):
        xs, xk = load_xnsl(nq)
        for m in range(8):
            ps, kps = io.pb[m % 2], f"pb{m%2}"
            mmacc(p, ps[:, 0:512],
                  [(_wslices(wA, k, m * 128, 128), xs[:, k * 512:(k + 1) * 512])
                   for k in range(8)],
                  reads=[f"{sq}_wA", xk], wkey=kps)
            p.run("act", lambda s, ps=ps, m=m, nq=nq: s.activation(
                pre[m][:, nq * 512:(nq + 1) * 512], ps[:, 0:512], AF.Silu,
                bias=io.acol("bpre", m)),
                reads=[kps, "arena"], writes=[f"{sq}_pre{m}"])
        for m in range(8):
            ps, kps = io.pb[2 + m % 2], f"pb{2+m%2}"
            mmacc(p, ps[:, 0:512],
                  [(_wslices(wB, k, m * 128, 128), xs[:, k * 512:(k + 1) * 512])
                   for k in range(8)],
                  reads=[f"{sq}_wB", xk], wkey=kps)
            p.run("dve", lambda v, ps=ps, m=m, nq=nq: v.tensor_copy(
                kk[m][:, nq * 512:(nq + 1) * 512], ps[:, 0:512]),
                reads=[kps], writes=[f"{sq}_k{m}"])

    ldw(wA, f"{sq}_wA", "Wv")
    ldw(wB, f"{sq}_wB", "Wq")
    for t in range(16):
        p.run("dve", lambda v, t=t: v.memset(
            vx[t][:].rearrange("p (h x) -> p h x", x=65)[:, :, 64:65], 1.0),
            writes=[f"{sq}_vx{t}"])
    for nq in range(4):
        xs, xk = load_xnsl(nq)
        for tb in range(4):
            mc = nq * 4 + tb
            for half in range(2):
                ps, kps = io.pb[4 + half], f"pb{4+half}"
                mmacc(p, ps[:, 0:512],
                      [(xs[:, k * 512 + tb * 128: k * 512 + tb * 128 + 128],
                        _wslices(wA, k, half * 512, 512)) for k in range(8)],
                      reads=[f"{sq}_wA", xk], wkey=kps)
                p.run("dve", lambda v, ps=ps, mc=mc, half=half: v.tensor_copy(
                    vx[mc][:, half * 520:half * 520 + 520].rearrange(
                        "p (h x) -> p h x", x=65)[:, :, 0:64],
                    ps[:, 0:512].rearrange("p (h x) -> p h x", x=64)),
                    reads=[kps], writes=[f"{sq}_vx{mc}"])
        for m in range(8):
            ps, kps = io.pb[m % 2], f"pb{m%2}"
            mmacc(p, ps[:, 0:512],
                  [(_wslices(wB, k, m * 128, 128), xs[:, k * 512:(k + 1) * 512])
                   for k in range(8)],
                  reads=[f"{sq}_wB", xk], wkey=kps)
            p.run("dve", lambda v, ps=ps, m=m, nq=nq: v.tensor_scalar(
                qq[m][:, nq * 512:(nq + 1) * 512], ps[:, 0:512], 0.125, None,
                op0=ALU.mult),
                reads=[kps], writes=[f"{sq}_q{m}"])

    p.run("dve", lambda v: v.memset(rows[:, 0:64], -1e30), writes=[f"{sq}_rows"])
    for (src_t, kind) in ((qq, "q"), (kk, "k"), (pre, "p")):
        for c in range(8):
            skey = f"{sq}_pre{c}" if kind == "p" else (
                f"{sq}_q{c}" if kind == "q" else f"{sq}_k{c}")
            p.run("dve", lambda v, c=c, src_t=src_t: v.tensor_tensor(
                sqt[:], src_t[c][:], src_t[c][:], op=ALU.mult),
                reads=[skey], writes=[f"{sq}_sqt"])
            if kind == "p":
                for nq in range(4):
                    p.run("te", lambda t, c=c, nq=nq: t.matmul(
                        io.pb[nq][0:1, 0:512], io.cbf[:, CBF_ONES:CBF_ONES + 1],
                        sqt[:, nq * 512:(nq + 1) * 512],
                        start=(c == 0), stop=(c == 7)),
                        reads=[f"{sq}_sqt", "cbf"], writes=[f"pb{nq}"])
            else:
                o = 0 if kind == "q" else 16
                for hh in range(2):
                    h = 2 * c + hh
                    for nq in range(4):
                        pbk = 4 + (nq % 2)
                        p.run("te", lambda t, hh=hh, nq=nq, pbk=pbk: t.matmul(
                            io.pb[pbk][0:1, 0:512],
                            io.cbf[64 * hh:64 * hh + 64, CBF_ONES:CBF_ONES + 1],
                            sqt[64 * hh:64 * hh + 64, nq * 512:(nq + 1) * 512],
                            start=True, stop=True),
                            reads=[f"{sq}_sqt", "cbf"], writes=[f"pb{pbk}"])
                        p.run("dve", lambda v, pbk=pbk: v.tensor_reduce(
                            rows[:, 63:64], io.pb[pbk][0:1, 0:512], axis=AX.X,
                            op=ALU.max),
                            reads=[f"pb{pbk}"], writes=[f"{sq}_rtmp"])
                        p.run("dve", lambda v, o=o, h=h: v.tensor_tensor(
                            rows[:, o + h:o + h + 1], rows[:, o + h:o + h + 1],
                            rows[:, 63:64], op=ALU.max),
                            reads=[f"{sq}_rtmp", f"{sq}_rows"], writes=[f"{sq}_rows"])
    for nq in range(4):
        p.run("dve", lambda v, nq=nq: v.tensor_reduce(
            rows[:, 48 + nq:49 + nq], io.pb[nq][0:1, 0:512], axis=AX.X, op=ALU.max),
            reads=[f"pb{nq}"], writes=[f"{sq}_rows"])
    p.run("dve", lambda v: v.tensor_reduce(rows[:, 56:57], rows[:, 48:52],
                                           axis=AX.X, op=ALU.max),
          reads=[f"{sq}_rows"], writes=[f"{sq}_rows"])
    p.run("dve", lambda v: v.tensor_tensor(rows[:, 32:48], rows[:, 0:16],
                                           rows[:, 16:32], op=ALU.mult),
          reads=[f"{sq}_rows"], writes=[f"{sq}_rows"])
    p.run("act", lambda s: s.activation(rows[:, 32:48], rows[:, 32:48], AF.Sqrt),
          reads=[f"{sq}_rows"], writes=[f"{sq}_rows"])
    p.run("dve", lambda v: v.tensor_tensor(rows[:, 57:58], rows[:, 56:57],
                                           io.misc[0:1, 0:1], op=ALU.mult),
          reads=[f"{sq}_rows", "misc"], writes=[f"{sq}_rows"])
    p.run("dve", lambda v: v.tensor_scalar(rows[:, 64:80], rows[:, 32:48],
                                           rows[:, 57:58], -1.0, op0=ALU.add,
                                           op1=ALU.mult),
          reads=[f"{sq}_rows"], writes=[f"{sq}_rows"])
    p.run("dve", lambda v: v.tensor_copy(rowbf[:], rows[:, 64:80]),
          reads=[f"{sq}_rows"], writes=[f"{sq}_rowbf"])
    p.run("te", lambda t: t.matmul(io.pb[6][:, 0:16], io.cbf[0:1, CBF_ONES:CBF_ONES + 128],
                                   rowbf[:], start=True, stop=True),
          reads=[f"{sq}_rowbf", "cbf"], writes=["pb6"])
    p.run("dve", lambda v: v.tensor_copy(negB[:], io.pb[6][:, 0:16]),
          reads=["pb6"], writes=[f"{sq}_negB"])

    p.barrier()
    io.ov_release(mark)

    wP = io.salloc(f"{sq}_wP", [128, 8192], BF16)
    Lp = [io.salloc(f"{sq}_L{j}", [128, 512], F32) for j in range(8)]
    ub = [io.salloc(f"{sq}_u{i}", [128, 512], F32) for i in range(4)]
    exb = [io.salloc(f"{sq}_ex{i}", [128, 512], BF16) for i in range(4)]
    acat = [io.salloc(f"{sq}_ac{c}", [128, 256], BF16) for c in range(8)]
    rbt = io.salloc(f"{sq}_rb", [128, 256], F32)
    rrow = io.salloc(f"{sq}_rrow", [1, 256], F32)
    oprb = io.salloc(f"{sq}_oprb", [128, 256], F32)
    mrb = io.salloc(f"{sq}_mrb", [128, 256], F32)
    x0t = [io.salloc(f"{sq}_x0t{i}", [128, 256], F32) for i in range(2)]
    ptt = [io.salloc(f"{sq}_pt{i}", [128, 256], F32) for i in range(1)]
    t1b = [io.salloc(f"{sq}_t1{i}", [128, 256], F32) for i in range(1)]
    onb = io.salloc(f"{sq}_onb", [128, 512], F32)
    srows = io.salloc(f"{sq}_srows", [1, 768], F32)

    ldw(wP, f"{sq}_wP", "Wproj")
    st_base = 0 * 4 + seq * 2
    for nb in range(8):
        col0 = nb * 256
        p.dma("sp", srows[:, 0:256], io.d["stats"][st_base:st_base + 1, col0:col0 + 256],
              reads=["stats"], writes=[f"{sq}_srA"])
        p.dma("sp", srows[:, 256:512], io.d["stats"][st_base + 1:st_base + 2,
                                                     col0:col0 + 256],
              reads=["stats"], writes=[f"{sq}_srB"])
        p.run("dve", lambda v: v.tensor_scalar(srows[:, 512:768], srows[:, 256:512],
                                               1.0, None, op0=ALU.add),
              reads=[f"{sq}_srB"], writes=[f"{sq}_srC"])
        p.run("dve", lambda v: v.tensor_tensor(srows[:, 0:256], srows[:, 0:256],
                                               srows[:, 256:512], op=ALU.mult),
              reads=[f"{sq}_srA", f"{sq}_srB"], writes=[f"{sq}_srA"])
        p.run("te", lambda t: t.matmul(io.pb[6][:, 0:256], io.ones_f32(0, 1, 128),
                                       srows[:, 512:768], start=True, stop=True),
              reads=[f"{sq}_srC", "cf32"], writes=["pb6"])
        p.run("dve", lambda v: v.tensor_copy(oprb[:], io.pb[6][:, 0:256]),
              reads=["pb6"], writes=[f"{sq}_oprb"])
        p.run("te", lambda t: t.matmul(io.pb[6][:, 256:512], io.ones_f32(0, 1, 128),
                                       srows[:, 0:256], start=True, stop=True),
              reads=[f"{sq}_srA", "cf32"], writes=["pb6"])
        p.run("dve", lambda v: v.tensor_copy(mrb[:], io.pb[6][:, 256:512]),
              reads=["pb6"], writes=[f"{sq}_mrb"])
        for mc in range(16):
            ps, kps = io.pb[mc % 2], f"pb{mc%2}"
            mmacc(p, ps[:, 0:256],
                  [(pre[c][:, mc * 128:(mc + 1) * 128], pre[c][:, col0:col0 + 256])
                   for c in range(8)],
                  reads=[f"{sq}_pre{c}" for c in range(8)], wkey=kps)
            p.run("dve", lambda v, ps=ps, mc=mc: v.tensor_scalar(
                Lp[mc // 2][:, (mc % 2) * 256:(mc % 2) * 256 + 256], ps[:, 0:256],
                io.misc[:, 0:1], None, op0=ALU.mult),
                reads=[kps, "misc"], writes=[f"{sq}_Lp{mc//2}"])
        def _finish_head(h):
            c, off = h // 2, 64 * (h % 2)
            psO, kO = io.pb[4 + h % 2], f"pb{4+h%2}"
            p.run("dve", lambda v, psO=psO: v.reciprocal(rrow[:], psO[64:65, 0:256]),
                  reads=[kO], writes=[f"{sq}_rrow"])
            p.run("te", lambda t: t.matmul(io.pb[6][0:64, 0:256],
                                           io.ones_f32(0, 1, 64), rrow[:],
                                           start=True, stop=True),
                  reads=[f"{sq}_rrow", "cf32"], writes=["pb6"])
            p.run("dve", lambda v: v.tensor_copy(rbt[0:64, :], io.pb[6][0:64, 0:256]),
                  reads=["pb6"], writes=[f"{sq}_rbt"])
            p.run("dve", lambda v, psO=psO, c=c, off=off: v.tensor_tensor(
                acat[c][off:off + 64, :], psO[0:64, 0:256], rbt[0:64, :], op=ALU.mult),
                reads=[kO, f"{sq}_rbt"], writes=[f"{sq}_ac{c}"])

        def _av(i):
            h, j = divmod(i, 8)
            psO, kO = io.pb[4 + h % 2], f"pb{4+h%2}"
            p.group("te", [
                lambda t, psO=psO, mc=2 * j + half, half=half, i=i, h=h: t.matmul(
                    psO[0:65, 0:256], vx[mc][:, 65 * h:65 * h + 65],
                    exb[i % 4][:, half * 256:half * 256 + 256],
                    start=(mc == 0), stop=(mc == 15))
                for half in range(2)],
                reads=[f"{sq}_vx{2*j}", f"{sq}_vx{2*j+1}", f"{sq}_ex{i%4}"],
                writes=[kO])
            if j == 7:
                _finish_head(h)

        for i in range(16 * 8):
            h, j = divmod(i, 8)
            c, off = h // 2, 64 * (h % 2)
            psS, kS = io.pb[i % 4], f"pb{i%4}"
            p.group("te", [
                lambda t, psS=psS, c=c, off=off, mc=2 * j + half, half=half,
                col0=col0: t.matmul(
                    psS[:, half * 256:half * 256 + 256],
                    kk[c][off:off + 64, mc * 128:(mc + 1) * 128],
                    qq[c][off:off + 64, col0:col0 + 256], start=True, stop=True)
                for half in range(2)],
                reads=[f"{sq}_k{c}", f"{sq}_q{c}"], writes=[kS])
            p.run("dve", lambda v, psS=psS, i=i, j=j: v.tensor_tensor(
                ub[i % 4][:], psS[:, 0:512], Lp[j][:], op=ALU.add),
                reads=[kS, f"{sq}_Lp{j}"], writes=[f"{sq}_u{i%4}"])
            p.run("act", lambda s, i=i, h=h: s.activation(
                exb[i % 4][:], ub[i % 4][:], AF.Exp, bias=negB[:, h:h + 1]),
                reads=[f"{sq}_u{i%4}", f"{sq}_negB"], writes=[f"{sq}_ex{i%4}"])
            if i >= 3:
                _av(i - 3)
        for i in range(16 * 8 - 3, 16 * 8):
            _av(i)
        for m in range(8):
            ps, kps = io.pb[2 + m % 2], f"pb{2+m%2}"
            mmacc(p, ps[:, 0:256],
                  [(_wslices(wP, k, m * 128, 128), acat[k][:]) for k in range(8)],
                  reads=[f"{sq}_wP"] + [f"{sq}_ac{k}" for k in range(8)], wkey=kps)
            p.dma("sp", x0t[m % 2][:], io.d["x0"][row0 + m * 128: row0 + (m + 1) * 128,
                                                  col0:col0 + 256],
                  reads=["x0"], writes=[f"{sq}_x0t{m%2}"])
            p.run("pool", lambda g, m=m: g.tensor_tensor(
                ptt[0][:], x0t[m % 2][:], oprb[:], op=ALU.mult),
                reads=[f"{sq}_x0t{m%2}", f"{sq}_oprb"], writes=[f"{sq}_pt0"])
            p.run("dve", lambda v, ps=ps, m=m: v.scalar_tensor_tensor(
                t1b[0][:], ps[:, 0:256], io.acol("bproj", m), mrb[:],
                op0=ALU.add, op1=ALU.subtract),
                reads=[kps, "arena", f"{sq}_mrb"], writes=[f"{sq}_t10"])
            p.run("dve", lambda v, m=m: v.tensor_tensor(
                onb[:, (m % 2) * 256:(m % 2) * 256 + 256], t1b[0][:],
                ptt[0][:], op=ALU.add),
                reads=[f"{sq}_t10", f"{sq}_pt0"], writes=[f"{sq}_onb"])
            if m % 2 == 1:
                g = m // 2
                dst = io.d["xr"][row0 + g * 256:row0 + g * 256 + 256,
                                 col0:col0 + 256].rearrange("(m p) n -> p m n", p=128)
                p.dma("act", dst, onb[:].rearrange("p (m n) -> p m n", n=256),
                      reads=[f"{sq}_onb"], writes=["xr"])
    p.barrier()



def emit_moe0(p, io):
    nc = io.nc
    io.ov_reset()
    xnt = [io.salloc(f"m2_xn{c}", [128, S], BF16) for c in range(8)]
    acc = [io.salloc(f"m2_acc{c}", [128, S], F32) for c in range(8)]
    combT = io.salloc("m2_combT", [32, T], BF16)
    wg = io.salloc("m2_wg", [128, 8 * 32], BF16)
    guw = [io.salloc(f"m2_guw{i}", [128, 8192], BF16) for i in range(2)]
    dw = [io.salloc(f"m2_dw{i}", [128, 4096], BF16) for i in range(2)]
    hgt = [io.salloc(f"m2_hg{i}", [128, 512], BF16) for i in range(8)]
    sgt = [io.salloc(f"m2_sg{i}", [128, 512], BF16) for i in range(2)]
    htt = [io.salloc(f"m2_h{i}", [128, 512], BF16) for i in range(2)]
    gb = [io.salloc(f"m2_gb{i}", [128, 512], BF16) for i in range(2)]
    probs = io.salloc("m2_probs", [128, 32], F32)
    cur = io.salloc("m2_cur", [128, 32], F32)
    selm = io.salloc("m2_selm", [128, 32], F32)
    eqt = io.salloc("m2_eq", [128, 32], F32)
    tmp32 = io.salloc("m2_tmp", [128, 32], F32)
    combf = io.salloc("m2_combf", [128, 32], F32)
    combbf = io.salloc("m2_combbf", [128, 32], BF16)
    cols = io.salloc("m2_cols", [128, 8], F32)

    p.dma("pool", wg[:].rearrange("p (c m) -> p c m", m=32),
          io.d["gwT"].rearrange("(c p) m -> p c m", p=128),
          reads=["gwT"], writes=["m2_wg"])

    for sb in range(2):
        row0 = sb * 1024
        tok0 = sb * S
        for c in range(8):
            p.dma("sp", xnt[c][:], io.d["xnh"][row0 + c * 128: row0 + (c + 1) * 128, :],
                  reads=["xnh"], writes=[f"m2_xn{c}"])
        for c in range(8):
            p.dma("act", acc[c][:], io.d["xr"][row0 + c * 128: row0 + (c + 1) * 128, :],
                  reads=["xr"], writes=[f"m2_acc{c}"])
        def _gate_tt(tt):
            ps, kps = io.pb[6], "pb6"
            mmacc(p, ps[:, 0:32],
                  [(xnt[c][:, tt * 128:(tt + 1) * 128], wg[:, c * 32:(c + 1) * 32])
                   for c in range(8)],
                  reads=[f"m2_xn{c}" for c in range(8)] + ["m2_wg"], wkey=kps)
            p.run("dve", lambda v, ps=ps: v.reduce_max(cols[:, 0:1], ps[:, 0:32],
                                                       axis=AX.X, negate=True),
                  reads=[kps], writes=["m2_cols"])
            p.run("act", lambda s, ps=ps: s.activation(probs[:], ps[:, 0:32], AF.Exp,
                                                       bias=cols[:, 0:1]),
                  reads=[kps, "m2_cols"], writes=["m2_probs"])
            p.run("dve", lambda v: v.tensor_reduce(cols[:, 2:3], probs[:], axis=AX.X,
                                                   op=ALU.add),
                  reads=["m2_probs"], writes=["m2_cols"])
            p.run("dve", lambda v: v.reciprocal(cols[:, 3:4], cols[:, 2:3]),
                  reads=["m2_cols"], writes=["m2_cols"])
            p.run("dve", lambda v: v.tensor_scalar(probs[:], probs[:], cols[:, 3:4],
                                                   None, op0=ALU.mult),
                  reads=["m2_probs", "m2_cols"], writes=["m2_probs"])
            p.run("dve", lambda v: v.tensor_copy(cur[:], probs[:]),
                  reads=["m2_probs"], writes=["m2_cur"])
            p.run("dve", lambda v: v.memset(selm[:], 0.0), writes=["m2_selm"])
            p.run("dve", lambda v: v.memset(cols[:, 4:5], 0.0), writes=["m2_cols"])
            for i in range(8):
                p.run("dve", lambda v: v.reduce_max(cols[:, 5:6], cur[:], axis=AX.X),
                      reads=["m2_cur"], writes=["m2_cols"])
                p.run("dve", lambda v: v.tensor_tensor(cols[:, 4:5], cols[:, 4:5],
                                                       cols[:, 5:6], op=ALU.add),
                      reads=["m2_cols"], writes=["m2_cols"])
                p.run("dve", lambda v: v.tensor_scalar(eqt[:], cur[:], cols[:, 5:6],
                                                       None, op0=ALU.is_equal),
                      reads=["m2_cur", "m2_cols"], writes=["m2_eq"])
                p.run("dve", lambda v: v.tensor_tensor(selm[:], selm[:], eqt[:],
                                                       op=ALU.add),
                      reads=["m2_selm", "m2_eq"], writes=["m2_selm"])
                p.run("dve", lambda v: v.scalar_tensor_tensor(
                    tmp32[:], eqt[:], cols[:, 5:6], eqt[:], op0=ALU.mult, op1=ALU.add),
                    reads=["m2_eq", "m2_cols"], writes=["m2_tmp"])
                p.run("dve", lambda v: v.tensor_tensor(cur[:], cur[:], tmp32[:],
                                                       op=ALU.subtract),
                      reads=["m2_cur", "m2_tmp"], writes=["m2_cur"])
            p.run("dve", lambda v: v.reciprocal(cols[:, 6:7], cols[:, 4:5]),
                  reads=["m2_cols"], writes=["m2_cols"])
            p.run("dve", lambda v: v.tensor_tensor(combf[:], selm[:], probs[:],
                                                   op=ALU.mult),
                  reads=["m2_selm", "m2_probs"], writes=["m2_combf"])
            p.run("dve", lambda v: v.tensor_scalar(combbf[:], combf[:], cols[:, 6:7],
                                                   None, op0=ALU.mult),
                  reads=["m2_combf", "m2_cols"], writes=["m2_combbf"])
            p.run("te", lambda t: t.transpose(io.pbt[0:32, 0:128], combbf[:],
                                              io.ident()),
                  reads=["m2_combbf", "cbf"], writes=["pbt"])
            p.run("dve", lambda v, tt=tt, tok0=tok0: v.tensor_copy(
                combT[:, tok0 + tt * 128: tok0 + (tt + 1) * 128], io.pbt[0:32, 0:128]),
                reads=["pbt"], writes=[f"m2_combT{tt}"])
        for e in range(32):
            b = e % 2
            p.dma("pool", guw[b][:].rearrange("p (c m) -> p c m", m=1024),
                  io.d["guwT"][e].rearrange("(c p) m -> p c m", p=128),
                  reads=["guwT"], writes=[f"m2_guw{b}"])
            p.dma("pool", dw[b][:].rearrange("p (c m) -> p c m", m=1024),
                  io.d["dwT"][e].rearrange("(c p) m -> p c m", p=128),
                  reads=["dwT"], writes=[f"m2_dw{b}"])
            def _emit_y(nb, b=b, col0_of=lambda nb: nb * 512):
                col0 = col0_of(nb)
                for out in range(8):
                    psY, kY = io.pb[4 + out % 2], f"pb{4+out%2}"
                    hks = [4 * (nb % 2) + k for k in range(4)]
                    mmacc(p, psY[:, 0:512],
                          [(_wslices(dw[b], k, out * 128, 128), hgt[4 * (nb % 2) + k][:])
                           for k in range(4)],
                          reads=[f"m2_dw{b}"] + [f"m2_hg{k}" for k in hks],
                          wkey=kY)
                    p.run("dve", lambda v, psY=psY, out=out, col0=col0: v.tensor_tensor(
                        acc[out][:, col0:col0 + 512], psY[:, 0:512],
                        acc[out][:, col0:col0 + 512], op=ALU.add),
                        reads=[kY, f"m2_acc{out}"], writes=[f"m2_acc{out}"])

            for nb in range(4):
                col0 = nb * 512
                if e == 0:
                    for tt in range(4 * nb, 4 * nb + 4):
                        _gate_tt(tt)
                p.run("te", lambda t, e=e, tok0=tok0, col0=col0: t.matmul(
                    io.pb[6][:, 0:512],
                    io.cbf[0:32, CBF_OH32 + e * 128: CBF_OH32 + (e + 1) * 128],
                    combT[0:32, tok0 + col0: tok0 + col0 + 512],
                    start=True, stop=True),
                    reads=[f"m2_combT{tt}" for tt in range(4 * nb, 4 * nb + 4)] + ["cbf"],
                    writes=["pb6"])
                p.run("dve", lambda v, nb=nb: v.tensor_copy(gb[nb % 2][:],
                                                            io.pb[6][:, 0:512]),
                      reads=["pb6"], writes=[f"m2_gb{nb%2}"])
                xnks = [f"m2_xn{k}" for k in range(8)]
                for mp in range(4):
                    psG, kG = io.pb[mp % 2], f"pb{mp%2}"
                    psU, kU = io.pb[2 + mp % 2], f"pb{2+mp%2}"
                    mmacc(p, psG[:, 0:512],
                          [(_wslices(guw[b], k, mp * 128, 128),
                            xnt[k][:, col0:col0 + 512]) for k in range(8)],
                          reads=[f"m2_guw{b}"] + xnks, wkey=kG)
                    mmacc(p, psU[:, 0:512],
                          [(_wslices(guw[b], k, (mp + 4) * 128, 128),
                            xnt[k][:, col0:col0 + 512]) for k in range(8)],
                          reads=[f"m2_guw{b}"] + xnks, wkey=kU)
                    hj = 4 * (nb % 2) + mp
                    p.run("act", lambda s, psG=psG, mp=mp: s.activation(
                        sgt[mp % 2][:], psG[:, 0:512], AF.Silu),
                        reads=[kG], writes=[f"m2_sg{mp%2}"])
                    p.run("dve", lambda v, psU=psU, mp=mp: v.tensor_tensor(
                        htt[mp % 2][:], psU[:, 0:512], sgt[mp % 2][:], op=ALU.mult),
                        reads=[kU, f"m2_sg{mp%2}"], writes=[f"m2_h{mp%2}"])
                    p.run("pool", lambda g, mp=mp, nb=nb, hj=hj: g.tensor_tensor(
                        hgt[hj][:], htt[mp % 2][:], gb[nb % 2][:], op=ALU.mult),
                        reads=[f"m2_h{mp%2}", f"m2_gb{nb%2}"], writes=[f"m2_hg{hj}"])
                if nb > 0:
                    _emit_y(nb - 1)
            _emit_y(3)
        for c in range(8):
            p.dma("act", io.d["xr"][row0 + c * 128: row0 + (c + 1) * 128, :], acc[c][:],
                  reads=[f"m2_acc{c}"], writes=["xr"])
    p.barrier()



def _ldcol(p, io, buf, key, wname, col_lo, col_cnt, rows=1024, e=None):
    ap = io.d[wname] if e is None else io.d[wname][e]
    src = ap[:, col_lo:col_lo + col_cnt].rearrange("(c p) m -> p c m", p=128)
    dst = buf[:].rearrange("p (c m) -> p c m", m=col_cnt)[:, 0:rows // 128, :]
    p.dma("pool", dst, src, reads=[wname], writes=[key])


def emit_mixer1(p, io, seq):
    nc = io.nc
    sq = f"x{seq}"
    row0 = seq * 1024
    io.ov_reset()
    normed = [io.salloc(f"{sq}_nm{c}", [128, S], BF16) for c in range(8)]
    rwkv = [io.salloc(f"{sq}_rw{c}", [128, S], BF16) for c in range(8)]

    for j in range(8):
        p.run("act", lambda s, j=j: s.activation(io.misc[:, 8 + j:9 + j],
                                                 io.acol("td", j), AF.Exp),
              reads=["arena"], writes=["misc_ew"])
        p.run("act", lambda s, j=j: s.activation(io.misc[:, 8 + j:9 + j],
                                                 io.misc[:, 8 + j:9 + j], AF.Exp,
                                                 scale=-1.0),
              reads=["misc_ew"], writes=["misc_ew"])
        p.run("act", lambda s, j=j: s.activation(io.misc[:, 16 + j:17 + j],
                                                 io.acol("tf", j), AF.Exp),
              reads=["arena"], writes=["misc_eu"])

    mark = io.ov_mark()
    xnt = [io.salloc(f"{sq}_xt{c}", [128, S], BF16) for c in range(8)]
    sqt = io.salloc(f"{sq}_sqt", [128, 512], BF16)
    rrows = io.salloc(f"{sq}_rr", [1, 512], F32)
    rbt = io.salloc(f"{sq}_rbt", [128, 512], F32)
    for c in range(8):
        p.dma("sp", xnt[c][:], io.d["xnh"][row0 + c * 128: row0 + (c + 1) * 128, :],
              reads=["xnh"], writes=[f"{sq}_xt{c}"])
    for nq in range(4):
        ps, kps = io.pb[nq % 2], f"pb{nq%2}"
        for c in range(8):
            p.run("dve", lambda v, c=c, nq=nq: v.tensor_tensor(
                sqt[:], xnt[c][:, nq * 512:(nq + 1) * 512], xnt[c][:, nq * 512:(nq + 1) * 512],
                op=ALU.mult),
                reads=[f"{sq}_xt{c}"], writes=[f"{sq}_sqt"])
            p.run("te", lambda t, ps=ps, c=c: t.matmul(
                ps[0:1, 0:512], io.cbf[:, CBF_ONES:CBF_ONES + 1], sqt[:],
                start=(c == 0), stop=(c == 7)),
                reads=[f"{sq}_sqt", "cbf"], writes=[kps])
        p.run("dve", lambda v, ps=ps: v.tensor_scalar(rrows[:], ps[0:1, 0:512],
                                                      1.0 / 1024, None, op0=ALU.mult),
              reads=[kps], writes=[f"{sq}_rr"])
        p.run("act", lambda s: s.activation(rrows[:], rrows[:], AF.Sqrt, bias=1e-6),
              reads=[f"{sq}_rr"], writes=[f"{sq}_rr"])
        p.run("dve", lambda v: v.reciprocal(rrows[:], rrows[:]),
              reads=[f"{sq}_rr"], writes=[f"{sq}_rr"])
        p.run("te", lambda t: t.matmul(io.pb[6][:, 0:512], io.ones_f32(0, 1, 128),
                                       rrows[:], start=True, stop=True),
              reads=[f"{sq}_rr", "cf32"], writes=["pb6"])
        p.run("dve", lambda v: v.tensor_copy(rbt[:], io.pb[6][:, 0:512]),
              reads=["pb6"], writes=[f"{sq}_rbt"])
        for c in range(8):
            p.run("dve", lambda v, c=c, nq=nq: v.tensor_tensor(
                normed[c][:, nq * 512:(nq + 1) * 512],
                xnt[c][:, nq * 512:(nq + 1) * 512], rbt[:], op=ALU.mult),
                reads=[f"{sq}_xt{c}", f"{sq}_rbt"], writes=[f"{sq}_nm{c}"])
    p.barrier()
    io.ov_release(mark)

    wk = [io.salloc(f"{sq}_wk{i}", [128, 1024], BF16) for i in range(2)]
    wv = [io.salloc(f"{sq}_wv{i}", [128, 1024], BF16) for i in range(2)]
    wr = [io.salloc(f"{sq}_wr{i}", [128, 1024], BF16) for i in range(2)]
    ksb = io.salloc(f"{sq}_ksb", [128, S], F32)
    kvb = io.salloc(f"{sq}_kv", [128, S], F32)
    rsb = io.salloc(f"{sq}_rsb", [128, S], BF16)
    ewt = io.salloc(f"{sq}_ewt", [128, S], F32)
    scan = io.salloc(f"{sq}_scan", [128, S + 1], F32)
    _ldcol(p, io, wk[0], f"{sq}_wk0", "WkT1", 0, 128)
    _ldcol(p, io, wv[0], f"{sq}_wv0", "WvT1", 0, 128)
    _ldcol(p, io, wr[0], f"{sq}_wr0", "WrT1", 0, 128)
    for oc in range(8):
        b = oc % 2
        if oc < 7:
            nb_ = (oc + 1) % 2
            _ldcol(p, io, wk[nb_], f"{sq}_wk{nb_}", "WkT1", (oc + 1) * 128, 128)
            _ldcol(p, io, wv[nb_], f"{sq}_wv{nb_}", "WvT1", (oc + 1) * 128, 128)
            _ldcol(p, io, wr[nb_], f"{sq}_wr{nb_}", "WrT1", (oc + 1) * 128, 128)
        for nq in range(4):
            psK, kK = io.pb[0], "pb0"
            psV, kV = io.pb[1], "pb1"
            psR, kR = io.pb[2], "pb2"
            mmacc(p, psK[:, 0:512],
                  [(wk[b][:, k * 128:(k + 1) * 128],
                    normed[k][:, nq * 512:(nq + 1) * 512]) for k in range(8)],
                  reads=[f"{sq}_wk{b}"] + [f"{sq}_nm{k}" for k in range(8)], wkey=kK)
            p.run("act", lambda s, psK=psK, nq=nq: s.activation(
                ksb[:, nq * 512:(nq + 1) * 512], psK[:, 0:512], AF.Copy),
                reads=[kK], writes=[f"{sq}_ksb"])
            mmacc(p, psV[:, 0:512],
                  [(wv[b][:, k * 128:(k + 1) * 128],
                    normed[k][:, nq * 512:(nq + 1) * 512]) for k in range(8)],
                  reads=[f"{sq}_wv{b}"] + [f"{sq}_nm{k}" for k in range(8)], wkey=kV)
            p.run("dve", lambda v, psV=psV, nq=nq: v.tensor_tensor(
                kvb[:, nq * 512:(nq + 1) * 512], psV[:, 0:512],
                ksb[:, nq * 512:(nq + 1) * 512], op=ALU.mult),
                reads=[kV, f"{sq}_ksb"], writes=[f"{sq}_kv"])
            mmacc(p, psR[:, 0:512],
                  [(wr[b][:, k * 128:(k + 1) * 128],
                    normed[k][:, nq * 512:(nq + 1) * 512]) for k in range(8)],
                  reads=[f"{sq}_wr{b}"] + [f"{sq}_nm{k}" for k in range(8)], wkey=kR)
            p.run("act", lambda s, psR=psR, nq=nq: s.activation(
                rsb[:, nq * 512:(nq + 1) * 512], psR[:, 0:512], AF.Sigmoid),
                reads=[kR], writes=[f"{sq}_rsb"])
        p.run("pool", lambda g: g.memset(ewt[:], 1.0), writes=[f"{sq}_ewt"])
        p.run("dve", lambda v, oc=oc: v.tensor_scalar(
            ewt[:], ewt[:], io.misc[:, 8 + oc:9 + oc], None, op0=ALU.mult),
            reads=[f"{sq}_ewt", "misc_ew"], writes=[f"{sq}_ewt"])
        p.run("dve", lambda v: v.memset(scan[:, 0:1], 0.0), writes=[f"{sq}_scan"])
        p.run("dve", lambda v: v.tensor_tensor_scan(
            scan[:, 1:S + 1], ewt[:], kvb[:], 0.0, op0=ALU.mult, op1=ALU.add),
            reads=[f"{sq}_ewt", f"{sq}_kv", f"{sq}_scan"], writes=[f"{sq}_scan"])
        p.run("dve", lambda v, oc=oc: v.scalar_tensor_tensor(
            ewt[:], kvb[:], io.misc[:, 16 + oc:17 + oc], scan[:, 0:S],
            op0=ALU.mult, op1=ALU.add),
            reads=[f"{sq}_kv", "misc_eu", f"{sq}_scan", f"{sq}_ewt"], writes=[f"{sq}_ewt"])
        p.run("pool", lambda g, oc=oc: g.tensor_tensor(
            rwkv[oc][:], ewt[:], rsb[:], op=ALU.mult),
            reads=[f"{sq}_ewt", f"{sq}_rsb"], writes=[f"{sq}_rw{oc}"])
    wo = [io.salloc(f"{sq}_wo{i}", [128, 1024], BF16) for i in range(2)]
    srows = io.salloc(f"{sq}_sr", [1, 1024], F32)
    r1b4 = io.salloc(f"{sq}_r1b", [128, S], F32)
    mrb4 = io.salloc(f"{sq}_mrb", [128, S], F32)
    x1t = [io.salloc(f"{sq}_x1t{i}", [128, 512], F32) for i in range(2)]
    t1t = [io.salloc(f"{sq}_t1t{i}", [128, 512], F32) for i in range(2)]
    t2t = [io.salloc(f"{sq}_t2t{i}", [128, 512], F32) for i in range(2)]
    xit = [io.salloc(f"{sq}_xit{i}", [128, 512], F32) for i in range(2)]
    st_base = 4 + seq * 2
    for nq in range(4):
        col0 = nq * 512
        p.dma("sp", srows[:, 0:512], io.d["stats"][st_base:st_base + 1, col0:col0 + 512],
              reads=["stats"], writes=[f"{sq}_srA"])
        p.dma("sp", srows[:, 512:1024], io.d["stats"][st_base + 1:st_base + 2,
                                                      col0:col0 + 512],
              reads=["stats"], writes=[f"{sq}_srB"])
        p.run("te", lambda t, col0=col0: t.matmul(
            io.pb[6][:, 0:512], io.ones_f32(0, 1, 128),
            srows[:, 512:1024], start=True, stop=True),
            reads=[f"{sq}_srB", "cf32"], writes=["pb6"])
        p.run("dve", lambda v, col0=col0: v.tensor_copy(
            r1b4[:, col0:col0 + 512], io.pb[6][:, 0:512]),
            reads=["pb6"], writes=[f"{sq}_r1b"])
        p.run("dve", lambda v: v.tensor_tensor(srows[:, 0:512], srows[:, 0:512],
                                               srows[:, 512:1024], op=ALU.mult),
              reads=[f"{sq}_srA", f"{sq}_srB"], writes=[f"{sq}_srA"])
        p.run("te", lambda t: t.matmul(io.pb[6][:, 0:512], io.ones_f32(0, 1, 128),
                                       srows[:, 0:512], start=True, stop=True),
              reads=[f"{sq}_srA", "cf32"], writes=["pb6"])
        p.run("dve", lambda v, col0=col0: v.tensor_copy(
            mrb4[:, col0:col0 + 512], io.pb[6][:, 0:512]),
            reads=["pb6"], writes=[f"{sq}_mrb"])
    _ldcol(p, io, wo[0], f"{sq}_wo0", "WoT1", 0, 128)
    for out in range(8):
        b2_ = out % 2
        if out < 7:
            nb_ = (out + 1) % 2
            _ldcol(p, io, wo[nb_], f"{sq}_wo{nb_}", "WoT1", (out + 1) * 128, 128)
        for nq in range(4):
            col0 = nq * 512
            ps, kps = io.pb[3 + nq % 2], f"pb{3+nq%2}"
            mmacc(p, ps[:, 0:512],
                  [(wo[b2_][:, k * 128:(k + 1) * 128], rwkv[k][:, col0:col0 + 512])
                   for k in range(8)],
                  reads=[f"{sq}_wo{b2_}"] + [f"{sq}_rw{k}" for k in range(8)], wkey=kps)
            p.run("dve", lambda v, ps=ps, out=out, col0=col0, nq=nq: v.scalar_tensor_tensor(
                t1t[nq % 2][:], ps[:, 0:512], io.acol("ms", out),
                mrb4[:, col0:col0 + 512], op0=ALU.mult, op1=ALU.subtract),
                reads=[kps, "arena", f"{sq}_mrb"], writes=[f"{sq}_t1t{nq%2}"])
            p.dma("sp", x1t[nq % 2][:], io.d["xr"][row0 + out * 128: row0 + out * 128 + 128,
                                                   col0:col0 + 512],
                  reads=["xr"], writes=[f"{sq}_x1t{nq%2}"])
            p.run("pool", lambda g, nq=nq, col0=col0: g.tensor_tensor(
                t2t[nq % 2][:], x1t[nq % 2][:], r1b4[:, col0:col0 + 512], op=ALU.mult),
                reads=[f"{sq}_x1t{nq%2}", f"{sq}_r1b"], writes=[f"{sq}_t2t{nq%2}"])
            p.run("dve", lambda v, nq=nq: v.tensor_tensor(
                xit[nq % 2][:], t1t[nq % 2][:], t2t[nq % 2][:], op=ALU.add),
                reads=[f"{sq}_t1t{nq%2}", f"{sq}_t2t{nq%2}"], writes=[f"{sq}_xit{nq%2}"])
            p.dma("act", io.d["xih"][row0 + out * 128: row0 + out * 128 + 128,
                                     col0:col0 + 512], xit[nq % 2][:],
                  reads=[f"{sq}_xit{nq%2}"], writes=["xih"])
    p.barrier()



def emit_mlp1(p, io, seq):
    sq = f"y{seq}"
    row0 = seq * 1024
    io.ov_reset()
    xi = [io.salloc(f"{sq}_xi{c}", [128, S], F32) for c in range(8)]
    nm2 = [io.salloc(f"{sq}_n2{c}", [128, S], BF16) for c in range(8)]
    h2 = [io.salloc(f"{sq}_h2{c}", [128, 512], BF16) for c in range(32)]
    sqt = io.salloc(f"{sq}_sqt", [128, 512], F32)
    lrt = [io.salloc(f"{sq}_lr{i}", [128, 512], BF16) for i in range(2)]
    rrows = io.salloc(f"{sq}_rr", [1, 512], F32)
    rbt = io.salloc(f"{sq}_rbt", [128, 512], F32)
    wfc = [io.salloc(f"{sq}_wfc{i}", [128, 4096], BF16) for i in range(2)]
    wml = [io.salloc(f"{sq}_wml{i}", [128, 4096], BF16) for i in range(2)]
    x1t = [io.salloc(f"{sq}_x1t{i}", [128, 512], F32) for i in range(2)]
    xot = [io.salloc(f"{sq}_xot{i}", [128, 512], F32) for i in range(2)]
    for c in range(8):
        p.dma("sp", xi[c][:], io.d["xih"][row0 + c * 128: row0 + (c + 1) * 128, :],
              reads=["xih"], writes=[f"{sq}_xi{c}"])
    for nq in range(4):
        ps, kps = io.pb[nq % 2], f"pb{nq%2}"
        for c in range(8):
            p.run("act", lambda s, c=c, nq=nq: s.activation(
                sqt[:], xi[c][:, nq * 512:(nq + 1) * 512], AF.Square),
                reads=[f"{sq}_xi{c}"], writes=[f"{sq}_sqt"])
            p.run("te", lambda t, ps=ps, c=c: t.matmul(
                ps[0:1, 0:512], io.ones_f32(0, 128, 1), sqt[:],
                start=(c == 0), stop=(c == 7)),
                reads=[f"{sq}_sqt", "cf32"], writes=[kps])
        p.run("dve", lambda v, ps=ps: v.tensor_scalar(rrows[:], ps[0:1, 0:512],
                                                      1.0 / 1024, None, op0=ALU.mult),
              reads=[kps], writes=[f"{sq}_rr"])
        p.run("act", lambda s: s.activation(rrows[:], rrows[:], AF.Sqrt, bias=1e-6),
              reads=[f"{sq}_rr"], writes=[f"{sq}_rr"])
        p.run("dve", lambda v: v.reciprocal(rrows[:], rrows[:]),
              reads=[f"{sq}_rr"], writes=[f"{sq}_rr"])
        p.run("te", lambda t: t.matmul(io.pb[6][:, 0:512], io.ones_f32(0, 1, 128),
                                       rrows[:], start=True, stop=True),
              reads=[f"{sq}_rr", "cf32"], writes=["pb6"])
        p.run("dve", lambda v: v.tensor_copy(rbt[:], io.pb[6][:, 0:512]),
              reads=["pb6"], writes=[f"{sq}_rbt"])
        for c in range(8):
            p.run("dve", lambda v, c=c, nq=nq: v.tensor_tensor(
                nm2[c][:, nq * 512:(nq + 1) * 512], xi[c][:, nq * 512:(nq + 1) * 512],
                rbt[:], op=ALU.mult),
                reads=[f"{sq}_xi{c}", f"{sq}_rbt"], writes=[f"{sq}_n2{c}"])
    _ldcol(p, io, wfc[0], f"{sq}_wfc0", "WfcT", 0, 512)
    for nq in range(4):
        col0 = nq * 512
        for ocg in range(8):
            b = ocg % 2
            nxt = ocg + 1
            if nxt < 8:
                _ldcol(p, io, wfc[nxt % 2], f"{sq}_wfc{nxt%2}", "WfcT", nxt * 512, 512)
            elif nq < 3:
                _ldcol(p, io, wfc[0], f"{sq}_wfc0", "WfcT", 0, 512)
            for oi in range(4):
                oc = ocg * 4 + oi
                ps, kps = io.pb[oc % 2], f"pb{oc%2}"
                mmacc(p, ps[:, 0:512],
                      [(wfc[b][:, k * 512 + oi * 128: k * 512 + oi * 128 + 128],
                        nm2[k][:, col0:col0 + 512]) for k in range(8)],
                      reads=[f"{sq}_wfc{b}"] + [f"{sq}_n2{k}" for k in range(8)],
                      wkey=kps)
                p.run("act", lambda s, ps=ps, oc=oc: s.activation(
                    lrt[oc % 2][:], ps[:, 0:512], AF.Prelu, alpha=0.5),
                    reads=[kps], writes=[f"{sq}_lr{oc%2}"])
                p.run("act", lambda s, oc=oc: s.activation(
                    h2[oc][:], lrt[oc % 2][:], AF.Square),
                    reads=[f"{sq}_lr{oc%2}"], writes=[f"{sq}_h2{oc}"])
        if nq == 0:
            _ldcol(p, io, wml[0], f"{sq}_wml0", "WmlpT", 0, 128, rows=4096)
        for out in range(8):
            b = out % 2
            nxt = out + 1
            if nxt < 8:
                _ldcol(p, io, wml[nxt % 2], f"{sq}_wml{nxt%2}", "WmlpT",
                       nxt * 128, 128, rows=4096)
            elif nq < 3:
                _ldcol(p, io, wml[0], f"{sq}_wml0", "WmlpT", 0, 128, rows=4096)
            ps, kps = io.pb[3 + out % 2], f"pb{3+out%2}"
            mmacc(p, ps[:, 0:512],
                  [(wml[b][:, k * 128:(k + 1) * 128], h2[k][:]) for k in range(32)],
                  reads=[f"{sq}_wml{b}"] + [f"{sq}_h2{k}" for k in range(32)],
                  wkey=kps)
            p.dma("sp", x1t[out % 2][:], io.d["xr"][row0 + out * 128: row0 + out * 128 + 128,
                                                    col0:col0 + 512],
                  reads=["xr"], writes=[f"{sq}_x1t{out%2}"])
            p.run("dve", lambda v, ps=ps, out=out, col0=col0: v.scalar_tensor_tensor(
                xot[out % 2][:], ps[:, 0:512], io.acol("mls", out),
                xi[out][:, col0:col0 + 512], op0=ALU.mult, op1=ALU.add),
                reads=[kps, "arena", f"{sq}_xi{out}"], writes=[f"{sq}_xot{out%2}"])
            p.run("dve", lambda v, out=out: v.tensor_tensor(
                xot[out % 2][:], xot[out % 2][:], x1t[out % 2][:], op=ALU.add),
                reads=[f"{sq}_xot{out%2}", f"{sq}_x1t{out%2}"], writes=[f"{sq}_xot{out%2}"])
            p.dma("act", io.d["xr2"][row0 + out * 128: row0 + out * 128 + 128,
                                     col0:col0 + 512], xot[out % 2][:],
                  reads=[f"{sq}_xot{out%2}"], writes=["xr2"])
    p.barrier()



def emit_moe1(p, io):
    io.ov_reset()
    xnt = [io.salloc(f"m6_xn{c}", [128, 1024], BF16) for c in range(8)]
    acc = [io.salloc(f"m6_acc{c}", [128, 1024], F32) for c in range(8)]
    hgt = [io.salloc(f"m6_hg{j}", [128, 1024], BF16) for j in range(16)]
    gatesT = io.salloc("m6_gT", [16, T], BF16)
    wbank = [io.salloc(f"m6_w{i}", [128, 8192], BF16) for i in range(5)]
    wgl = io.salloc("m6_wgl", [128, 8 * 16], BF16)
    b2sb = io.salloc("m6_b2", [16, 1024], BF16)
    gb = io.salloc("m6_gb", [128, 1024], BF16)
    sgt = [io.salloc(f"m6_sg{i}", [128, 512], BF16) for i in range(2)]
    htmp = [io.salloc(f"m6_ht{i}", [128, 512], BF16) for i in range(2)]
    lsb = io.salloc("m6_lsb", [128, 16], F32)
    l2sb = io.salloc("m6_l2", [128, 16], F32)
    eq1 = io.salloc("m6_eq1", [128, 16], F32)
    eq2 = io.salloc("m6_eq2", [128, 16], F32)
    gts = io.salloc("m6_gts", [128, 16], F32)
    gtb = io.salloc("m6_gtb", [128, 16], BF16)
    cols = io.salloc("m6_cols", [128, 8], F32)

    p.dma("pool", wgl[:].rearrange("p (c m) -> p c m", m=16),
          io.d["Wg"].rearrange("(c p) m -> p c m", p=128),
          reads=["Wg"], writes=["m6_wgl"])
    p.dma("pool", b2sb[:], io.d["b2e"], reads=["b2e"], writes=["m6_b2"])

    def wload(bi, wname, col_lo, col_cnt, rows, e=None):
        _ldcol(p, io, wbank[bi], f"m6_w{bi}", wname, col_lo, col_cnt, rows=rows, e=e)

    for pa in range(4):
        seq, part = pa // 2, pa % 2
        row0 = seq * 1024
        tok0 = pa * 1024
        colp = part * 1024
        for c in range(8):
            p.dma("sp", xnt[c][:], io.d["xnh"][row0 + c * 128: row0 + (c + 1) * 128,
                                               colp:colp + 1024],
                  reads=["xnh"], writes=[f"m6_xn{c}"])
        for c in range(8):
            p.dma("act", acc[c][:], io.d["xr2"][row0 + c * 128: row0 + (c + 1) * 128,
                                               colp:colp + 1024],
                  reads=["xr2"], writes=[f"m6_acc{c}"])
        def _gate_tt(tt):
            ps, kps = io.pb[6], "pb6"
            mmacc(p, ps[:, 0:16],
                  [(xnt[c][:, tt * 128:(tt + 1) * 128], wgl[:, c * 16:(c + 1) * 16])
                   for c in range(8)],
                  reads=[f"m6_xn{c}" for c in range(8)] + ["m6_wgl"], wkey=kps)
            p.run("dve", lambda v, ps=ps: v.tensor_copy(lsb[:], ps[:, 0:16]),
                  reads=[kps], writes=["m6_lsb"])
            p.run("dve", lambda v: v.reduce_max(cols[:, 0:1], lsb[:], axis=AX.X),
                  reads=["m6_lsb"], writes=["m6_cols"])
            p.run("dve", lambda v: v.tensor_scalar(eq1[:], lsb[:], cols[:, 0:1],
                                                   None, op0=ALU.is_equal),
                  reads=["m6_lsb", "m6_cols"], writes=["m6_eq1"])
            p.run("dve", lambda v: v.scalar_tensor_tensor(
                l2sb[:], eq1[:], -1e30, lsb[:], op0=ALU.mult, op1=ALU.add),
                reads=["m6_eq1", "m6_lsb"], writes=["m6_l2"])
            p.run("dve", lambda v: v.reduce_max(cols[:, 1:2], l2sb[:], axis=AX.X),
                  reads=["m6_l2"], writes=["m6_cols"])
            p.run("dve", lambda v: v.tensor_scalar(eq2[:], l2sb[:], cols[:, 1:2],
                                                   None, op0=ALU.is_equal),
                  reads=["m6_l2", "m6_cols"], writes=["m6_eq2"])
            p.run("dve", lambda v: v.tensor_tensor(cols[:, 2:3], cols[:, 1:2],
                                                   cols[:, 0:1], op=ALU.subtract),
                  reads=["m6_cols"], writes=["m6_cols"])
            p.run("act", lambda s: s.activation(cols[:, 3:4], cols[:, 2:3], AF.Exp),
                  reads=["m6_cols"], writes=["m6_cols"])
            p.run("dve", lambda v: v.tensor_scalar(cols[:, 4:5], cols[:, 3:4], 1.0,
                                                   None, op0=ALU.add),
                  reads=["m6_cols"], writes=["m6_cols"])
            p.run("dve", lambda v: v.reciprocal(cols[:, 5:6], cols[:, 4:5]),
                  reads=["m6_cols"], writes=["m6_cols"])
            p.run("dve", lambda v: v.tensor_tensor(cols[:, 6:7], cols[:, 3:4],
                                                   cols[:, 5:6], op=ALU.mult),
                  reads=["m6_cols"], writes=["m6_cols"])
            p.run("dve", lambda v: v.tensor_scalar(gts[:], eq1[:], cols[:, 5:6],
                                                   None, op0=ALU.mult),
                  reads=["m6_eq1", "m6_cols"], writes=["m6_gts"])
            p.run("dve", lambda v: v.scalar_tensor_tensor(
                gts[:], eq2[:], cols[:, 6:7], gts[:], op0=ALU.mult, op1=ALU.add),
                reads=["m6_eq2", "m6_cols", "m6_gts"], writes=["m6_gts"])
            p.run("dve", lambda v: v.tensor_copy(gtb[:], gts[:]),
                  reads=["m6_gts"], writes=["m6_gtb"])
            p.run("te", lambda t: t.transpose(io.pbt[0:16, 0:128], gtb[:], io.ident()),
                  reads=["m6_gtb", "cbf"], writes=["pbt"])
            p.run("dve", lambda v, tt=tt, tok0=tok0: v.tensor_copy(
                gatesT[:, tok0 + tt * 128: tok0 + (tt + 1) * 128], io.pbt[0:16, 0:128]),
                reads=["pbt"], writes=[f"m6_gT{tt}"])
        for e in range(16):
            if e == 0:
                for tt in range(4):
                    _gate_tt(tt)
            p.run("te", lambda t, e=e, tok0=tok0: t.matmul(
                io.pb[6][:, 0:512],
                io.cbf[0:16, CBF_OH16 + e * 128: CBF_OH16 + (e + 1) * 128],
                gatesT[0:16, tok0: tok0 + 512], start=True, stop=True),
                reads=[f"m6_gT{tt}" for tt in range(4)] + ["cbf"], writes=["pb6"])
            p.run("dve", lambda v: v.tensor_copy(gb[:, 0:512], io.pb[6][:, 0:512]),
                  reads=["pb6"], writes=["m6_gb"])
            if e == 0:
                for tt in range(4, 8):
                    _gate_tt(tt)
            p.run("te", lambda t, e=e, tok0=tok0: t.matmul(
                io.pb[6][:, 0:512],
                io.cbf[0:16, CBF_OH16 + e * 128: CBF_OH16 + (e + 1) * 128],
                gatesT[0:16, tok0 + 512: tok0 + 1024], start=True, stop=True),
                reads=[f"m6_gT{tt}" for tt in range(4, 8)] + ["cbf"], writes=["pb6"])
            p.run("dve", lambda v: v.tensor_copy(gb[:, 512:1024], io.pb[6][:, 0:512]),
                  reads=["pb6"], writes=["m6_gb"])
            for mh in range(2):
                w1b, w11b = (0, 1) if mh == 0 else (2, 3)
                wload(w1b, "W1", mh * 1024, 1024, 1024, e=e)
                wload(w11b, "W11", mh * 1024, 1024, 1024, e=e)
                if mh == 1:
                    wload(0, "W2", 0 * 512, 512, 2048, e=e)
                    wload(1, "W2", 1 * 512, 512, 2048, e=e)
                for m in range(8):
                    j = mh * 8 + m
                    for nb in range(2):
                        psG, kG = io.pb[nb], f"pb{nb}"
                        psU, kU = io.pb[2 + nb], f"pb{2+nb}"
                        xnks = [f"m6_xn{k}" for k in range(8)]
                        mmacc(p, psG[:, 0:512],
                              [(_wslices(wbank[w1b], k, m * 128, 128),
                                xnt[k][:, nb * 512:(nb + 1) * 512]) for k in range(8)],
                              reads=[f"m6_w{w1b}"] + xnks, wkey=kG)
                        mmacc(p, psU[:, 0:512],
                              [(_wslices(wbank[w11b], k, m * 128, 128),
                                xnt[k][:, nb * 512:(nb + 1) * 512]) for k in range(8)],
                              reads=[f"m6_w{w11b}"] + xnks, wkey=kU)
                        p.run("act", lambda s, psG=psG, e=e, j=j, nb=nb: s.activation(
                            sgt[nb][:], psG[:, 0:512], AF.Silu,
                            bias=io.acol("b1", e * 16 + j)),
                            reads=[kG, "arena"], writes=[f"m6_sg{nb}"])
                        p.run("dve", lambda v, psU=psU, e=e, j=j, nb=nb: v.scalar_tensor_tensor(
                            htmp[nb][:], psU[:, 0:512], io.acol("b11", e * 16 + j),
                            sgt[nb][:], op0=ALU.add, op1=ALU.mult),
                            reads=[kU, "arena", f"m6_sg{nb}"], writes=[f"m6_ht{nb}"])
                        p.run("pool", lambda g, j=j, nb=nb: g.tensor_tensor(
                            hgt[j][:, nb * 512:(nb + 1) * 512], htmp[nb][:],
                            gb[:, nb * 512:(nb + 1) * 512], op=ALU.mult),
                            reads=[f"m6_ht{nb}", "m6_gb"], writes=[f"m6_hg{j}"])
            for oh in range(2):
                w2b = oh
                for nb in range(2):
                    for out in range(4):
                        o = oh * 4 + out
                        psY, kY = io.pb[4 + out % 2], f"pb{4+out%2}"
                        mmacc(p, psY[:, 0:512],
                              [(wbank[w2b][:, k * 512 + out * 128: k * 512 + out * 128 + 128],
                                hgt[k][:, nb * 512:(nb + 1) * 512]) for k in range(16)],
                              reads=[f"m6_w{w2b}"] + [f"m6_hg{k}" for k in range(16)],
                              wkey=kY)
                        p.run("dve", lambda v, psY=psY, o=o, nb=nb: v.tensor_tensor(
                            acc[o][:, nb * 512:(nb + 1) * 512], psY[:, 0:512],
                            acc[o][:, nb * 512:(nb + 1) * 512], op=ALU.add),
                            reads=[kY, f"m6_acc{o}"], writes=[f"m6_acc{o}"])
        for out in range(8):
            for nb in range(2):
                psB, kB = io.pb[6], "pb6"
                p.run("te", lambda t, out=out, nb=nb, tok0=tok0, psB=psB: t.matmul(
                    psB[:, 0:512], b2sb[:, out * 128:(out + 1) * 128],
                    gatesT[0:16, tok0 + nb * 512: tok0 + (nb + 1) * 512],
                    start=True, stop=True),
                    reads=["m6_b2", "m6_gT"], writes=[kB])
                p.run("dve", lambda v, psB=psB, out=out, nb=nb: v.tensor_tensor(
                    acc[out][:, nb * 512:(nb + 1) * 512], psB[:, 0:512],
                    acc[out][:, nb * 512:(nb + 1) * 512], op=ALU.add),
                    reads=[kB, f"m6_acc{out}"], writes=[f"m6_acc{out}"])
        for qq in range(4):
            qb1, qb11 = (0, 1) if qq % 2 == 0 else (2, 3)
            wload(qb1, "Ws1", qq * 1024, 1024, 1024)
            wload(qb11, "Ws11", qq * 1024, 1024, 1024)
            for m in range(8):
                for nb in range(2):
                    psG, kG = io.pb[nb], f"pb{nb}"
                    psU, kU = io.pb[2 + nb], f"pb{2+nb}"
                    xnks = [f"m6_xn{k}" for k in range(8)]
                    mmacc(p, psG[:, 0:512],
                          [(_wslices(wbank[qb1], k, m * 128, 128),
                            xnt[k][:, nb * 512:(nb + 1) * 512]) for k in range(8)],
                          reads=[f"m6_w{qb1}"] + xnks, wkey=kG)
                    mmacc(p, psU[:, 0:512],
                          [(_wslices(wbank[qb11], k, m * 128, 128),
                            xnt[k][:, nb * 512:(nb + 1) * 512]) for k in range(8)],
                          reads=[f"m6_w{qb11}"] + xnks, wkey=kU)
                    p.run("act", lambda s, psG=psG, qq=qq, m=m, nb=nb: s.activation(
                        sgt[nb][:], psG[:, 0:512], AF.Silu,
                        bias=io.acol("bs1", qq * 8 + m)),
                        reads=[kG, "arena"], writes=[f"m6_sg{nb}"])
                    p.run("dve", lambda v, psU=psU, qq=qq, m=m, nb=nb: v.scalar_tensor_tensor(
                        hgt[m][:, nb * 512:(nb + 1) * 512], psU[:, 0:512],
                        io.acol("bs11", qq * 8 + m), sgt[nb][:],
                        op0=ALU.add, op1=ALU.mult),
                        reads=[kU, "arena", f"m6_sg{nb}"], writes=[f"m6_hg{m}"])
            srcq = io.d["Ws2"][qq * 1024:(qq + 1) * 1024, :].rearrange(
                "(c p) m -> p c m", p=128)
            p.dma("pool", wbank[4][:].rearrange("p (c m) -> p c m", m=1024), srcq,
                  reads=["Ws2"], writes=["m6_w4"])
            for out in range(8):
                for nb in range(2):
                    psY, kY = io.pb[4 + out % 2], f"pb{4+out%2}"
                    mmacc(p, psY[:, 0:512],
                          [(_wslices(wbank[4], k, out * 128, 128),
                            hgt[k][:, nb * 512:(nb + 1) * 512]) for k in range(8)],
                          reads=["m6_w4"] + [f"m6_hg{k}" for k in range(8)], wkey=kY)
                    if qq == 3:
                        p.run("dve", lambda v, psY=psY, out=out, nb=nb: v.scalar_tensor_tensor(
                            acc[out][:, nb * 512:(nb + 1) * 512], psY[:, 0:512],
                            io.acol("bs2", out), acc[out][:, nb * 512:(nb + 1) * 512],
                            op0=ALU.add, op1=ALU.add),
                            reads=[kY, "arena", f"m6_acc{out}"], writes=[f"m6_acc{out}"])
                    else:
                        p.run("dve", lambda v, psY=psY, out=out, nb=nb: v.tensor_tensor(
                            acc[out][:, nb * 512:(nb + 1) * 512], psY[:, 0:512],
                            acc[out][:, nb * 512:(nb + 1) * 512], op=ALU.add),
                            reads=[kY, f"m6_acc{out}"], writes=[f"m6_acc{out}"])
        for c in range(8):
            p.dma("act", io.d["out"][row0 + c * 128: row0 + (c + 1) * 128,
                                     colp:colp + 1024], acc[c][:],
                  reads=[f"m6_acc{c}"], writes=["out"])
    p.barrier()



def _run_phases(in_maps, build_fns, ext_io, trace=False):
    nc = bass.Bass("TRN2", target_bir_lowering=False)
    io = IO(nc, *ext_io)
    p = Prog(nc)
    io.load_consts(p)
    for fn in build_fns:
        fn(p, io)
    p.finish()
    p.flush()
    res = run_bass_kernel_spmd(nc, in_maps, core_ids=list(range(8)), trace=trace)
    return res


def build_full(nc):
    io = IO(nc, set(), set())
    p = Prog(nc)
    io.load_consts(p)
    emit_ln(p, io, "x0", 0)
    emit_attn0(p, io, 0)
    emit_attn0(p, io, 1)
    emit_moe0(p, io)
    emit_ln(p, io, "xr", 1)
    emit_mixer1(p, io, 0)
    emit_mixer1(p, io, 1)
    emit_mlp1(p, io, 0)
    emit_mlp1(p, io, 1)
    emit_moe1(p, io)
    p.finish()
    p.flush()
    return p


def kernel(**inputs):
    in_maps = host_prepare(inputs)
    nc = bass.Bass("TRN2", target_bir_lowering=False)
    build_full(nc)
    res = run_bass_kernel_spmd(nc, in_maps, core_ids=list(range(8)))
    return host_gather(res.results)
